# version 08
# baseline (speedup 1.0000x reference)
import numpy as np
from contextlib import ExitStack

import concourse.bass as bass
import concourse.mybir as mybir
from concourse.bass_utils import run_bass_kernel_spmd

F32 = mybir.dt.float32
BF16 = mybir.dt.bfloat16
I32 = mybir.dt.int32
ALU = mybir.AluOpType
AF = mybir.ActivationFunctionType
AXX = mybir.AxisListType.X

FULL_CFG = dict(ncores=8, HC=8, T=2048, TC=512, NE=8, ICH=4)


class _Dummy:
    def then_inc(self, *a, **k):
        return self


class _EngP:

    def __getattr__(self, k):
        return lambda *a, **kw: _Dummy()


_DUMMY_ENG = _EngP()


_COMPUTE_OPS = {
    "tensor_tensor", "tensor_scalar", "tensor_copy", "tensor_add",
    "tensor_sub", "tensor_mul", "tensor_max", "tensor_scalar_add",
    "tensor_scalar_mul", "tensor_scalar_max", "tensor_scalar_min",
    "tensor_single_scalar", "scalar_tensor_tensor", "tensor_tensor_scan",
    "tensor_reduce", "reduce_max", "reduce_sum", "reciprocal", "memset",
    "activation", "copy", "add", "mul", "iota", "select", "copy_predicated",
}


class _ParanoidEng:

    def __init__(self, eng):
        self._e = eng

    def __getattr__(self, k):
        a = getattr(self._e, k)
        if k in _COMPUTE_OPS:
            def wrapped(*args, **kw):
                self._e.drain()
                return a(*args, **kw)
            return wrapped
        return a


def build_program(cfg, dbg_out=False):
    ncores = cfg["ncores"]
    HC, T, TC, NE, ICH = (cfg["HC"], cfg["T"], cfg["TC"], cfg["NE"],
                          cfg["ICH"])
    NCH = T // TC
    NPE = NE + 2
    KC = NPE * ICH
    NSTRIP = T // 128
    SPT = TC // 128
    NJG = (NSTRIP + 3) // 4
    invH = 1.0 / (128 * HC)
    EPS = 1e-5

    nc = bass.Bass("TRN2", target_bir_lowering=False,
                   detect_race_conditions=bool(cfg.get("race", False)))

    d_x = nc.dram_tensor("x", [128, HC * T], F32, kind="ExternalInput")
    d_wv = nc.dram_tensor("wv", [128, HC * HC * 128], F32, kind="ExternalInput")
    d_wr = nc.dram_tensor("wr", [128, HC * HC * 128], F32, kind="ExternalInput")
    d_wo = nc.dram_tensor("wo", [128, HC * HC * 128], F32, kind="ExternalInput")
    d_td = nc.dram_tensor("td", [128, HC], F32, kind="ExternalInput")
    d_tmv = nc.dram_tensor("tmv", [128, HC], F32, kind="ExternalInput")
    d_tmr = nc.dram_tensor("tmr", [128, HC], F32, kind="ExternalInput")
    d_gatew = nc.dram_tensor("gatew", [128, HC * NE], F32, kind="ExternalInput")
    d_lbrow = nc.dram_tensor("lbrow", [1, NE], F32, kind="ExternalInput")
    d_gwuw = nc.dram_tensor("gwuw", [NPE, 128, 2 * HC * ICH * 128], F32,
                            kind="ExternalInput")
    d_dw = nc.dram_tensor("dwcat", [128, HC * KC * 128], F32,
                          kind="ExternalInput")
    d_out = nc.dram_tensor("out", [128, HC * T], F32, kind="ExternalOutput")
    d_gwuwb = nc.dram_tensor("gwuwb", [NPE, 128, 2 * HC * ICH * 128], BF16)
    d_dwb = nc.dram_tensor("dwb", [128, HC * KC * 128], BF16)
    dbg_kind = dict(kind="ExternalOutput") if dbg_out else {}
    d_x1 = nc.dram_tensor("x1d", [128, HC * T], F32, **dbg_kind)
    d_ln2 = nc.dram_tensor("ln2d", [128, HC * T], BF16, **dbg_kind)
    d_sem = nc.dram_tensor("semrows", [8, T], F32, **dbg_kind)
    d_ccin = nc.dram_tensor("ccin", [1, 16], F32)
    d_ccout = nc.dram_tensor("ccout", [1, 16], F32)

    st = ExitStack()
    nc._ctx_keepalive = st

    def sb(name, shape, dt):
        return st.enter_context(nc.sbuf_tensor(name, shape, dt))

    ones_bfc = sb("ones_bfc", [128, 1], BF16)
    ones_fc = sb("ones_fc", [128, 1], F32)
    ones_fr = sb("ones_fr", [1, 128], F32)
    i128f = sb("i128f", [128, 128], F32)
    i8f = sb("i8f", [8, 8], F32)
    itmp = sb("itmp", [128, 128], I32)
    dcol = sb("dcol", [128, HC], F32)
    dtmp = sb("dtmp", [128, HC], F32)
    tmvc = sb("tmvc", [128, HC], F32)
    tmrc = sb("tmrc", [128, HC], F32)
    gatew = sb("gatew_sb", [128, HC * NE], F32)
    lbrow = sb("lbrow_sb", [1, NE], F32)
    carry = sb("carry", [128, HC], F32)
    rawsem = sb("rawsem", [8, T], F32)
    raw_em = rawsem
    s_em = rawsem
    g_t1 = sb("g_t1", [128, NSTRIP * 8], F32)
    g_lg = sb("g_lg", [128, NSTRIP * 8], F32)
    g_sc = sb("g_sc", [128, NSTRIP * 8], F32)
    g_sel = sb("g_sel", [128, NSTRIP * 8], F32)
    g_wc = sb("g_wc", [128, NSTRIP * 8], F32)
    g_m1 = sb("g_m1", [128, NSTRIP], F32)
    g_m2 = sb("g_m2", [128, NSTRIP], F32)
    g_s12 = sb("g_s12", [128, NSTRIP], F32)
    g_i12 = sb("g_i12", [128, NSTRIP], F32)
    mcols = sb("mcols", [128, NSTRIP], F32)
    iscols = sb("iscols", [128, NSTRIP], F32)
    gsumr = sb("gsumr", [1, NE], F32)
    gsumbc = sb("gsumbc", [128, NE], F32)
    lbbc = sb("lbbc", [128, NE], F32)
    wmrow = sb("wmrow", [1, NE], F32)
    wmcol = sb("wmcol", [8, 1], F32)
    ccin_sb = sb("ccin_sb", [1, 16], F32)
    ccout_sb = sb("ccout_sb", [1, 16], F32)
    srow = sb("srow", [1, 2 * TC], F32)

    LAY = {}
    off = [0]

    def lay(name, nbytes):
        a = off[0]
        off[0] = (a + nbytes + 31) & ~31
        LAY[name] = a

    lay("xres", 3 * HC * TC * 4)
    lay("wv", HC * HC * 128 * 2)
    lay("wr", HC * HC * 128 * 2)
    lay("wo", HC * HC * 128 * 2)
    lay("vin", 2 * HC * TC * 2)
    lay("rin", 2 * HC * TC * 2)
    lay("v_c", HC * TC * 2)
    lay("r_c", HC * TC * 2)
    lay("lnb", TC * 4)
    lay("diffb", TC * 4)
    lay("zb", (TC + 1) * 4)
    lay("cast1", 2 * 2 * TC * 2)
    lay("cast2", 2 * 2 * TC * 2)
    lay("ln2bf", 2 * TC * 2)
    lay("ab1", 2 * TC * 4)
    lay("ab2", 2 * TC * 4)
    lay("r_mean", TC * 4)
    lay("r_e2", TC * 4)
    lay("r_msq", TC * 4)
    lay("r_is", TC * 4)
    lay("r_b", TC * 4)
    lay("r2_mean", TC * 4)
    lay("r2_is", TC * 4)
    lay("r2_b", TC * 4)
    attn_end = off[0]
    off[0] = 0
    lay("x1t", 2 * HC * TC * 4)
    lay("ln2t", 2 * HC * TC * 2)
    assert off[0] <= LAY["wv"], "x1t/ln2t must stay inside xres"
    off[0] = LAY["wv"]
    lay("gwuws", 2 * 2 * HC * ICH * 128 * 2)
    lay("dws", 2 * KC * 128 * 2)
    lay("hid", KC * TC * 2)
    lay("gsb", 2 * TC * 4)
    lay("h1b", TC * 4)
    ffn_end = off[0]
    ARENA = max(attn_end, ffn_end)
    arena = sb("arena", [128, ARENA], mybir.dt.uint8)
    assert LAY["gwuws"] == LAY["wv"]

    def view(name, nbytes, dt):
        a = LAY[name]
        return arena[:, a:a + nbytes].bitcast(dt)

    v_xres = view("xres", 3 * HC * TC * 4, F32).rearrange(
        "p (s h t) -> p s h t", s=3, h=HC)
    v_wv = view("wv", HC * HC * 128 * 2, BF16)
    v_wr = view("wr", HC * HC * 128 * 2, BF16)
    v_wo = view("wo", HC * HC * 128 * 2, BF16)
    v_vin = view("vin", 2 * HC * TC * 2, BF16).rearrange(
        "p (s h t) -> p s h t", s=2, h=HC)
    v_rin = view("rin", 2 * HC * TC * 2, BF16).rearrange(
        "p (s h t) -> p s h t", s=2, h=HC)
    v_vc = view("v_c", HC * TC * 2, BF16).rearrange("p (h t) -> p h t", h=HC)
    v_rc = view("r_c", HC * TC * 2, BF16).rearrange("p (h t) -> p h t", h=HC)
    v_lnb = view("lnb", TC * 4, F32)
    v_diff = view("diffb", TC * 4, F32)
    v_zb = view("zb", (TC + 1) * 4, F32)
    v_cast1 = view("cast1", 2 * 2 * TC * 2, BF16).rearrange(
        "p (s w t) -> p s w t", s=2, w=2)
    v_cast2 = view("cast2", 2 * 2 * TC * 2, BF16).rearrange(
        "p (s w t) -> p s w t", s=2, w=2)
    v_ln2bf = view("ln2bf", 2 * TC * 2, BF16).rearrange(
        "p (s t) -> p s t", s=2)
    v_ab1 = view("ab1", 2 * TC * 4, F32)
    v_ab2 = view("ab2", 2 * TC * 4, F32)
    v_rmean = view("r_mean", TC * 4, F32)
    v_re2 = view("r_e2", TC * 4, F32)
    v_rmsq = view("r_msq", TC * 4, F32)
    v_ris = view("r_is", TC * 4, F32)
    v_rb = view("r_b", TC * 4, F32)
    v_r2mean = view("r2_mean", TC * 4, F32)
    v_r2is = view("r2_is", TC * 4, F32)
    v_r2b = view("r2_b", TC * 4, F32)
    v_x1t = view("x1t", 2 * HC * TC * 4, F32).rearrange(
        "p (s h t) -> p s h t", s=2, h=HC)
    v_ln2t = view("ln2t", 2 * HC * TC * 2, BF16).rearrange(
        "p (s h t) -> p s h t", s=2, h=HC)
    v_gwuw = view("gwuws", 2 * 2 * HC * ICH * 128 * 2, BF16).rearrange(
        "p (s g h i j) -> p s g h i j", s=2, g=2, h=HC, i=ICH)
    v_dws = view("dws", 2 * KC * 128 * 2, BF16).rearrange(
        "p (s k j) -> p s k j", s=2, k=KC)
    v_hid = view("hid", KC * TC * 2, BF16).rearrange("p (k t) -> p k t", k=KC)
    v_gsb = view("gsb", 2 * TC * 4, F32)
    v_h1b = view("h1b", TC * 4, F32)

    def xr(c):
        return v_xres[:, c % 3, :, :]

    def wvw(w, oc, hc):
        o = (oc * HC + hc) * 128
        return w[:, o:o + 128]

    pb = [st.enter_context(nc.psum_tensor(f"pb{i}", [128, 512], F32))
          for i in range(8)]

    sem_names = ["ge", "cc", "pe", "ve", "se",
                 "sm", "wat", "cst", "gwa", "gwb", "dwa", "dwb", "ccb",
                 "sx0", "sx1", "sx2", "x1s0", "x1s1", "x1s2", "x1s3",
                 "l2a", "l2b", "smr", "xta", "xtb", "lta", "ltb",
                 "sra", "srb", "out"]
    SEM = {n: st.enter_context(nc.semaphore(n)) for n in sem_names}

    def dsem(key):
        if key == "smalls":
            return "sm"
        if key in ("wv", "wr", "wo"):
            return "wat"
        if key == "dwcast" or key.startswith("gwcast"):
            return "cst"
        if key.startswith("gws_"):
            _, p, e = key.split("_")
            eord = [NE, NE + 1] + list(range(NE))
            si = eord.index(int(e))
            return "gwa" if (int(p) * NPE + si) % 2 == 0 else "gwb"
        if key.startswith("dws_"):
            _, p, hc = key.split("_")
            return "dwa" if (int(p) * HC + int(hc)) % 2 == 0 else "dwb"
        if key in ("ccin_d", "ccout"):
            return "ccb"
        if key.startswith("x_"):
            return f"sx{int(key[2:]) % 3}"
        if key.startswith("x1sp_"):
            return f"x1s{int(key[5:])}"
        if key.startswith("ln2sp_"):
            _, c, hc = key.split("_")
            return "l2a" if (int(c) * HC + int(hc)) % 2 == 0 else "l2b"
        if key == "semrows_d":
            return "smr"
        if key.startswith("x1t_"):
            return "xta" if int(key[4:]) % 2 == 0 else "xtb"
        if key.startswith("ln2t_"):
            return "lta" if int(key[5:]) % 2 == 0 else "ltb"
        if key.startswith("srow_"):
            _, p, e = key.split("_")
            return "sra" if (int(p) * NE + int(e)) % 2 == 0 else "srb"
        if key.startswith("out_"):
            return "out"
        raise KeyError(key)

    S = {}

    def reset(emit):
        S["emit"] = emit
        S["cnt"] = {n: 0 for n in sem_names}
        S["pe_rd"] = {}
        if not emit:
            S["led"] = {}

    def INC(inst, sem, key=None, n=1):
        if sem in ("gw", "sx", "sp"):
            sem = dsem(key)
            n = 16
        S["cnt"][sem] += n
        if key is not None and not S["emit"]:
            S["led"][(sem, key)] = S["cnt"][sem]
        if S["emit"] and inst is not None and not isinstance(inst, _Dummy):
            if sem in ("se", "ve") and S.get("cur") is not None and cfg.get("dr", 1):
                S["cur"].drain()
                inst = S["cur"].nop()
            inst.then_inc(SEM[sem], n)

    def ALIAS(sem, key, src):
        if not S["emit"]:
            S["led"][(sem, key)] = S["led"][(sem, src)]

    def WAIT(eng, sem, key):
        if sem in ("gw", "sx", "sp"):
            sem = dsem(key)
        if S["emit"]:
            v = S["led"].get((sem, key))
            assert v is not None, f"missing ledger {(sem, key)}"
            eng.wait_ge(SEM[sem], v)

    def bank_w(eng, bi, reader):
        prev = S["pe_rd"].get(bi)
        if prev is not None:
            WAIT(eng, *prev)
        S["pe_rd"][bi] = reader

    def emit_gpsimd(g):
        S["cur"] = g if not isinstance(g, _EngP) else None
        for dst, src in ((dtmp, d_td), (tmvc, d_tmv), (tmrc, d_tmr),
                         (gatew, d_gatew)):
            i0 = g.dma_start(out=dst[:, :], in_=src[:, :])
            INC(i0, "gw", "smalls")
        i0 = g.dma_start(out=lbrow[:, :], in_=d_lbrow[:, :])
        INC(i0, "gw", "smalls")
        g.iota(itmp[:, :], pattern=[[1, 128]], base=0, channel_multiplier=-1)
        g.drain()
        g.tensor_scalar(out=i128f[:, :], in0=itmp[:, :], scalar1=0.0,
                        scalar2=None, op0=ALU.is_equal)
        g.iota(itmp[0:8, 0:8], pattern=[[1, 8]], base=0, channel_multiplier=-1)
        g.drain()
        i0 = g.tensor_scalar(out=i8f[:, :], in0=itmp[0:8, 0:8], scalar1=0.0,
                             scalar2=None, op0=ALU.is_equal)
        g.drain()
        INC(i0, "ge", "ident")
        i0 = g.dma_start(out=v_wv, in_=d_wv[:, :])
        INC(i0, "gw", "wv")
        i0 = g.dma_start(out=v_wr, in_=d_wr[:, :])
        INC(i0, "gw", "wr")
        i0 = g.dma_start(out=v_wo, in_=d_wo[:, :])
        INC(i0, "gw", "wo")
        pace = {2: 0, 5: 1, 8: 2}
        for e in range(NPE):
            if e in pace:
                WAIT(g, "sp", f"x1sp_{min(pace[e], NCH - 1)}")
            i0 = g.dma_start(out=d_gwuwb[e, :, :], in_=d_gwuw[e, :, :])
            INC(i0, "gw", f"gwcast{e}")
        i0 = g.dma_start(out=d_dwb[:, :], in_=d_dw[:, :])
        INC(i0, "gw", "dwcast")
        EORDER_G = [NE, NE + 1] + list(range(NE))
        for s0 in range(min(2, NPE)):
            e0 = EORDER_G[s0]
            WAIT(g, "pe", f"wr_{NCH - 1}_{HC - 1}")
            WAIT(g, "gw", f"gwcast{NPE - 1}")
            i0 = g.dma_start(out=v_gwuw[:, s0 % 2, :, :, :, :],
                             in_=d_gwuwb[e0, :, :])
            INC(i0, "gw", f"gws_0_{e0}")
        WAIT(g, "gw", "ccin_d")
        i0 = g.collective_compute(
            "AllReduce", ALU.add, replica_groups=[list(range(ncores))],
            ins=[d_ccin.ap().opt()], outs=[d_ccout.ap().opt()])
        INC(i0, "cc", "ccdone")
        for p in range(NCH):
            for si, e in enumerate(EORDER_G):
                k = p * NPE + si
                if k < 2:
                    continue
                pk, sk = divmod(k - 2, NPE)
                WAIT(g, "pe", f"guexp_{pk}_{sk}")
                WAIT(g, "gw", f"gwcast{NPE - 1}")
                i0 = g.dma_start(out=v_gwuw[:, k % 2, :, :, :, :],
                                 in_=d_gwuwb[e, :, :])
                INC(i0, "gw", f"gws_{p}_{e}")
            for hc in range(HC):
                k = p * HC + hc
                if k == 0:
                    WAIT(g, "pe", f"wo_{NCH - 1}_{HC - 1}")
                    WAIT(g, "gw", "dwcast")
                if k >= 2:
                    pk, hk = divmod(k - 2, HC)
                    WAIT(g, "pe", f"dn_{pk}_{hk}")
                i0 = g.dma_start(out=v_dws[:, k % 2, :, :],
                                 in_=d_dwb[:, hc * KC * 128:(hc + 1) * KC * 128])
                INC(i0, "gw", f"dws_{p}_{hc}")

    def emit_sync(sy):
        S["cur"] = None
        for c in range(min(3, NCH)):
            i0 = sy.dma_start(
                out=xr(c)[:, :, :],
                in_=d_x[:, :].rearrange("p (h t) -> p h t", h=HC)[:, :, c * TC:(c + 1) * TC])
            INC(i0, "sx", f"x_{c}")
        for i in range(NCH + 2):
            if 1 <= i <= NCH:
                c = i - 1
                WAIT(sy, "ve", f"epi_{c}")
                i0 = sy.dma_start(
                    out=d_x1[:, :].rearrange("p (h t) -> p h t", h=HC)[:, :, c * TC:(c + 1) * TC],
                    in_=xr(c)[:, :, :])
                INC(i0, "sp", f"x1sp_{c}")
            if 2 <= i <= NCH + 1:
                c = i - 2
                for hc in range(HC):
                    WAIT(sy, "ve", f"ln2_{c}_{hc}")
                    i0 = sy.dma_start(
                        out=d_ln2[:, :].rearrange("p (h t) -> p h t", h=HC)[:, hc, c * TC:(c + 1) * TC],
                        in_=v_ln2bf[:, (c * HC + hc) % 2, :])
                    INC(i0, "sp", f"ln2sp_{c}_{hc}")
            c = i + 1
            if 3 <= c < NCH:
                WAIT(sy, "pe", f"raw_{c - 3}")
                WAIT(sy, "ve", f"ln2_{c - 3}_{HC - 1}")
                i0 = sy.dma_start(
                    out=xr(c)[:, :, :],
                    in_=d_x[:, :].rearrange("p (h t) -> p h t", h=HC)[:, :, c * TC:(c + 1) * TC])
                INC(i0, "sx", f"x_{c}")
        def rd_pass(p):
            WAIT(sy, "sp", f"x1sp_{p}")
            if p == 0:
                WAIT(sy, "pe", f"raw_{NCH - 1}")
                WAIT(sy, "ve", f"ln2_{NCH - 1}_{HC - 1}")
                for pp in range(NCH):
                    WAIT(sy, "sp", f"x1sp_{pp}")
            if p >= 2:
                WAIT(sy, "ve", f"evacend_{p - 2}")
            i0 = sy.dma_start(
                out=v_x1t[:, p % 2, :, :],
                in_=d_x1[:, :].rearrange("p (h t) -> p h t", h=HC)[:, :, p * TC:(p + 1) * TC])
            INC(i0, "sx", f"x1t_{p}")
            WAIT(sy, "sp", f"ln2sp_{p}_{HC - 1}")
            if p >= 2:
                WAIT(sy, "pe", f"guend_{p - 2}")
            i0 = sy.dma_start(
                out=v_ln2t[:, p % 2, :, :],
                in_=d_ln2[:, :].rearrange("p (h t) -> p h t", h=HC)[:, :, p * TC:(p + 1) * TC])
            INC(i0, "sx", f"ln2t_{p}")

        def rd_srows(p):
            for e in range(NE):
                k = p * NE + e
                if k == 0:
                    WAIT(sy, "sp", "semrows_d")
                if k >= 2:
                    pk, ek = divmod(k - 2, NE)
                    WAIT(sy, "pe", f"bselmm_{pk}_{ek}")
                i0 = sy.dma_start(out=srow[0:1, (k % 2) * TC:(k % 2 + 1) * TC],
                                  in_=d_sem[e:e + 1, p * TC:(p + 1) * TC])
                INC(i0, "sx", f"srow_{p}_{e}")

        rd_pass(0)
        rd_pass(1)
        WAIT(sy, "ve", "ccin")
        i0 = sy.dma_start(out=d_ccin[:, :], in_=ccin_sb[:, :])
        INC(i0, "gw", "ccin_d")
        WAIT(sy, "cc", "ccdone")
        i0 = sy.dma_start(out=ccout_sb[:, :], in_=d_ccout[:, :])
        INC(i0, "gw", "ccout")
        WAIT(sy, "ve", "semrows")
        i0 = sy.dma_start(out=d_sem[:, :], in_=s_em[:, :])
        INC(i0, "sp", "semrows_d")
        rd_srows(0)
        for p in range(NCH):
            if p + 1 < NCH:
                rd_srows(p + 1)
            for hc in range(HC):
                WAIT(sy, "ve", f"evac_{p}_{hc}")
                i0 = sy.dma_start(
                    out=d_out[:, :].rearrange("p (h t) -> p h t", h=HC)[:, hc, p * TC:(p + 1) * TC],
                    in_=v_x1t[:, p % 2, hc, :])
                INC(i0, "sp", f"out_{p}_{hc}")
            if p + 2 < NCH:
                rd_pass(p + 2)
        WAIT(sy, "sp", f"out_{NCH - 1}_{HC - 1}")

    def emit_scalar(s):
        S["cur"] = s if not isinstance(s, _EngP) else None
        WAIT(s, "gw", "smalls")
        s.activation(out=dtmp[:, :], in_=dtmp[:, :], func=AF.Exp)
        s.drain()
        i0 = s.activation(out=dcol[:, :], in_=dtmp[:, :], func=AF.Exp,
                          scale=-1.0)
        s.drain()
        INC(i0, "se", "dcol")

        def casts(c, cviews, tag):
            if tag == "1":
                WAIT(s, "sx", f"x_{c}")
            for hc in range(HC):
                k = c * HC + hc
                if tag == "2":
                    WAIT(s, "ve", f"epioc_{c}_{hc}")
                if k >= 2:
                    pk, hk = divmod(k - 2, HC)
                    WAIT(s, "pe", f"st{tag}mm_{pk}_{hk}")
                s.copy(out=cviews[:, hc % 2, 0, :], in_=xr(c)[:, hc, :])
                i0 = s.activation(out=cviews[:, hc % 2, 1, :],
                                  in_=xr(c)[:, hc, :], func=AF.Square)
                if hc % 2 == 1 or hc == HC - 1:
                    INC(i0, "se", f"cast{tag}_{c}_{hc}")
                    if hc % 2 == 1:
                        ALIAS("se", f"cast{tag}_{c}_{hc - 1}", f"cast{tag}_{c}_{hc}")

        for i in range(NCH + 2):
            if i < NCH:
                casts(i, v_cast1, "1")
            if 2 <= i <= NCH + 1:
                c = i - 2
                WAIT(s, "ve", f"vrow2_{c}")
                i0 = s.activation(out=v_rmsq[0:1, :], in_=v_re2[0:1, :], func=AF.Sqrt)
                INC(i0, "se", f"sqrt2_{c}")
            if i < NCH:
                WAIT(s, "ve", f"vrow1_{i}")
                i0 = s.activation(out=v_rmsq[0:1, :], in_=v_re2[0:1, :], func=AF.Sqrt)
                INC(i0, "se", f"sqrt1_{i}")
            if 1 <= i <= NCH:
                c = i - 1
                for oc in range(HC):
                    WAIT(s, "pe", f"wv_{c}_{oc}")
                    i0 = s.copy(out=v_vc[:, oc, :], in_=pb[4 + oc % 2][:, 0:TC])
                    if oc % 2 == 1 or oc == HC - 1:
                        INC(i0, "se", f"evv_{c}_{oc}")
                        if oc % 2 == 1:
                            ALIAS("se", f"evv_{c}_{oc - 1}", f"evv_{c}_{oc}")
                    else:
                        pass
            if i < NCH:
                c = i
                if c >= 1:
                    WAIT(s, "ve", f"mix_{c - 1}")
                WAIT(s, "pe", f"bc1a_{c}")
                s.copy(out=v_ab1[:, 0:TC], in_=pb[2][:, 0:TC])
                WAIT(s, "pe", f"bc1_{c}")
                i0 = s.copy(out=v_ab1[:, TC:2 * TC], in_=pb[3][:, 0:TC])
                INC(i0, "se", f"ab1_{c}")
            if 2 <= i <= NCH + 1:
                c = i - 2
                if c >= 1:
                    WAIT(s, "ve", f"ln2_{c - 1}_{HC - 1}")
                WAIT(s, "pe", f"bc2a_{c}")
                s.copy(out=v_ab2[:, 0:TC], in_=pb[2][:, 0:TC])
                WAIT(s, "pe", f"bc2_{c}")
                i0 = s.copy(out=v_ab2[:, TC:2 * TC], in_=pb[3][:, 0:TC])
                INC(i0, "se", f"ab2_{c}")
            if 2 <= i <= NCH + 1:
                c = i - 2
                WAIT(s, "pe", f"mcolT_{c}")
                s.copy(out=mcols[:, c * SPT:(c + 1) * SPT],
                       in_=pb[4][:, 0:SPT])
                i0 = s.copy(out=iscols[:, c * SPT:(c + 1) * SPT],
                            in_=pb[5][:, 0:SPT])
                INC(i0, "se", f"mcev_{c}")
            if 1 <= i <= NCH:
                c = i - 1
                for oc in range(HC):
                    WAIT(s, "pe", f"wr_{c}_{oc}")
                    i0 = s.activation(out=v_rc[:, oc, :],
                                      in_=pb[6 + oc % 2][:, 0:TC],
                                      func=AF.Sigmoid)
                    if oc % 2 == 1 or oc == HC - 1:
                        INC(i0, "se", f"sig_{c}_{oc}")
                        if oc % 2 == 1:
                            ALIAS("se", f"sig_{c}_{oc - 1}", f"sig_{c}_{oc}")
                    else:
                        pass
            if 1 <= i <= NCH:
                casts(i - 1, v_cast2, "2")
        WAIT(s, "pe", "gsumbc_mm")
        i0 = s.copy(out=gsumbc[:, :], in_=pb[6][:, 0:NE])
        INC(i0, "se", "gsumbc")
        WAIT(s, "pe", "lbbc_mm")
        i0 = s.copy(out=lbbc[:, :], in_=pb[7][:, 0:NE])
        INC(i0, "se", "lbbc")
        WAIT(s, "ve", "logits")
        i0 = s.activation(out=g_sc[:, :], in_=g_lg[:, :], func=AF.Sigmoid)
        INC(i0, "se", "scores")
        EORDER_S = [NE, NE + 1] + list(range(NE))
        sq_keys = []
        for p in range(NCH):
            for si, e in enumerate(EORDER_S):
                for ic in range(ICH):
                    kc = e * ICH + ic
                    q = len(sq_keys)
                    sq_keys.append((p, kc))
                    if q == 0:
                        k_last = NCH * HC - 1
                        WAIT(s, "sp", f"ln2sp_{(k_last - 1) // HC}_{(k_last - 1) % HC}")
                        WAIT(s, "sp", f"ln2sp_{k_last // HC}_{k_last % HC}")
                    if q >= 2:
                        pk, kk = sq_keys[q - 2]
                        WAIT(s, "ve", f"hid_{pk}_{kk}")
                    WAIT(s, "pe", f"g_{p}_{e}_{ic}")
                    i0 = s.activation(out=v_gsb[:, (q % 2) * TC:(q % 2 + 1) * TC],
                                      in_=pb[q % 2][:, 0:TC], func=AF.Silu)
                    INC(i0, "se", f"silu_{p}_{e}_{ic}")

    def emit_pe(t):
        S["cur"] = t if not isinstance(t, _EngP) else None
        WAIT(t, "ve", "ones")
        def stats_mm(c, tag, cviews, blo, bhi):
            for hc in range(HC):
                WAIT(t, "se", f"cast{tag}_{c}_{hc}")
                if hc == 0:
                    bank_w(t, blo, ("ve", f"stread{tag}_{c}"))
                    bank_w(t, bhi, ("ve", f"stread{tag}_{c}"))
                t.matmul(pb[blo][0:1, 0:TC], ones_bfc[:, :],
                         cviews[:, hc % 2, 0, :],
                         start=(hc == 0), stop=(hc == HC - 1))
                i0 = t.matmul(pb[bhi][0:1, 0:TC], ones_bfc[:, :],
                              cviews[:, hc % 2, 1, :],
                              start=(hc == 0), stop=(hc == HC - 1))
                INC(i0, "pe", f"st{tag}mm_{c}_{hc}")
            ALIAS("pe", f"st{tag}_{c}", f"st{tag}mm_{c}_{HC - 1}")

        for i in range(NCH + 2):
            if i < NCH:
                stats_mm(i, "1", v_cast1, 0, 1)
            if 1 <= i <= NCH:
                c = i - 1
                WAIT(t, "ve", f"mix_{c}")
                if c == 0:
                    WAIT(t, "gw", "wo")
                for oc in range(HC):
                    bank_w(t, 4 + oc % 2, ("se", f"evv_{c}_{oc}"))
                    for hc in range(HC):
                        i0 = t.matmul(pb[4 + oc % 2][:, 0:TC],
                                      wvw(v_wv, oc, hc), v_vin[:, c % 2, hc, :],
                                      start=(hc == 0), stop=(hc == HC - 1))
                    INC(i0, "pe", f"wv_{c}_{oc}")
            if i < NCH:
                c = i
                WAIT(t, "ve", f"rows1_{c}")
                bank_w(t, 2, ("se", f"ab1_{c}"))
                i0 = t.matmul(pb[2][:, 0:TC], ones_fr[:, :], v_ris[0:1, :],
                              start=True, stop=True)
                INC(i0, "pe", f"bc1a_{c}")
                bank_w(t, 3, ("se", f"ab1_{c}"))
                i0 = t.matmul(pb[3][:, 0:TC], ones_fr[:, :], v_rb[0:1, :],
                              start=True, stop=True)
                INC(i0, "pe", f"bc1_{c}")
            if 2 <= i <= NCH + 1:
                c = i - 2
                WAIT(t, "ve", f"rows2_{c}")
                bank_w(t, 2, ("se", f"ab2_{c}"))
                i0 = t.matmul(pb[2][:, 0:TC], ones_fr[:, :], v_r2is[0:1, :],
                              start=True, stop=True)
                INC(i0, "pe", f"bc2a_{c}")
                bank_w(t, 3, ("se", f"ab2_{c}"))
                i0 = t.matmul(pb[3][:, 0:TC], ones_fr[:, :], v_r2b[0:1, :],
                              start=True, stop=True)
                INC(i0, "pe", f"bc2_{c}")
                bank_w(t, 4, ("se", f"mcev_{c}"))
                for sp in range(SPT):
                    i0 = t.matmul(pb[4][:, sp:sp + 1],
                                  v_r2mean[0:1, sp * 128:(sp + 1) * 128],
                                  ones_fr[0:1, 0:1], start=True, stop=True)
                INC(i0, "pe", f"mcolT_m_{c}")
                bank_w(t, 5, ("se", f"mcev_{c}"))
                for sp in range(SPT):
                    i0 = t.matmul(pb[5][:, sp:sp + 1],
                                  v_r2is[0:1, sp * 128:(sp + 1) * 128],
                                  ones_fr[0:1, 0:1], start=True, stop=True)
                INC(i0, "pe", f"mcolT_{c}")
                WAIT(t, "se", f"ab2_{c}")
                bank_w(t, 2, ("ve", f"rawev_{c}"))
                for hc in range(HC):
                    i0 = t.matmul(pb[2][0:8, 0:TC],
                                  gatew[:, hc * NE:(hc + 1) * NE],
                                  xr(c)[:, hc, :],
                                  start=(hc == 0), stop=(hc == HC - 1))
                INC(i0, "pe", f"raw_{c}")
            if 1 <= i <= NCH:
                c = i - 1
                for oc in range(HC):
                    bank_w(t, 6 + oc % 2, ("se", f"sig_{c}_{oc}"))
                    for hc in range(HC):
                        i0 = t.matmul(pb[6 + oc % 2][:, 0:TC],
                                      wvw(v_wr, oc, hc), v_rin[:, c % 2, hc, :],
                                      start=(hc == 0), stop=(hc == HC - 1))
                    INC(i0, "pe", f"wr_{c}_{oc}")
            if 1 <= i <= NCH:
                c = i - 1
                WAIT(t, "se", f"evv_{c}_{HC - 1}")
                for oc in range(HC):
                    bank_w(t, 6 + oc % 2, ("ve", f"epioc_{c}_{oc}"))
                    for vc in range(HC):
                        i0 = t.matmul(pb[6 + oc % 2][:, 0:TC],
                                      wvw(v_wo, oc, vc), v_vc[:, vc, :],
                                      start=(vc == 0), stop=(vc == HC - 1))
                    INC(i0, "pe", f"wo_{c}_{oc}")
            if 1 <= i <= NCH:
                stats_mm(i - 1, "2", v_cast2, 4, 5)
        WAIT(t, "ge", "ident")
        bank_w(t, 6, ("ve", "gsumr"))
        i0 = t.matmul(pb[6][0:1, 0:HC * NE], ones_fc[:, :], gatew[:, :],
                      start=True, stop=True)
        INC(i0, "pe", "gsum_mm")
        WAIT(t, "ve", "gsumr")
        bank_w(t, 6, ("se", "gsumbc"))
        i0 = t.matmul(pb[6][:, 0:NE], ones_fr[:, :], gsumr[:, :],
                      start=True, stop=True)
        INC(i0, "pe", "gsumbc_mm")
        bank_w(t, 7, ("se", "lbbc"))
        WAIT(t, "gw", "smalls")
        i0 = t.matmul(pb[7][:, 0:NE], ones_fr[:, :], lbrow[:, :],
                      start=True, stop=True)
        INC(i0, "pe", "lbbc_mm")
        for j in range(NSTRIP):
            bi = j % 2
            bank_w(t, bi, ("ve", f"t1_{j}"))
            i0 = t.matmul(pb[bi][:, (j // 2 % 2) * 8:(j // 2 % 2) * 8 + 8],
                          raw_em[0:8, j * 128:(j + 1) * 128], i8f[:, :],
                          start=True, stop=True)
            INC(i0, "pe", f"rawT_{j}")
        def gate_tail_a():
            WAIT(t, "ve", "wc")
            bank_w(t, 6, ("ve", "ccin"))
            i0 = t.matmul(pb[6][0:1, 0:NSTRIP * 8], ones_fc[:, :], g_wc[:, :],
                          start=True, stop=True)
            INC(i0, "pe", "part_wc")
            bank_w(t, 7, ("ve", "ccin"))
            i0 = t.matmul(pb[7][0:1, 0:NSTRIP * 8], ones_fc[:, :], g_sel[:, :],
                          start=True, stop=True)
            INC(i0, "pe", "part_sel")

        def gate_tail_b():
            WAIT(t, "ve", "wmrow")
            bank_w(t, 6, ("ve", "wmcol"))
            i0 = t.matmul(pb[6][0:8, 0:1], wmrow[:, :], ones_fr[0:1, 0:1],
                          start=True, stop=True)
            INC(i0, "pe", "wmcol_mm")
            for jg in range(NJG):
                bank_w(t, 7, ("ve", f"sem_{jg}"))
                n_in_g = min(4, NSTRIP - jg * 4)
                for jj in range(n_in_g):
                    j = jg * 4 + jj
                    i0 = t.matmul(pb[7][0:8, jj * 128:(jj + 1) * 128],
                                  g_sel[:, :].rearrange("p (j e) -> p j e", e=8)[:, j, :],
                                  i128f[:, :], start=True, stop=True)
                INC(i0, "pe", f"selT_{jg}")

        EORDER = [NE, NE + 1] + list(range(NE))

        def bselmms(p, es):
            for e in es:
                WAIT(t, "sx", f"srow_{p}_{e}")
                if p == 0:
                    bank_w(t, 4, ("ve", f"hsc_{e}"))
                else:
                    bank_w(t, 4, ("ve", f"hid_{p}_{e * ICH + ICH - 1}"))
                k = p * NE + e
                i0 = t.matmul(pb[4][:, 0:TC], ones_fr[:, :],
                              srow[0:1, (k % 2) * TC:(k % 2 + 1) * TC],
                              start=True, stop=True)
                INC(i0, "pe", f"bselmm_{p}_{e}")

        for p in range(NCH):
            WAIT(t, "sx", f"ln2t_{p}")
            for si, e in enumerate(EORDER):
                if p == 0 and si == 2:
                    gate_tail_a()
                if p > 0 and e < NE:
                    bselmms(p, [e])
                WAIT(t, "gw", f"gws_{p}_{e}")
                slot = (p * NPE + si) % 2
                for ic in range(ICH):
                    kc = e * ICH + ic
                    sq = si * ICH + ic
                    bank_w(t, sq % 2, ("se", f"silu_{p}_{e}_{ic}"))
                    for hc in range(HC):
                        i0 = t.matmul(pb[sq % 2][:, 0:TC],
                                      v_gwuw[:, slot, 0, hc, ic, :],
                                      v_ln2t[:, p % 2, hc, :],
                                      start=(hc == 0), stop=(hc == HC - 1))
                    INC(i0, "pe", f"g_{p}_{e}_{ic}")
                    bank_w(t, 2 + sq % 2, ("ve", f"hid_{p}_{kc}"))
                    for hc in range(HC):
                        i0 = t.matmul(pb[2 + sq % 2][:, 0:TC],
                                      v_gwuw[:, slot, 1, hc, ic, :],
                                      v_ln2t[:, p % 2, hc, :],
                                      start=(hc == 0), stop=(hc == HC - 1))
                    INC(i0, "pe", f"u_{p}_{e}_{ic}")
                ALIAS("pe", f"guexp_{p}_{si}", f"u_{p}_{e}_{ICH - 1}")
            ALIAS("pe", f"guend_{p}", f"u_{p}_{EORDER[-1]}_{ICH - 1}")
            if p == 0:
                gate_tail_b()
                bselmms(0, range(NE))
            for hc in range(HC):
                WAIT(t, "gw", f"dws_{p}_{hc}")
                bank_w(t, 5 + hc % 2, ("ve", f"evac_{p}_{hc}"))
                slot = (p * HC + hc) % 2
                for kc in range(KC):
                    if hc == 0 and kc == 0:
                        WAIT(t, "ve", f"hidend_{p}")
                    i0 = t.matmul(pb[5 + hc % 2][:, 0:TC],
                                  v_dws[:, slot, kc, :], v_hid[:, kc, :],
                                  start=(kc == 0), stop=(kc == KC - 1))
                INC(i0, "pe", f"dn_{p}_{hc}")

    def emit_vector(v):
        S["cur"] = v if not isinstance(v, _EngP) else None
        v.memset(carry[:, :], 0.0)
        v.memset(ones_bfc[:, :], 1.0)
        v.memset(ones_fc[:, :], 1.0)
        v.memset(ones_fr[:, :], 1.0)
        i0 = v.nop()
        INC(i0, "ve", "ones")
        rows_prev = [None]

        def rows(c, tag):
            if tag == "1":
                rm, ri, rb2 = v_rmean, v_ris, v_rb
                prev_pe = f"bc1_{c - 1}" if c >= 1 else None
                blo, bhi = 0, 1
            else:
                rm, ri, rb2 = v_r2mean, v_r2is, v_r2b
                prev_pe = f"mcolT_{c - 1}" if c >= 1 else None
                blo, bhi = 4, 5
            if prev_pe is not None:
                WAIT(v, "pe", prev_pe)
            if rows_prev[0] is not None:
                WAIT(v, "se", rows_prev[0])
            rows_prev[0] = f"sqrt{tag}_{c}"
            WAIT(v, "pe", f"st{tag}_{c}")
            v.tensor_scalar(out=rm[0:1, :], in0=pb[blo][0:1, 0:TC],
                            scalar1=invH, scalar2=None, op0=ALU.mult)
            v.tensor_scalar(out=v_re2[0:1, :], in0=pb[bhi][0:1, 0:TC],
                            scalar1=invH, scalar2=EPS,
                            op0=ALU.mult, op1=ALU.add)
            i0 = v.nop()
            INC(i0, "ve", f"stread{tag}_{c}")
            v.tensor_tensor(out=v_rmsq[0:1, :], in0=rm[0:1, :],
                            in1=rm[0:1, :], op=ALU.mult)
            v.drain()
            v.tensor_tensor(out=v_re2[0:1, :], in0=v_re2[0:1, :],
                            in1=v_rmsq[0:1, :], op=ALU.subtract)
            i0 = v.nop()
            INC(i0, "ve", f"vrow{tag}_{c}")
            WAIT(v, "se", f"sqrt{tag}_{c}")
            v.reciprocal(out=ri[0:1, :], in_=v_rmsq[0:1, :])
            v.drain()
            i0 = v.tensor_tensor(out=rb2[0:1, :], in0=rm[0:1, :],
                                 in1=ri[0:1, :], op=ALU.mult)
            INC(i0, "ve", f"rows{tag}_{c}")

        for i in range(NCH + 2):
            if 2 <= i <= NCH + 1:
                rows(i - 2, "2")
            if i < NCH:
                rows(i, "1")
            if i < NCH:
                c = i
                WAIT(v, "se", f"ab1_{c}")
                WAIT(v, "se", "dcol")
                if c >= 2:
                    WAIT(v, "pe", f"wr_{c - 2}_{HC - 1}")
                for hc in range(HC):
                    v.tensor_copy(out=v_zb[:, 0:1], in_=carry[:, hc:hc + 1])
                    v.tensor_tensor(out=v_lnb, in0=xr(c)[:, hc, :],
                                    in1=v_ab1[:, 0:TC], op=ALU.mult)
                    v.tensor_tensor(out=v_lnb, in0=v_lnb,
                                    in1=v_ab1[:, TC:2 * TC], op=ALU.subtract)
                    v.tensor_tensor_scan(
                        out=v_zb[:, 1:TC + 1],
                        data0=dcol[:, hc:hc + 1].to_broadcast([128, TC]),
                        data1=v_lnb, initial=v_zb[:, 0:1],
                        op0=ALU.mult, op1=ALU.add)
                    v.tensor_tensor(out=v_diff, in0=v_lnb, in1=v_zb[:, 0:TC],
                                    op=ALU.subtract)
                    v.scalar_tensor_tensor(
                        out=v_vin[:, c % 2, hc, :], in0=v_diff,
                        scalar=tmvc[:, hc:hc + 1], in1=v_zb[:, 0:TC],
                        op0=ALU.mult, op1=ALU.add)
                    v.scalar_tensor_tensor(
                        out=v_rin[:, c % 2, hc, :], in0=v_diff,
                        scalar=tmrc[:, hc:hc + 1], in1=v_zb[:, 0:TC],
                        op0=ALU.mult, op1=ALU.add)
                    v.tensor_copy(out=carry[:, hc:hc + 1],
                                  in_=v_zb[:, TC:TC + 1])
                    if hc == HC - 1:
                        i0 = v.nop()
                        INC(i0, "ve", f"mix_{c}")
            if 1 <= i <= NCH:
                c = i - 1
                for oc in range(HC):
                    WAIT(v, "pe", f"wo_{c}_{oc}")
                    WAIT(v, "se", f"sig_{c}_{oc}")
                    v.tensor_tensor(out=v_diff, in0=pb[6 + oc % 2][:, 0:TC],
                                    in1=v_rc[:, oc, :], op=ALU.mult)
                    i0 = v.tensor_tensor(out=xr(c)[:, oc, :], in0=v_diff,
                                         in1=xr(c)[:, oc, :], op=ALU.add)
                    if oc % 2 == 1 or oc == HC - 1:
                        INC(i0, "ve", f"epioc_{c}_{oc}")
                        if oc % 2 == 1:
                            ALIAS("ve", f"epioc_{c}_{oc - 1}", f"epioc_{c}_{oc}")
                i0 = v.nop()
                INC(i0, "ve", f"epi_{c}")
            if 2 <= i <= NCH + 1:
                c = i - 2
                WAIT(v, "se", f"ab2_{c}")
                for hc in range(HC):
                    k = c * HC + hc
                    if k >= 2:
                        pk, hk = divmod(k - 2, HC)
                        WAIT(v, "sp", f"ln2sp_{pk}_{hk}")
                    v.tensor_tensor(out=v_lnb, in0=xr(c)[:, hc, :],
                                    in1=v_ab2[:, 0:TC], op=ALU.mult)
                    i0 = v.tensor_tensor(out=v_ln2bf[:, k % 2, :],
                                         in0=v_lnb, in1=v_ab2[:, TC:2 * TC],
                                         op=ALU.subtract)
                    if hc % 2 == 1 or hc == HC - 1:
                        INC(i0, "ve", f"ln2_{c}_{hc}")
                        if hc % 2 == 1:
                            ALIAS("ve", f"ln2_{c}_{hc - 1}", f"ln2_{c}_{hc}")
                WAIT(v, "pe", f"raw_{c}")
                v.tensor_copy(out=raw_em[:, c * TC:(c + 1) * TC],
                              in_=pb[2][0:8, 0:TC])
                i0 = v.nop()
                INC(i0, "ve", f"rawev_{c}")
        WAIT(v, "pe", "gsum_mm")
        i0 = v.tensor_reduce(
            out=gsumr[0:1, :].unsqueeze(2),
            in_=pb[6][0:1, 0:HC * NE].rearrange("p (h e) -> p e h", e=NE),
            axis=AXX, op=ALU.add)
        i0 = v.nop()
        INC(i0, "ve", "gsumr")
        WAIT(v, "se", "gsumbc")
        for j in range(NSTRIP):
            WAIT(v, "pe", f"rawT_{j}")
            reg = (j // 2 % 2) * 8
            v.scalar_tensor_tensor(
                out=g_t1[:, j * 8:(j + 1) * 8], in0=gsumbc[:, :],
                scalar=mcols[:, j:j + 1], in1=pb[j % 2][:, reg:reg + 8],
                op0=ALU.mult, op1=ALU.subtract)
            i0 = v.nop()
            INC(i0, "ve", f"t1_{j}")
        for j in range(NSTRIP):
            v.tensor_scalar(out=g_t1[:, j * 8:(j + 1) * 8],
                            in0=g_t1[:, j * 8:(j + 1) * 8],
                            scalar1=iscols[:, j:j + 1], scalar2=-1.0,
                            op0=ALU.mult, op1=ALU.mult)
        v.drain()
        WAIT(v, "se", "lbbc")
        for j in range(NSTRIP):
            i0 = v.tensor_tensor(out=g_lg[:, j * 8:(j + 1) * 8],
                                 in0=g_t1[:, j * 8:(j + 1) * 8],
                                 in1=lbbc[:, :], op=ALU.add)
        i0 = v.nop()
        INC(i0, "ve", "logits")
        WAIT(v, "se", "scores")
        v.reduce_max(out=g_m1[:, :].unsqueeze(2),
                     in_=g_sc[:, :].rearrange("p (j e) -> p j e", e=8),
                     axis=AXX)
        v.drain()
        v.tensor_tensor(out=g_lg[:, :].rearrange("p (j e) -> p j e", e=8),
                        in0=g_sc[:, :].rearrange("p (j e) -> p j e", e=8),
                        in1=g_m1[:, :].unsqueeze(2).to_broadcast([128, NSTRIP, 8]),
                        op=ALU.is_ge)
        v.drain()
        v.scalar_tensor_tensor(out=g_lg[:, :], in0=g_lg[:, :], scalar=1e30,
                               in1=g_sc[:, :], op0=ALU.mult, op1=ALU.subtract)
        v.drain()
        v.tensor_reduce(out=g_m2[:, :].unsqueeze(2),
                        in_=g_lg[:, :].rearrange("p (j e) -> p j e", e=8),
                        axis=AXX, op=ALU.min, negate=True)
        v.drain()
        v.tensor_tensor(out=g_sel[:, :].rearrange("p (j e) -> p j e", e=8),
                        in0=g_sc[:, :].rearrange("p (j e) -> p j e", e=8),
                        in1=g_m2[:, :].unsqueeze(2).to_broadcast([128, NSTRIP, 8]),
                        op=ALU.is_ge)
        v.tensor_tensor(out=g_s12[:, :], in0=g_m1[:, :], in1=g_m2[:, :],
                        op=ALU.add)
        v.drain()
        v.tensor_scalar(out=g_s12[:, :], in0=g_s12[:, :], scalar1=1e-8,
                        scalar2=None, op0=ALU.add)
        v.drain()
        v.reciprocal(out=g_i12[:, :], in_=g_s12[:, :])
        v.tensor_tensor(out=g_wc[:, :], in0=g_sc[:, :], in1=g_sel[:, :],
                        op=ALU.mult)
        v.drain()
        i0 = v.tensor_tensor(
            out=g_wc[:, :].rearrange("p (j e) -> p j e", e=8),
            in0=g_wc[:, :].rearrange("p (j e) -> p j e", e=8),
            in1=g_i12[:, :].unsqueeze(2).to_broadcast([128, NSTRIP, 8]),
            op=ALU.mult)
        i0 = v.nop()
        INC(i0, "ve", "wc")

        EORDER_V = [NE, NE + 1] + list(range(NE))
        qv = [0]

        def hid_ops(p, si_list):
            for si in si_list:
                e = EORDER_V[si]
                for ic in range(ICH):
                    kc = e * ICH + ic
                    q = qv[0]
                    qv[0] += 1
                    WAIT(v, "se", f"silu_{p}_{e}_{ic}")
                    WAIT(v, "pe", f"u_{p}_{e}_{ic}")
                    gs = v_gsb[:, (q % 2) * TC:(q % 2 + 1) * TC]
                    if e < NE and p > 0:
                        WAIT(v, "pe", f"bselmm_{p}_{e}")
                        v.tensor_tensor(out=v_h1b, in0=gs,
                                        in1=pb[2 + q % 2][:, 0:TC],
                                        op=ALU.mult)
                        v.tensor_tensor(out=v_hid[:, kc, :], in0=v_h1b,
                                        in1=pb[4][:, 0:TC], op=ALU.mult)
                    else:
                        v.tensor_tensor(out=v_hid[:, kc, :], in0=gs,
                                        in1=pb[2 + q % 2][:, 0:TC],
                                        op=ALU.mult)
                    i0 = v.nop()
                    INC(i0, "ve", f"hid_{p}_{kc}")

        def hid_scale0():
            for e in range(NE):
                WAIT(v, "pe", f"bselmm_0_{e}")
                for ic in range(ICH):
                    kc = e * ICH + ic
                    v.tensor_tensor(out=v_hid[:, kc, :], in0=v_hid[:, kc, :],
                                    in1=pb[4][:, 0:TC], op=ALU.mult)
                i0 = v.nop()
                INC(i0, "ve", f"hsc_{e}")

        hid_ops(0, [0, 1])
        WAIT(v, "pe", "part_wc")
        v.tensor_reduce(
            out=ccin_sb[0:1, 0:8].unsqueeze(2),
            in_=pb[6][0:1, 0:NSTRIP * 8].rearrange("p (j e) -> p e j", e=8),
            axis=AXX, op=ALU.add)
        WAIT(v, "pe", "part_sel")
        v.tensor_reduce(
            out=ccin_sb[0:1, 8:16].unsqueeze(2),
            in_=pb[7][0:1, 0:NSTRIP * 8].rearrange("p (j e) -> p e j", e=8),
            axis=AXX, op=ALU.add)
        i0 = v.nop()
        INC(i0, "ve", "ccin")
        WAIT(v, "gw", "ccout")
        v.tensor_scalar_max(out=ccout_sb[0:1, 8:16], in0=ccout_sb[0:1, 8:16],
                            scalar1=1.0)
        v.drain()
        v.reciprocal(out=ccout_sb[0:1, 8:16], in_=ccout_sb[0:1, 8:16])
        v.drain()
        v.tensor_tensor(out=wmrow[:, :], in0=ccout_sb[0:1, 0:8],
                        in1=ccout_sb[0:1, 8:16], op=ALU.mult)
        i0 = v.nop()
        INC(i0, "ve", "wmrow")
        for p in range(NCH):
            if p >= 1:
                WAIT(v, "pe", f"dn_{p - 1}_{HC - 1}")
                hid_ops(p, range(2))
            hid_ops(p, range(2, NPE))
            if p == 0:
                WAIT(v, "pe", "wmcol_mm")
                i0 = v.tensor_copy(out=wmcol[:, :], in_=pb[6][0:8, 0:1])
                v.drain()
                INC(i0, "ve", "wmcol")
                for jg in range(NJG):
                    WAIT(v, "pe", f"selT_{jg}")
                    n_in_g = min(4, NSTRIP - jg * 4)
                    v.tensor_scalar(
                        out=s_em[:, jg * 512:jg * 512 + n_in_g * 128],
                        in0=pb[7][0:8, 0:n_in_g * 128],
                        scalar1=wmcol[:, :], scalar2=None, op0=ALU.mult)
                    i0 = v.nop()
                    INC(i0, "ve", f"sem_{jg}")
                i0 = v.nop()
                INC(i0, "ve", "semrows")
                hid_scale0()
            i0 = v.nop()
            INC(i0, "ve", f"hidend_{p}")
            WAIT(v, "sx", f"x1t_{p}")
            for hc in range(HC):
                WAIT(v, "pe", f"dn_{p}_{hc}")
                v.tensor_tensor(out=v_x1t[:, p % 2, hc, :],
                                in0=pb[5 + hc % 2][:, 0:TC],
                                in1=v_x1t[:, p % 2, hc, :], op=ALU.add)
                i0 = v.nop()
                INC(i0, "ve", f"evac_{p}_{hc}")
            i0 = v.nop()
            INC(i0, "ve", f"evacend_{p}")

    def run_all(engs):
        emit_gpsimd(engs["gpsimd"])
        emit_sync(engs["sync"])
        emit_scalar(engs["scalar"])
        emit_pe(engs["tensor"])
        emit_vector(engs["vector"])

    dummies = {k: _DUMMY_ENG for k in
               ["gpsimd", "sync", "scalar", "tensor", "vector"]}
    reset(False)
    run_all(dummies)
    led = dict(S["led"])
    nc._debug_led = led

    with nc.Block() as block:

        def mk(name):
            def f(eng):
                reset(True)
                S["led"] = led
                engs = dict(dummies)
                if cfg.get("paranoid") and name in ("vector", "scalar", "gpsimd"):
                    eng = _ParanoidEng(eng)
                engs[name] = eng
                run_all(engs)
            f.__name__ = f"_{name}"
            return f

        block.gpsimd(mk("gpsimd"))
        block.sync(mk("sync"))
        block.scalar(mk("scalar"))
        block.tensor(mk("tensor"))
        block.vector(mk("vector"))

    return nc



def _pm_proj(W, HC):
    O = W.shape[0]
    OC = O // 128
    WT = np.ascontiguousarray(W.T)
    return np.ascontiguousarray(
        WT.reshape(HC, 128, OC, 128).transpose(1, 2, 0, 3).reshape(128, -1)
    ).astype(np.float32)


def _pm_col(vec):
    n = vec.shape[0]
    return np.ascontiguousarray(vec.reshape(n // 128, 128).T).astype(np.float32)


def _pm_gu(M, HC, ICH):
    MT = np.ascontiguousarray(M.T)
    return MT.reshape(HC, 128, ICH, 128).transpose(1, 0, 2, 3).reshape(128, -1)


def host_inputs(cfg, inputs):
    HC, T, NE, ICH = cfg["HC"], cfg["T"], cfg["NE"], cfg["ICH"]
    NPE = NE + 2
    KC = NPE * ICH
    I = ICH * 128

    f = lambda k: np.asarray(inputs[k], dtype=np.float32)
    x = f("L0_attn_x")
    egw, euw, edw = (f("L0_ffn_expert_gate_w"), f("L0_ffn_expert_up_w"),
                     f("L0_ffn_expert_down_w"))
    sgw, suw, sdw = (f("L0_ffn_shared_gate_w"), f("L0_ffn_shared_up_w"),
                     f("L0_ffn_shared_down_w"))
    gw = f("L0_ffn_gate_w")

    shared = {
        "wv": _pm_proj(f("L0_attn_Wv"), HC),
        "wr": _pm_proj(f("L0_attn_Wr"), HC),
        "wo": _pm_proj(f("L0_attn_Wo"), HC),
        "td": _pm_col(f("L0_attn_time_decay")),
        "tmv": _pm_col(f("L0_attn_time_mix_v")),
        "tmr": _pm_col(f("L0_attn_time_mix_r")),
        "gatew": np.ascontiguousarray(
            gw.T.reshape(HC, 128, NE).transpose(1, 0, 2).reshape(128, -1)
        ).astype(np.float32),
        "lbrow": np.ascontiguousarray(
            f("L0_ffn_lb_bias").reshape(1, NE)).astype(np.float32),
    }
    gu = []
    for e in range(NE):
        gu.append(np.stack([_pm_gu(egw[e], HC, ICH), _pm_gu(euw[e], HC, ICH)],
                           axis=1).reshape(128, -1))
    gu.append(np.stack([_pm_gu(sgw[0:I], HC, ICH), _pm_gu(suw[0:I], HC, ICH)],
                       axis=1).reshape(128, -1))
    gu.append(np.stack([_pm_gu(sgw[I:2 * I], HC, ICH),
                        _pm_gu(suw[I:2 * I], HC, ICH)], axis=1).reshape(128, -1))
    shared["gwuw"] = np.ascontiguousarray(np.stack(gu, axis=0)).astype(np.float32)
    dwT = np.concatenate([edw[e].T for e in range(NE)] + [sdw.T], axis=0)
    shared["dwcat"] = np.ascontiguousarray(
        dwT.reshape(KC, 128, HC, 128).transpose(1, 2, 0, 3).reshape(128, -1)
    ).astype(np.float32)

    in_maps = []
    for b in range(cfg["ncores"]):
        m = dict(shared)
        m["x"] = np.ascontiguousarray(
            x[b].T.reshape(HC, 128, T).transpose(1, 0, 2).reshape(128, -1)
        ).astype(np.float32)
        in_maps.append(m)
    return in_maps


def host_output(cfg, results):
    HC, T = cfg["HC"], cfg["T"]
    H = 128 * HC
    outs = []
    for r in results:
        arr = np.asarray(r["out"]).reshape(128, HC, T).transpose(1, 0, 2)
        outs.append(arr.reshape(H, T).T)
    return np.ascontiguousarray(np.stack(outs, axis=0)).astype(np.float32)


def kernel(**inputs):
    cfg = FULL_CFG
    nc = build_program(cfg)
    in_maps = host_inputs(cfg, inputs)
    res = run_bass_kernel_spmd(nc, in_maps, core_ids=list(range(cfg["ncores"])))
    return host_output(cfg, res.results)


# version 09
# speedup vs baseline: 1.0235x; 1.0235x over previous
import numpy as np
from contextlib import ExitStack

import concourse.bass as bass
import concourse.mybir as mybir
from concourse.bass_utils import run_bass_kernel_spmd

F32 = mybir.dt.float32
BF16 = mybir.dt.bfloat16
I32 = mybir.dt.int32
ALU = mybir.AluOpType
AF = mybir.ActivationFunctionType
AXX = mybir.AxisListType.X

FULL_CFG = dict(ncores=8, HC=8, T=2048, TC=512, NE=8, ICH=4)


class _Dummy:
    def then_inc(self, *a, **k):
        return self


class _EngP:

    def __getattr__(self, k):
        return lambda *a, **kw: _Dummy()


_DUMMY_ENG = _EngP()


_COMPUTE_OPS = {
    "tensor_tensor", "tensor_scalar", "tensor_copy", "tensor_add",
    "tensor_sub", "tensor_mul", "tensor_max", "tensor_scalar_add",
    "tensor_scalar_mul", "tensor_scalar_max", "tensor_scalar_min",
    "tensor_single_scalar", "scalar_tensor_tensor", "tensor_tensor_scan",
    "tensor_reduce", "reduce_max", "reduce_sum", "reciprocal", "memset",
    "activation", "copy", "add", "mul", "iota", "select", "copy_predicated",
}


class _ParanoidEng:

    def __init__(self, eng):
        self._e = eng

    def __getattr__(self, k):
        a = getattr(self._e, k)
        if k in _COMPUTE_OPS:
            def wrapped(*args, **kw):
                self._e.drain()
                return a(*args, **kw)
            return wrapped
        return a


def build_program(cfg, dbg_out=False):
    ncores = cfg["ncores"]
    HC, T, TC, NE, ICH = (cfg["HC"], cfg["T"], cfg["TC"], cfg["NE"],
                          cfg["ICH"])
    NCH = T // TC
    NPE = NE + 2
    KC = NPE * ICH
    NSTRIP = T // 128
    SPT = TC // 128
    NJG = (NSTRIP + 3) // 4
    invH = 1.0 / (128 * HC)
    EPS = 1e-5

    nc = bass.Bass("TRN2", target_bir_lowering=False,
                   detect_race_conditions=bool(cfg.get("race", False)))

    d_x = nc.dram_tensor("x", [128, HC * T], F32, kind="ExternalInput")
    d_wv = nc.dram_tensor("wv", [128, HC * HC * 128], F32, kind="ExternalInput")
    d_wr = nc.dram_tensor("wr", [128, HC * HC * 128], F32, kind="ExternalInput")
    d_wo = nc.dram_tensor("wo", [128, HC * HC * 128], F32, kind="ExternalInput")
    d_td = nc.dram_tensor("td", [128, HC], F32, kind="ExternalInput")
    d_tmv = nc.dram_tensor("tmv", [128, HC], F32, kind="ExternalInput")
    d_tmr = nc.dram_tensor("tmr", [128, HC], F32, kind="ExternalInput")
    d_gatew = nc.dram_tensor("gatew", [128, HC * NE], F32, kind="ExternalInput")
    d_lbrow = nc.dram_tensor("lbrow", [1, NE], F32, kind="ExternalInput")
    d_gwuw = nc.dram_tensor("gwuw", [NPE, 128, 2 * HC * ICH * 128], F32,
                            kind="ExternalInput")
    d_dw = nc.dram_tensor("dwcat", [128, HC * KC * 128], F32,
                          kind="ExternalInput")
    d_out = nc.dram_tensor("out", [128, HC * T], F32, kind="ExternalOutput")
    d_gwuwb = nc.dram_tensor("gwuwb", [NPE, 128, 2 * HC * ICH * 128], BF16)
    d_dwb = nc.dram_tensor("dwb", [128, HC * KC * 128], BF16)
    dbg_kind = dict(kind="ExternalOutput") if dbg_out else {}
    d_x1 = nc.dram_tensor("x1d", [128, HC * T], F32, **dbg_kind)
    d_ln2 = nc.dram_tensor("ln2d", [128, HC * T], BF16, **dbg_kind)
    d_sem = nc.dram_tensor("semrows", [8, T], F32, **dbg_kind)
    d_ccin = nc.dram_tensor("ccin", [1, 16], F32)
    d_ccout = nc.dram_tensor("ccout", [1, 16], F32)

    st = ExitStack()
    nc._ctx_keepalive = st

    def sb(name, shape, dt):
        return st.enter_context(nc.sbuf_tensor(name, shape, dt))

    ones_bfc = sb("ones_bfc", [128, 1], BF16)
    ones_fc = sb("ones_fc", [128, 1], F32)
    ones_fr = sb("ones_fr", [1, 128], F32)
    i128f = sb("i128f", [128, 128], F32)
    i8f = sb("i8f", [8, 8], F32)
    itmp = sb("itmp", [128, 128], I32)
    dcol = sb("dcol", [128, HC], F32)
    dtmp = sb("dtmp", [128, HC], F32)
    tmvc = sb("tmvc", [128, HC], F32)
    tmrc = sb("tmrc", [128, HC], F32)
    gatew = sb("gatew_sb", [128, HC * NE], F32)
    lbrow = sb("lbrow_sb", [1, NE], F32)
    carry = sb("carry", [128, HC], F32)
    rawsem = sb("rawsem", [8, T], F32)
    raw_em = rawsem
    s_em = rawsem
    g_t1 = sb("g_t1", [128, NSTRIP * 8], F32)
    g_lg = sb("g_lg", [128, NSTRIP * 8], F32)
    g_sc = sb("g_sc", [128, NSTRIP * 8], F32)
    g_sel = sb("g_sel", [128, NSTRIP * 8], F32)
    g_wc = sb("g_wc", [128, NSTRIP * 8], F32)
    g_m1 = sb("g_m1", [128, NSTRIP], F32)
    g_m2 = sb("g_m2", [128, NSTRIP], F32)
    g_s12 = sb("g_s12", [128, NSTRIP], F32)
    g_i12 = sb("g_i12", [128, NSTRIP], F32)
    mcols = sb("mcols", [128, NSTRIP], F32)
    iscols = sb("iscols", [128, NSTRIP], F32)
    gsumr = sb("gsumr", [1, NE], F32)
    gsumbc = sb("gsumbc", [128, NE], F32)
    lbbc = sb("lbbc", [128, NE], F32)
    wmrow = sb("wmrow", [1, NE], F32)
    wmcol = sb("wmcol", [8, 1], F32)
    ccin_sb = sb("ccin_sb", [1, 16], F32)
    ccout_sb = sb("ccout_sb", [1, 16], F32)
    srow = sb("srow", [1, 2 * TC], F32)

    LAY = {}
    off = [0]

    def lay(name, nbytes):
        a = off[0]
        off[0] = (a + nbytes + 31) & ~31
        LAY[name] = a

    lay("xres", 3 * HC * TC * 4)
    lay("wv", HC * HC * 128 * 2)
    lay("wr", HC * HC * 128 * 2)
    lay("wo", HC * HC * 128 * 2)
    lay("vin", 2 * HC * TC * 2)
    lay("rin", 2 * HC * TC * 2)
    lay("v_c", HC * TC * 2)
    lay("r_c", HC * TC * 2)
    lay("lnb", TC * 4)
    lay("diffb", TC * 4)
    lay("zb", (TC + 1) * 4)
    lay("cast1", 2 * 2 * TC * 2)
    lay("cast2", 2 * 2 * TC * 2)
    lay("ln2bf", 4 * TC * 2)
    lay("ab1", 2 * TC * 4)
    lay("ab2", 2 * TC * 4)
    lay("r_mean", TC * 4)
    lay("r_e2", TC * 4)
    lay("r_msq", TC * 4)
    lay("r_is", TC * 4)
    lay("r_b", TC * 4)
    lay("r2_mean", TC * 4)
    lay("r2_is", TC * 4)
    lay("r2_b", TC * 4)
    attn_end = off[0]
    off[0] = 0
    lay("x1t", 2 * HC * TC * 4)
    lay("ln2t", 2 * HC * TC * 2)
    assert off[0] <= LAY["wv"], "x1t/ln2t must stay inside xres"
    off[0] = LAY["wv"]
    lay("gwuws", 2 * 2 * HC * ICH * 128 * 2)
    lay("dws", 2 * KC * 128 * 2)
    lay("hid", KC * TC * 2)
    lay("gsb", 2 * TC * 4)
    lay("h1b", TC * 4)
    ffn_end = off[0]
    ARENA = max(attn_end, ffn_end)
    arena = sb("arena", [128, ARENA], mybir.dt.uint8)
    assert LAY["gwuws"] == LAY["wv"]

    def view(name, nbytes, dt):
        a = LAY[name]
        return arena[:, a:a + nbytes].bitcast(dt)

    v_xres = view("xres", 3 * HC * TC * 4, F32).rearrange(
        "p (s h t) -> p s h t", s=3, h=HC)
    v_wv = view("wv", HC * HC * 128 * 2, BF16)
    v_wr = view("wr", HC * HC * 128 * 2, BF16)
    v_wo = view("wo", HC * HC * 128 * 2, BF16)
    v_vin = view("vin", 2 * HC * TC * 2, BF16).rearrange(
        "p (s h t) -> p s h t", s=2, h=HC)
    v_rin = view("rin", 2 * HC * TC * 2, BF16).rearrange(
        "p (s h t) -> p s h t", s=2, h=HC)
    v_vc = view("v_c", HC * TC * 2, BF16).rearrange("p (h t) -> p h t", h=HC)
    v_rc = view("r_c", HC * TC * 2, BF16).rearrange("p (h t) -> p h t", h=HC)
    v_lnb = view("lnb", TC * 4, F32)
    v_diff = view("diffb", TC * 4, F32)
    v_zb = view("zb", (TC + 1) * 4, F32)
    v_cast1 = view("cast1", 2 * 2 * TC * 2, BF16).rearrange(
        "p (s w t) -> p s w t", s=2, w=2)
    v_cast2 = view("cast2", 2 * 2 * TC * 2, BF16).rearrange(
        "p (s w t) -> p s w t", s=2, w=2)
    v_ln2bf = view("ln2bf", 4 * TC * 2, BF16).rearrange(
        "p (s t) -> p s t", s=4)
    v_ab1 = view("ab1", 2 * TC * 4, F32)
    v_ab2 = view("ab2", 2 * TC * 4, F32)
    v_rmean = view("r_mean", TC * 4, F32)
    v_re2 = view("r_e2", TC * 4, F32)
    v_rmsq = view("r_msq", TC * 4, F32)
    v_ris = view("r_is", TC * 4, F32)
    v_rb = view("r_b", TC * 4, F32)
    v_r2mean = view("r2_mean", TC * 4, F32)
    v_r2is = view("r2_is", TC * 4, F32)
    v_r2b = view("r2_b", TC * 4, F32)
    v_x1t = view("x1t", 2 * HC * TC * 4, F32).rearrange(
        "p (s h t) -> p s h t", s=2, h=HC)
    v_ln2t = view("ln2t", 2 * HC * TC * 2, BF16).rearrange(
        "p (s h t) -> p s h t", s=2, h=HC)
    v_gwuw = view("gwuws", 2 * 2 * HC * ICH * 128 * 2, BF16).rearrange(
        "p (s g h i j) -> p s g h i j", s=2, g=2, h=HC, i=ICH)
    v_dws = view("dws", 2 * KC * 128 * 2, BF16).rearrange(
        "p (s k j) -> p s k j", s=2, k=KC)
    v_hid = view("hid", KC * TC * 2, BF16).rearrange("p (k t) -> p k t", k=KC)
    v_gsb = view("gsb", 2 * TC * 4, F32)
    v_h1b = view("h1b", TC * 4, F32)

    def xr(c):
        return v_xres[:, c % 3, :, :]

    def wvw(w, oc, hc):
        o = (oc * HC + hc) * 128
        return w[:, o:o + 128]

    pb = [st.enter_context(nc.psum_tensor(f"pb{i}", [128, 512], F32))
          for i in range(8)]

    sem_names = ["ge", "cc", "pe", "ve", "se",
                 "sm", "wat", "cst", "gwa", "gwb", "dwa", "dwb", "ccb",
                 "sx0", "sx1", "sx2", "x1s0", "x1s1", "x1s2", "x1s3",
                 "l2a", "l2b", "l2c", "l2d", "smr", "xta", "xtb", "lta", "ltb",
                 "sra", "srb", "out"]
    SEM = {n: st.enter_context(nc.semaphore(n)) for n in sem_names}

    def dsem(key):
        if key == "smalls":
            return "sm"
        if key in ("wv", "wr", "wo"):
            return "wat"
        if key == "dwcast" or key.startswith("gwcast"):
            return "cst"
        if key.startswith("gws_"):
            _, p, e = key.split("_")
            eord = [NE, NE + 1] + list(range(NE))
            si = eord.index(int(e))
            return "gwa" if (int(p) * NPE + si) % 2 == 0 else "gwb"
        if key.startswith("dws_"):
            _, p, hc = key.split("_")
            return "dwa" if (int(p) * HC + int(hc)) % 2 == 0 else "dwb"
        if key in ("ccin_d", "ccout"):
            return "ccb"
        if key.startswith("x_"):
            return f"sx{int(key[2:]) % 3}"
        if key.startswith("x1sp_"):
            return f"x1s{int(key[5:])}"
        if key.startswith("ln2sp_"):
            _, c, hc = key.split("_")
            return ("l2a", "l2b", "l2c", "l2d")[(int(c) * HC + int(hc)) % 4]
        if key == "semrows_d":
            return "smr"
        if key.startswith("x1t_"):
            return "xta" if int(key[4:]) % 2 == 0 else "xtb"
        if key.startswith("ln2t_"):
            return "lta" if int(key[5:]) % 2 == 0 else "ltb"
        if key.startswith("srow_"):
            _, p, e = key.split("_")
            return "sra" if (int(p) * NE + int(e)) % 2 == 0 else "srb"
        if key.startswith("out_"):
            return "out"
        raise KeyError(key)

    S = {}

    def reset(emit):
        S["emit"] = emit
        S["cnt"] = {n: 0 for n in sem_names}
        S["pe_rd"] = {}
        if not emit:
            S["led"] = {}

    def INC(inst, sem, key=None, n=1):
        if sem in ("gw", "sx", "sp"):
            sem = dsem(key)
            n = 16
        S["cnt"][sem] += n
        if key is not None and not S["emit"]:
            S["led"][(sem, key)] = S["cnt"][sem]
        if S["emit"] and inst is not None and not isinstance(inst, _Dummy):
            if sem in ("se", "ve") and S.get("cur") is not None and cfg.get("dr", 1):
                S["cur"].drain()
                inst = S["cur"].nop()
            inst.then_inc(SEM[sem], n)

    def ALIAS(sem, key, src):
        if not S["emit"]:
            S["led"][(sem, key)] = S["led"][(sem, src)]

    def WAIT(eng, sem, key):
        if sem in ("gw", "sx", "sp"):
            sem = dsem(key)
        if S["emit"]:
            v = S["led"].get((sem, key))
            assert v is not None, f"missing ledger {(sem, key)}"
            eng.wait_ge(SEM[sem], v)

    def bank_w(eng, bi, reader):
        prev = S["pe_rd"].get(bi)
        if prev is not None:
            WAIT(eng, *prev)
        S["pe_rd"][bi] = reader

    def emit_gpsimd(g):
        S["cur"] = g if not isinstance(g, _EngP) else None
        for dst, src in ((dtmp, d_td), (tmvc, d_tmv), (tmrc, d_tmr),
                         (gatew, d_gatew)):
            i0 = g.dma_start(out=dst[:, :], in_=src[:, :])
            INC(i0, "gw", "smalls")
        i0 = g.dma_start(out=lbrow[:, :], in_=d_lbrow[:, :])
        INC(i0, "gw", "smalls")
        g.iota(itmp[:, :], pattern=[[1, 128]], base=0, channel_multiplier=-1)
        g.drain()
        g.tensor_scalar(out=i128f[:, :], in0=itmp[:, :], scalar1=0.0,
                        scalar2=None, op0=ALU.is_equal)
        g.iota(itmp[0:8, 0:8], pattern=[[1, 8]], base=0, channel_multiplier=-1)
        g.drain()
        i0 = g.tensor_scalar(out=i8f[:, :], in0=itmp[0:8, 0:8], scalar1=0.0,
                             scalar2=None, op0=ALU.is_equal)
        g.drain()
        INC(i0, "ge", "ident")
        i0 = g.dma_start(out=v_wv, in_=d_wv[:, :])
        INC(i0, "gw", "wv")
        i0 = g.dma_start(out=v_wr, in_=d_wr[:, :])
        INC(i0, "gw", "wr")
        i0 = g.dma_start(out=v_wo, in_=d_wo[:, :])
        INC(i0, "gw", "wo")
        pace = {2: 0, 5: 1, 8: 2}
        for e in range(NPE):
            if e in pace:
                WAIT(g, "sp", f"x1sp_{min(pace[e], NCH - 1)}")
            i0 = g.dma_start(out=d_gwuwb[e, :, :], in_=d_gwuw[e, :, :])
            INC(i0, "gw", f"gwcast{e}")
        i0 = g.dma_start(out=d_dwb[:, :], in_=d_dw[:, :])
        INC(i0, "gw", "dwcast")
        EORDER_G = [NE, NE + 1] + list(range(NE))
        for s0 in range(min(2, NPE)):
            e0 = EORDER_G[s0]
            WAIT(g, "pe", f"wr_{NCH - 1}_{HC - 1}")
            WAIT(g, "gw", f"gwcast{NPE - 1}")
            i0 = g.dma_start(out=v_gwuw[:, s0 % 2, :, :, :, :],
                             in_=d_gwuwb[e0, :, :])
            INC(i0, "gw", f"gws_0_{e0}")
        WAIT(g, "gw", "ccin_d")
        i0 = g.collective_compute(
            "AllReduce", ALU.add, replica_groups=[list(range(ncores))],
            ins=[d_ccin.ap().opt()], outs=[d_ccout.ap().opt()])
        INC(i0, "cc", "ccdone")
        for p in range(NCH):
            for si, e in enumerate(EORDER_G):
                k = p * NPE + si
                if k < 2:
                    continue
                pk, sk = divmod(k - 2, NPE)
                WAIT(g, "pe", f"guexp_{pk}_{sk}")
                WAIT(g, "gw", f"gwcast{NPE - 1}")
                i0 = g.dma_start(out=v_gwuw[:, k % 2, :, :, :, :],
                                 in_=d_gwuwb[e, :, :])
                INC(i0, "gw", f"gws_{p}_{e}")
            for hc in range(HC):
                k = p * HC + hc
                if k == 0:
                    WAIT(g, "pe", f"wo_{NCH - 1}_{HC - 1}")
                    WAIT(g, "gw", "dwcast")
                if k >= 2:
                    pk, hk = divmod(k - 2, HC)
                    WAIT(g, "pe", f"dn_{pk}_{hk}")
                i0 = g.dma_start(out=v_dws[:, k % 2, :, :],
                                 in_=d_dwb[:, hc * KC * 128:(hc + 1) * KC * 128])
                INC(i0, "gw", f"dws_{p}_{hc}")

    def emit_sync(sy):
        S["cur"] = None
        for c in range(min(3, NCH)):
            i0 = sy.dma_start(
                out=xr(c)[:, :, :],
                in_=d_x[:, :].rearrange("p (h t) -> p h t", h=HC)[:, :, c * TC:(c + 1) * TC])
            INC(i0, "sx", f"x_{c}")
        for i in range(NCH + 2):
            if 1 <= i <= NCH:
                c = i - 1
                WAIT(sy, "ve", f"epi_{c}")
                i0 = sy.dma_start(
                    out=d_x1[:, :].rearrange("p (h t) -> p h t", h=HC)[:, :, c * TC:(c + 1) * TC],
                    in_=xr(c)[:, :, :])
                INC(i0, "sp", f"x1sp_{c}")
            if 2 <= i <= NCH + 1:
                c = i - 2
                for hc in range(HC):
                    WAIT(sy, "ve", f"ln2_{c}_{hc}")
                    i0 = sy.dma_start(
                        out=d_ln2[:, :].rearrange("p (h t) -> p h t", h=HC)[:, hc, c * TC:(c + 1) * TC],
                        in_=v_ln2bf[:, (c * HC + hc) % 4, :])
                    INC(i0, "sp", f"ln2sp_{c}_{hc}")
            c = i + 1
            if 3 <= c < NCH:
                WAIT(sy, "pe", f"raw_{c - 3}")
                WAIT(sy, "ve", f"ln2_{c - 3}_{HC - 1}")
                i0 = sy.dma_start(
                    out=xr(c)[:, :, :],
                    in_=d_x[:, :].rearrange("p (h t) -> p h t", h=HC)[:, :, c * TC:(c + 1) * TC])
                INC(i0, "sx", f"x_{c}")
        def rd_pass(p):
            WAIT(sy, "sp", f"x1sp_{p}")
            if p == 0:
                WAIT(sy, "pe", f"raw_{NCH - 1}")
                WAIT(sy, "ve", f"ln2_{NCH - 1}_{HC - 1}")
                for pp in range(NCH):
                    WAIT(sy, "sp", f"x1sp_{pp}")
            if p >= 2:
                WAIT(sy, "ve", f"evacend_{p - 2}")
            i0 = sy.dma_start(
                out=v_x1t[:, p % 2, :, :],
                in_=d_x1[:, :].rearrange("p (h t) -> p h t", h=HC)[:, :, p * TC:(p + 1) * TC])
            INC(i0, "sx", f"x1t_{p}")
            k_last = p * HC + HC - 1
            for kk in range(max(0, k_last - 3), k_last + 1):
                WAIT(sy, "sp", f"ln2sp_{kk // HC}_{kk % HC}")
            if p >= 2:
                WAIT(sy, "pe", f"guend_{p - 2}")
            i0 = sy.dma_start(
                out=v_ln2t[:, p % 2, :, :],
                in_=d_ln2[:, :].rearrange("p (h t) -> p h t", h=HC)[:, :, p * TC:(p + 1) * TC])
            INC(i0, "sx", f"ln2t_{p}")

        def rd_srows(p):
            for e in range(NE):
                k = p * NE + e
                if k == 0:
                    WAIT(sy, "sp", "semrows_d")
                if k >= 2:
                    pk, ek = divmod(k - 2, NE)
                    WAIT(sy, "pe", f"bselmm_{pk}_{ek}")
                i0 = sy.dma_start(out=srow[0:1, (k % 2) * TC:(k % 2 + 1) * TC],
                                  in_=d_sem[e:e + 1, p * TC:(p + 1) * TC])
                INC(i0, "sx", f"srow_{p}_{e}")

        rd_pass(0)
        rd_pass(1)
        WAIT(sy, "ve", "ccin")
        i0 = sy.dma_start(out=d_ccin[:, :], in_=ccin_sb[:, :])
        INC(i0, "gw", "ccin_d")
        WAIT(sy, "cc", "ccdone")
        i0 = sy.dma_start(out=ccout_sb[:, :], in_=d_ccout[:, :])
        INC(i0, "gw", "ccout")
        WAIT(sy, "ve", "semrows")
        i0 = sy.dma_start(out=d_sem[:, :], in_=s_em[:, :])
        INC(i0, "sp", "semrows_d")
        rd_srows(0)
        for p in range(NCH):
            if p + 1 < NCH:
                rd_srows(p + 1)
            for hc in range(HC):
                WAIT(sy, "ve", f"evac_{p}_{hc}")
                i0 = sy.dma_start(
                    out=d_out[:, :].rearrange("p (h t) -> p h t", h=HC)[:, hc, p * TC:(p + 1) * TC],
                    in_=v_x1t[:, p % 2, hc, :])
                INC(i0, "sp", f"out_{p}_{hc}")
            if p + 2 < NCH:
                rd_pass(p + 2)
        WAIT(sy, "sp", f"out_{NCH - 1}_{HC - 1}")

    def emit_scalar(s):
        S["cur"] = s if not isinstance(s, _EngP) else None
        WAIT(s, "gw", "smalls")
        s.activation(out=dtmp[:, :], in_=dtmp[:, :], func=AF.Exp)
        s.drain()
        i0 = s.activation(out=dcol[:, :], in_=dtmp[:, :], func=AF.Exp,
                          scale=-1.0)
        s.drain()
        INC(i0, "se", "dcol")

        def casts(c, cviews, tag):
            if tag == "1":
                WAIT(s, "sx", f"x_{c}")
            for hc in range(HC):
                k = c * HC + hc
                if tag == "2":
                    WAIT(s, "ve", f"epioc_{c}_{hc}")
                if k >= 2:
                    pk, hk = divmod(k - 2, HC)
                    WAIT(s, "pe", f"st{tag}mm_{pk}_{hk}")
                s.copy(out=cviews[:, hc % 2, 0, :], in_=xr(c)[:, hc, :])
                i0 = s.activation(out=cviews[:, hc % 2, 1, :],
                                  in_=xr(c)[:, hc, :], func=AF.Square)
                if hc % 2 == 1 or hc == HC - 1:
                    INC(i0, "se", f"cast{tag}_{c}_{hc}")
                    if hc % 2 == 1:
                        ALIAS("se", f"cast{tag}_{c}_{hc - 1}", f"cast{tag}_{c}_{hc}")

        for i in range(NCH + 2):
            if i < NCH:
                casts(i, v_cast1, "1")
            if 2 <= i <= NCH + 1:
                c = i - 2
                WAIT(s, "ve", f"vrow2_{c}")
                i0 = s.activation(out=v_rmsq[0:1, :], in_=v_re2[0:1, :], func=AF.Sqrt)
                INC(i0, "se", f"sqrt2_{c}")
            if i < NCH:
                WAIT(s, "ve", f"vrow1_{i}")
                i0 = s.activation(out=v_rmsq[0:1, :], in_=v_re2[0:1, :], func=AF.Sqrt)
                INC(i0, "se", f"sqrt1_{i}")
            if 1 <= i <= NCH:
                c = i - 1
                for oc in range(HC):
                    WAIT(s, "pe", f"wv_{c}_{oc}")
                    i0 = s.copy(out=v_vc[:, oc, :], in_=pb[4 + oc % 2][:, 0:TC])
                    if oc % 2 == 1 or oc == HC - 1:
                        INC(i0, "se", f"evv_{c}_{oc}")
                        if oc % 2 == 1:
                            ALIAS("se", f"evv_{c}_{oc - 1}", f"evv_{c}_{oc}")
                    else:
                        pass
            if i < NCH:
                c = i
                if c >= 1:
                    WAIT(s, "ve", f"mix_{c - 1}")
                WAIT(s, "pe", f"bc1a_{c}")
                s.copy(out=v_ab1[:, 0:TC], in_=pb[2][:, 0:TC])
                WAIT(s, "pe", f"bc1_{c}")
                i0 = s.copy(out=v_ab1[:, TC:2 * TC], in_=pb[3][:, 0:TC])
                INC(i0, "se", f"ab1_{c}")
            if 2 <= i <= NCH + 1:
                c = i - 2
                if c >= 1:
                    WAIT(s, "ve", f"ln2_{c - 1}_{HC - 1}")
                WAIT(s, "pe", f"bc2a_{c}")
                s.copy(out=v_ab2[:, 0:TC], in_=pb[2][:, 0:TC])
                WAIT(s, "pe", f"bc2_{c}")
                i0 = s.copy(out=v_ab2[:, TC:2 * TC], in_=pb[3][:, 0:TC])
                INC(i0, "se", f"ab2_{c}")
            if 2 <= i <= NCH + 1:
                c = i - 2
                WAIT(s, "pe", f"mcolT_{c}")
                s.copy(out=mcols[:, c * SPT:(c + 1) * SPT],
                       in_=pb[4][:, 0:SPT])
                i0 = s.copy(out=iscols[:, c * SPT:(c + 1) * SPT],
                            in_=pb[5][:, 0:SPT])
                INC(i0, "se", f"mcev_{c}")
            if 1 <= i <= NCH:
                c = i - 1
                for oc in range(HC):
                    WAIT(s, "pe", f"wr_{c}_{oc}")
                    i0 = s.activation(out=v_rc[:, oc, :],
                                      in_=pb[6 + oc % 2][:, 0:TC],
                                      func=AF.Sigmoid)
                    if oc % 2 == 1 or oc == HC - 1:
                        INC(i0, "se", f"sig_{c}_{oc}")
                        if oc % 2 == 1:
                            ALIAS("se", f"sig_{c}_{oc - 1}", f"sig_{c}_{oc}")
                    else:
                        pass
            if 1 <= i <= NCH:
                casts(i - 1, v_cast2, "2")
        WAIT(s, "pe", "gsumbc_mm")
        i0 = s.copy(out=gsumbc[:, :], in_=pb[6][:, 0:NE])
        INC(i0, "se", "gsumbc")
        WAIT(s, "pe", "lbbc_mm")
        i0 = s.copy(out=lbbc[:, :], in_=pb[7][:, 0:NE])
        INC(i0, "se", "lbbc")
        WAIT(s, "ve", "logits")
        i0 = s.activation(out=g_sc[:, :], in_=g_lg[:, :], func=AF.Sigmoid)
        INC(i0, "se", "scores")
        EORDER_S = [NE, NE + 1] + list(range(NE))
        sq_keys = []
        for p in range(NCH):
            for si, e in enumerate(EORDER_S):
                for ic in range(ICH):
                    kc = e * ICH + ic
                    q = len(sq_keys)
                    sq_keys.append((p, kc))
                    if q == 0:
                        k_last = NCH * HC - 1
                        for kk in range(max(0, k_last - 3), k_last + 1):
                            WAIT(s, "sp", f"ln2sp_{kk // HC}_{kk % HC}")
                    if q >= 2:
                        pk, kk = sq_keys[q - 2]
                        WAIT(s, "ve", f"hid_{pk}_{kk}")
                    WAIT(s, "pe", f"g_{p}_{e}_{ic}")
                    i0 = s.activation(out=v_gsb[:, (q % 2) * TC:(q % 2 + 1) * TC],
                                      in_=pb[q % 2][:, 0:TC], func=AF.Silu)
                    INC(i0, "se", f"silu_{p}_{e}_{ic}")

    def emit_pe(t):
        S["cur"] = t if not isinstance(t, _EngP) else None
        WAIT(t, "ve", "ones")
        def stats_mm(c, tag, cviews, blo, bhi):
            for hc in range(HC):
                WAIT(t, "se", f"cast{tag}_{c}_{hc}")
                if hc == 0:
                    bank_w(t, blo, ("ve", f"stread{tag}_{c}"))
                    bank_w(t, bhi, ("ve", f"stread{tag}_{c}"))
                t.matmul(pb[blo][0:1, 0:TC], ones_bfc[:, :],
                         cviews[:, hc % 2, 0, :],
                         start=(hc == 0), stop=(hc == HC - 1))
                i0 = t.matmul(pb[bhi][0:1, 0:TC], ones_bfc[:, :],
                              cviews[:, hc % 2, 1, :],
                              start=(hc == 0), stop=(hc == HC - 1))
                INC(i0, "pe", f"st{tag}mm_{c}_{hc}")
            ALIAS("pe", f"st{tag}_{c}", f"st{tag}mm_{c}_{HC - 1}")

        for i in range(NCH + 2):
            if i < NCH:
                stats_mm(i, "1", v_cast1, 0, 1)
            if 1 <= i <= NCH:
                c = i - 1
                WAIT(t, "ve", f"mix_{c}")
                if c == 0:
                    WAIT(t, "gw", "wo")
                for oc in range(HC):
                    bank_w(t, 4 + oc % 2, ("se", f"evv_{c}_{oc}"))
                    for hc in range(HC):
                        i0 = t.matmul(pb[4 + oc % 2][:, 0:TC],
                                      wvw(v_wv, oc, hc), v_vin[:, c % 2, hc, :],
                                      start=(hc == 0), stop=(hc == HC - 1))
                    INC(i0, "pe", f"wv_{c}_{oc}")
            if i < NCH:
                c = i
                WAIT(t, "ve", f"rows1_{c}")
                bank_w(t, 2, ("se", f"ab1_{c}"))
                i0 = t.matmul(pb[2][:, 0:TC], ones_fr[:, :], v_ris[0:1, :],
                              start=True, stop=True)
                INC(i0, "pe", f"bc1a_{c}")
                bank_w(t, 3, ("se", f"ab1_{c}"))
                i0 = t.matmul(pb[3][:, 0:TC], ones_fr[:, :], v_rb[0:1, :],
                              start=True, stop=True)
                INC(i0, "pe", f"bc1_{c}")
            if 2 <= i <= NCH + 1:
                c = i - 2
                WAIT(t, "ve", f"rows2_{c}")
                bank_w(t, 2, ("se", f"ab2_{c}"))
                i0 = t.matmul(pb[2][:, 0:TC], ones_fr[:, :], v_r2is[0:1, :],
                              start=True, stop=True)
                INC(i0, "pe", f"bc2a_{c}")
                bank_w(t, 3, ("se", f"ab2_{c}"))
                i0 = t.matmul(pb[3][:, 0:TC], ones_fr[:, :], v_r2b[0:1, :],
                              start=True, stop=True)
                INC(i0, "pe", f"bc2_{c}")
                bank_w(t, 4, ("se", f"mcev_{c}"))
                for sp in range(SPT):
                    i0 = t.matmul(pb[4][:, sp:sp + 1],
                                  v_r2mean[0:1, sp * 128:(sp + 1) * 128],
                                  ones_fr[0:1, 0:1], start=True, stop=True)
                INC(i0, "pe", f"mcolT_m_{c}")
                bank_w(t, 5, ("se", f"mcev_{c}"))
                for sp in range(SPT):
                    i0 = t.matmul(pb[5][:, sp:sp + 1],
                                  v_r2is[0:1, sp * 128:(sp + 1) * 128],
                                  ones_fr[0:1, 0:1], start=True, stop=True)
                INC(i0, "pe", f"mcolT_{c}")
                WAIT(t, "se", f"ab2_{c}")
                bank_w(t, 2, ("ve", f"rawev_{c}"))
                for hc in range(HC):
                    i0 = t.matmul(pb[2][0:8, 0:TC],
                                  gatew[:, hc * NE:(hc + 1) * NE],
                                  xr(c)[:, hc, :],
                                  start=(hc == 0), stop=(hc == HC - 1))
                INC(i0, "pe", f"raw_{c}")
            if 1 <= i <= NCH:
                c = i - 1
                for oc in range(HC):
                    bank_w(t, 6 + oc % 2, ("se", f"sig_{c}_{oc}"))
                    for hc in range(HC):
                        i0 = t.matmul(pb[6 + oc % 2][:, 0:TC],
                                      wvw(v_wr, oc, hc), v_rin[:, c % 2, hc, :],
                                      start=(hc == 0), stop=(hc == HC - 1))
                    INC(i0, "pe", f"wr_{c}_{oc}")
            if 1 <= i <= NCH:
                c = i - 1
                WAIT(t, "se", f"evv_{c}_{HC - 1}")
                for oc in range(HC):
                    bank_w(t, 6 + oc % 2, ("ve", f"epioc_{c}_{oc}"))
                    for vc in range(HC):
                        i0 = t.matmul(pb[6 + oc % 2][:, 0:TC],
                                      wvw(v_wo, oc, vc), v_vc[:, vc, :],
                                      start=(vc == 0), stop=(vc == HC - 1))
                    INC(i0, "pe", f"wo_{c}_{oc}")
            if 1 <= i <= NCH:
                stats_mm(i - 1, "2", v_cast2, 4, 5)
        WAIT(t, "ge", "ident")
        bank_w(t, 6, ("ve", "gsumr"))
        i0 = t.matmul(pb[6][0:1, 0:HC * NE], ones_fc[:, :], gatew[:, :],
                      start=True, stop=True)
        INC(i0, "pe", "gsum_mm")
        WAIT(t, "ve", "gsumr")
        bank_w(t, 6, ("se", "gsumbc"))
        i0 = t.matmul(pb[6][:, 0:NE], ones_fr[:, :], gsumr[:, :],
                      start=True, stop=True)
        INC(i0, "pe", "gsumbc_mm")
        bank_w(t, 7, ("se", "lbbc"))
        WAIT(t, "gw", "smalls")
        i0 = t.matmul(pb[7][:, 0:NE], ones_fr[:, :], lbrow[:, :],
                      start=True, stop=True)
        INC(i0, "pe", "lbbc_mm")
        for j in range(NSTRIP):
            bi = j % 2
            bank_w(t, bi, ("ve", f"t1_{j}"))
            i0 = t.matmul(pb[bi][:, (j // 2 % 2) * 8:(j // 2 % 2) * 8 + 8],
                          raw_em[0:8, j * 128:(j + 1) * 128], i8f[:, :],
                          start=True, stop=True)
            INC(i0, "pe", f"rawT_{j}")
        def gate_tail_a():
            WAIT(t, "ve", "wc")
            bank_w(t, 6, ("ve", "ccin"))
            i0 = t.matmul(pb[6][0:1, 0:NSTRIP * 8], ones_fc[:, :], g_wc[:, :],
                          start=True, stop=True)
            INC(i0, "pe", "part_wc")
            bank_w(t, 7, ("ve", "ccin"))
            i0 = t.matmul(pb[7][0:1, 0:NSTRIP * 8], ones_fc[:, :], g_sel[:, :],
                          start=True, stop=True)
            INC(i0, "pe", "part_sel")

        def gate_tail_b():
            WAIT(t, "ve", "wmrow")
            bank_w(t, 6, ("ve", "wmcol"))
            i0 = t.matmul(pb[6][0:8, 0:1], wmrow[:, :], ones_fr[0:1, 0:1],
                          start=True, stop=True)
            INC(i0, "pe", "wmcol_mm")
            for jg in range(NJG):
                bank_w(t, 7, ("ve", f"sem_{jg}"))
                n_in_g = min(4, NSTRIP - jg * 4)
                for jj in range(n_in_g):
                    j = jg * 4 + jj
                    i0 = t.matmul(pb[7][0:8, jj * 128:(jj + 1) * 128],
                                  g_sel[:, :].rearrange("p (j e) -> p j e", e=8)[:, j, :],
                                  i128f[:, :], start=True, stop=True)
                INC(i0, "pe", f"selT_{jg}")

        EORDER = [NE, NE + 1] + list(range(NE))

        def bselmms(p, es):
            for e in es:
                WAIT(t, "sx", f"srow_{p}_{e}")
                if p == 0:
                    bank_w(t, 4, ("ve", f"hsc_{e}"))
                else:
                    bank_w(t, 4, ("ve", f"hid_{p}_{e * ICH + ICH - 1}"))
                k = p * NE + e
                i0 = t.matmul(pb[4][:, 0:TC], ones_fr[:, :],
                              srow[0:1, (k % 2) * TC:(k % 2 + 1) * TC],
                              start=True, stop=True)
                INC(i0, "pe", f"bselmm_{p}_{e}")

        for p in range(NCH):
            WAIT(t, "sx", f"ln2t_{p}")
            for si, e in enumerate(EORDER):
                if p == 0 and si == 2:
                    gate_tail_a()
                if p > 0 and e < NE:
                    bselmms(p, [e])
                WAIT(t, "gw", f"gws_{p}_{e}")
                slot = (p * NPE + si) % 2
                for ic in range(ICH):
                    kc = e * ICH + ic
                    sq = si * ICH + ic
                    bank_w(t, sq % 2, ("se", f"silu_{p}_{e}_{ic}"))
                    for hc in range(HC):
                        i0 = t.matmul(pb[sq % 2][:, 0:TC],
                                      v_gwuw[:, slot, 0, hc, ic, :],
                                      v_ln2t[:, p % 2, hc, :],
                                      start=(hc == 0), stop=(hc == HC - 1))
                    INC(i0, "pe", f"g_{p}_{e}_{ic}")
                    bank_w(t, 2 + sq % 2, ("ve", f"hid_{p}_{kc}"))
                    for hc in range(HC):
                        i0 = t.matmul(pb[2 + sq % 2][:, 0:TC],
                                      v_gwuw[:, slot, 1, hc, ic, :],
                                      v_ln2t[:, p % 2, hc, :],
                                      start=(hc == 0), stop=(hc == HC - 1))
                    INC(i0, "pe", f"u_{p}_{e}_{ic}")
                ALIAS("pe", f"guexp_{p}_{si}", f"u_{p}_{e}_{ICH - 1}")
            ALIAS("pe", f"guend_{p}", f"u_{p}_{EORDER[-1]}_{ICH - 1}")
            if p == 0:
                gate_tail_b()
                bselmms(0, range(NE))
            for hc in range(HC):
                WAIT(t, "gw", f"dws_{p}_{hc}")
                bank_w(t, 5 + hc % 2, ("ve", f"evac_{p}_{hc}"))
                slot = (p * HC + hc) % 2
                for kc in range(KC):
                    if hc == 0 and kc == 0:
                        WAIT(t, "ve", f"hidend_{p}")
                    i0 = t.matmul(pb[5 + hc % 2][:, 0:TC],
                                  v_dws[:, slot, kc, :], v_hid[:, kc, :],
                                  start=(kc == 0), stop=(kc == KC - 1))
                INC(i0, "pe", f"dn_{p}_{hc}")

    def emit_vector(v):
        S["cur"] = v if not isinstance(v, _EngP) else None
        v.memset(carry[:, :], 0.0)
        v.memset(ones_bfc[:, :], 1.0)
        v.memset(ones_fc[:, :], 1.0)
        v.memset(ones_fr[:, :], 1.0)
        i0 = v.nop()
        INC(i0, "ve", "ones")
        rows_prev = [None]

        def rows(c, tag):
            if tag == "1":
                rm, ri, rb2 = v_rmean, v_ris, v_rb
                prev_pe = f"bc1_{c - 1}" if c >= 1 else None
                blo, bhi = 0, 1
            else:
                rm, ri, rb2 = v_r2mean, v_r2is, v_r2b
                prev_pe = f"mcolT_{c - 1}" if c >= 1 else None
                blo, bhi = 4, 5
            if prev_pe is not None:
                WAIT(v, "pe", prev_pe)
            if rows_prev[0] is not None:
                WAIT(v, "se", rows_prev[0])
            rows_prev[0] = f"sqrt{tag}_{c}"
            WAIT(v, "pe", f"st{tag}_{c}")
            v.tensor_scalar(out=rm[0:1, :], in0=pb[blo][0:1, 0:TC],
                            scalar1=invH, scalar2=None, op0=ALU.mult)
            v.tensor_scalar(out=v_re2[0:1, :], in0=pb[bhi][0:1, 0:TC],
                            scalar1=invH, scalar2=EPS,
                            op0=ALU.mult, op1=ALU.add)
            i0 = v.nop()
            INC(i0, "ve", f"stread{tag}_{c}")
            v.tensor_tensor(out=v_rmsq[0:1, :], in0=rm[0:1, :],
                            in1=rm[0:1, :], op=ALU.mult)
            v.drain()
            v.tensor_tensor(out=v_re2[0:1, :], in0=v_re2[0:1, :],
                            in1=v_rmsq[0:1, :], op=ALU.subtract)
            i0 = v.nop()
            INC(i0, "ve", f"vrow{tag}_{c}")
            WAIT(v, "se", f"sqrt{tag}_{c}")
            v.reciprocal(out=ri[0:1, :], in_=v_rmsq[0:1, :])
            v.drain()
            i0 = v.tensor_tensor(out=rb2[0:1, :], in0=rm[0:1, :],
                                 in1=ri[0:1, :], op=ALU.mult)
            INC(i0, "ve", f"rows{tag}_{c}")

        for i in range(NCH + 2):
            if 2 <= i <= NCH + 1:
                rows(i - 2, "2")
            if i < NCH:
                rows(i, "1")
            if i < NCH:
                c = i
                WAIT(v, "se", f"ab1_{c}")
                WAIT(v, "se", "dcol")
                if c >= 2:
                    WAIT(v, "pe", f"wr_{c - 2}_{HC - 1}")
                for hc in range(HC):
                    v.tensor_copy(out=v_zb[:, 0:1], in_=carry[:, hc:hc + 1])
                    v.tensor_tensor(out=v_lnb, in0=xr(c)[:, hc, :],
                                    in1=v_ab1[:, 0:TC], op=ALU.mult)
                    v.tensor_tensor(out=v_lnb, in0=v_lnb,
                                    in1=v_ab1[:, TC:2 * TC], op=ALU.subtract)
                    v.tensor_tensor_scan(
                        out=v_zb[:, 1:TC + 1],
                        data0=dcol[:, hc:hc + 1].to_broadcast([128, TC]),
                        data1=v_lnb, initial=v_zb[:, 0:1],
                        op0=ALU.mult, op1=ALU.add)
                    v.tensor_tensor(out=v_diff, in0=v_lnb, in1=v_zb[:, 0:TC],
                                    op=ALU.subtract)
                    v.scalar_tensor_tensor(
                        out=v_vin[:, c % 2, hc, :], in0=v_diff,
                        scalar=tmvc[:, hc:hc + 1], in1=v_zb[:, 0:TC],
                        op0=ALU.mult, op1=ALU.add)
                    v.scalar_tensor_tensor(
                        out=v_rin[:, c % 2, hc, :], in0=v_diff,
                        scalar=tmrc[:, hc:hc + 1], in1=v_zb[:, 0:TC],
                        op0=ALU.mult, op1=ALU.add)
                    v.tensor_copy(out=carry[:, hc:hc + 1],
                                  in_=v_zb[:, TC:TC + 1])
                    if hc == HC - 1:
                        i0 = v.nop()
                        INC(i0, "ve", f"mix_{c}")
            if 1 <= i <= NCH:
                c = i - 1
                for oc in range(HC):
                    WAIT(v, "pe", f"wo_{c}_{oc}")
                    WAIT(v, "se", f"sig_{c}_{oc}")
                    v.tensor_tensor(out=v_diff, in0=pb[6 + oc % 2][:, 0:TC],
                                    in1=v_rc[:, oc, :], op=ALU.mult)
                    v.tensor_tensor(out=xr(c)[:, oc, :], in0=v_diff,
                                    in1=xr(c)[:, oc, :], op=ALU.add)
                    i0 = v.nop()
                    INC(i0, "ve", f"epioc_{c}_{oc}")
                i0 = v.nop()
                INC(i0, "ve", f"epi_{c}")
            if 2 <= i <= NCH + 1:
                c = i - 2
                WAIT(v, "se", f"ab2_{c}")
                for hc in range(HC):
                    k = c * HC + hc
                    if k >= 4:
                        pk, hk = divmod(k - 4, HC)
                        WAIT(v, "sp", f"ln2sp_{pk}_{hk}")
                    v.tensor_tensor(out=v_lnb, in0=xr(c)[:, hc, :],
                                    in1=v_ab2[:, 0:TC], op=ALU.mult)
                    v.tensor_tensor(out=v_ln2bf[:, k % 4, :],
                                    in0=v_lnb, in1=v_ab2[:, TC:2 * TC],
                                    op=ALU.subtract)
                    i0 = v.nop()
                    INC(i0, "ve", f"ln2_{c}_{hc}")
                WAIT(v, "pe", f"raw_{c}")
                v.tensor_copy(out=raw_em[:, c * TC:(c + 1) * TC],
                              in_=pb[2][0:8, 0:TC])
                i0 = v.nop()
                INC(i0, "ve", f"rawev_{c}")
        WAIT(v, "pe", "gsum_mm")
        i0 = v.tensor_reduce(
            out=gsumr[0:1, :].unsqueeze(2),
            in_=pb[6][0:1, 0:HC * NE].rearrange("p (h e) -> p e h", e=NE),
            axis=AXX, op=ALU.add)
        i0 = v.nop()
        INC(i0, "ve", "gsumr")
        WAIT(v, "se", "gsumbc")
        for j in range(NSTRIP):
            WAIT(v, "pe", f"rawT_{j}")
            reg = (j // 2 % 2) * 8
            v.scalar_tensor_tensor(
                out=g_t1[:, j * 8:(j + 1) * 8], in0=gsumbc[:, :],
                scalar=mcols[:, j:j + 1], in1=pb[j % 2][:, reg:reg + 8],
                op0=ALU.mult, op1=ALU.subtract)
            i0 = v.nop()
            INC(i0, "ve", f"t1_{j}")
        for j in range(NSTRIP):
            v.tensor_scalar(out=g_t1[:, j * 8:(j + 1) * 8],
                            in0=g_t1[:, j * 8:(j + 1) * 8],
                            scalar1=iscols[:, j:j + 1], scalar2=-1.0,
                            op0=ALU.mult, op1=ALU.mult)
        v.drain()
        WAIT(v, "se", "lbbc")
        for j in range(NSTRIP):
            i0 = v.tensor_tensor(out=g_lg[:, j * 8:(j + 1) * 8],
                                 in0=g_t1[:, j * 8:(j + 1) * 8],
                                 in1=lbbc[:, :], op=ALU.add)
        i0 = v.nop()
        INC(i0, "ve", "logits")
        WAIT(v, "se", "scores")
        v.reduce_max(out=g_m1[:, :].unsqueeze(2),
                     in_=g_sc[:, :].rearrange("p (j e) -> p j e", e=8),
                     axis=AXX)
        v.drain()
        v.tensor_tensor(out=g_lg[:, :].rearrange("p (j e) -> p j e", e=8),
                        in0=g_sc[:, :].rearrange("p (j e) -> p j e", e=8),
                        in1=g_m1[:, :].unsqueeze(2).to_broadcast([128, NSTRIP, 8]),
                        op=ALU.is_ge)
        v.drain()
        v.scalar_tensor_tensor(out=g_lg[:, :], in0=g_lg[:, :], scalar=1e30,
                               in1=g_sc[:, :], op0=ALU.mult, op1=ALU.subtract)
        v.drain()
        v.tensor_reduce(out=g_m2[:, :].unsqueeze(2),
                        in_=g_lg[:, :].rearrange("p (j e) -> p j e", e=8),
                        axis=AXX, op=ALU.min, negate=True)
        v.drain()
        v.tensor_tensor(out=g_sel[:, :].rearrange("p (j e) -> p j e", e=8),
                        in0=g_sc[:, :].rearrange("p (j e) -> p j e", e=8),
                        in1=g_m2[:, :].unsqueeze(2).to_broadcast([128, NSTRIP, 8]),
                        op=ALU.is_ge)
        v.tensor_tensor(out=g_s12[:, :], in0=g_m1[:, :], in1=g_m2[:, :],
                        op=ALU.add)
        v.drain()
        v.tensor_scalar(out=g_s12[:, :], in0=g_s12[:, :], scalar1=1e-8,
                        scalar2=None, op0=ALU.add)
        v.drain()
        v.reciprocal(out=g_i12[:, :], in_=g_s12[:, :])
        v.tensor_tensor(out=g_wc[:, :], in0=g_sc[:, :], in1=g_sel[:, :],
                        op=ALU.mult)
        v.drain()
        i0 = v.tensor_tensor(
            out=g_wc[:, :].rearrange("p (j e) -> p j e", e=8),
            in0=g_wc[:, :].rearrange("p (j e) -> p j e", e=8),
            in1=g_i12[:, :].unsqueeze(2).to_broadcast([128, NSTRIP, 8]),
            op=ALU.mult)
        i0 = v.nop()
        INC(i0, "ve", "wc")

        EORDER_V = [NE, NE + 1] + list(range(NE))
        qv = [0]

        def hid_ops(p, si_list):
            for si in si_list:
                e = EORDER_V[si]
                for ic in range(ICH):
                    kc = e * ICH + ic
                    q = qv[0]
                    qv[0] += 1
                    WAIT(v, "se", f"silu_{p}_{e}_{ic}")
                    WAIT(v, "pe", f"u_{p}_{e}_{ic}")
                    gs = v_gsb[:, (q % 2) * TC:(q % 2 + 1) * TC]
                    if e < NE and p > 0:
                        WAIT(v, "pe", f"bselmm_{p}_{e}")
                        v.tensor_tensor(out=v_h1b, in0=gs,
                                        in1=pb[2 + q % 2][:, 0:TC],
                                        op=ALU.mult)
                        v.tensor_tensor(out=v_hid[:, kc, :], in0=v_h1b,
                                        in1=pb[4][:, 0:TC], op=ALU.mult)
                    else:
                        v.tensor_tensor(out=v_hid[:, kc, :], in0=gs,
                                        in1=pb[2 + q % 2][:, 0:TC],
                                        op=ALU.mult)
                    i0 = v.nop()
                    INC(i0, "ve", f"hid_{p}_{kc}")

        def hid_scale0():
            for e in range(NE):
                WAIT(v, "pe", f"bselmm_0_{e}")
                for ic in range(ICH):
                    kc = e * ICH + ic
                    v.tensor_tensor(out=v_hid[:, kc, :], in0=v_hid[:, kc, :],
                                    in1=pb[4][:, 0:TC], op=ALU.mult)
                i0 = v.nop()
                INC(i0, "ve", f"hsc_{e}")

        hid_ops(0, [0, 1])
        WAIT(v, "pe", "part_wc")
        v.tensor_reduce(
            out=ccin_sb[0:1, 0:8].unsqueeze(2),
            in_=pb[6][0:1, 0:NSTRIP * 8].rearrange("p (j e) -> p e j", e=8),
            axis=AXX, op=ALU.add)
        WAIT(v, "pe", "part_sel")
        v.tensor_reduce(
            out=ccin_sb[0:1, 8:16].unsqueeze(2),
            in_=pb[7][0:1, 0:NSTRIP * 8].rearrange("p (j e) -> p e j", e=8),
            axis=AXX, op=ALU.add)
        i0 = v.nop()
        INC(i0, "ve", "ccin")
        WAIT(v, "gw", "ccout")
        v.tensor_scalar_max(out=ccout_sb[0:1, 8:16], in0=ccout_sb[0:1, 8:16],
                            scalar1=1.0)
        v.drain()
        v.reciprocal(out=ccout_sb[0:1, 8:16], in_=ccout_sb[0:1, 8:16])
        v.drain()
        v.tensor_tensor(out=wmrow[:, :], in0=ccout_sb[0:1, 0:8],
                        in1=ccout_sb[0:1, 8:16], op=ALU.mult)
        i0 = v.nop()
        INC(i0, "ve", "wmrow")
        for p in range(NCH):
            if p >= 1:
                WAIT(v, "pe", f"dn_{p - 1}_{HC - 1}")
                hid_ops(p, range(2))
            hid_ops(p, range(2, NPE))
            if p == 0:
                WAIT(v, "pe", "wmcol_mm")
                i0 = v.tensor_copy(out=wmcol[:, :], in_=pb[6][0:8, 0:1])
                v.drain()
                INC(i0, "ve", "wmcol")
                for jg in range(NJG):
                    WAIT(v, "pe", f"selT_{jg}")
                    n_in_g = min(4, NSTRIP - jg * 4)
                    v.tensor_scalar(
                        out=s_em[:, jg * 512:jg * 512 + n_in_g * 128],
                        in0=pb[7][0:8, 0:n_in_g * 128],
                        scalar1=wmcol[:, :], scalar2=None, op0=ALU.mult)
                    i0 = v.nop()
                    INC(i0, "ve", f"sem_{jg}")
                i0 = v.nop()
                INC(i0, "ve", "semrows")
                hid_scale0()
            i0 = v.nop()
            INC(i0, "ve", f"hidend_{p}")
            WAIT(v, "sx", f"x1t_{p}")
            for hc in range(HC):
                WAIT(v, "pe", f"dn_{p}_{hc}")
                v.tensor_tensor(out=v_x1t[:, p % 2, hc, :],
                                in0=pb[5 + hc % 2][:, 0:TC],
                                in1=v_x1t[:, p % 2, hc, :], op=ALU.add)
                i0 = v.nop()
                INC(i0, "ve", f"evac_{p}_{hc}")
            i0 = v.nop()
            INC(i0, "ve", f"evacend_{p}")

    def run_all(engs):
        emit_gpsimd(engs["gpsimd"])
        emit_sync(engs["sync"])
        emit_scalar(engs["scalar"])
        emit_pe(engs["tensor"])
        emit_vector(engs["vector"])

    dummies = {k: _DUMMY_ENG for k in
               ["gpsimd", "sync", "scalar", "tensor", "vector"]}
    reset(False)
    run_all(dummies)
    led = dict(S["led"])
    nc._debug_led = led

    with nc.Block() as block:

        def mk(name):
            def f(eng):
                reset(True)
                S["led"] = led
                engs = dict(dummies)
                if cfg.get("paranoid") and name in ("vector", "scalar", "gpsimd"):
                    eng = _ParanoidEng(eng)
                engs[name] = eng
                run_all(engs)
            f.__name__ = f"_{name}"
            return f

        block.gpsimd(mk("gpsimd"))
        block.sync(mk("sync"))
        block.scalar(mk("scalar"))
        block.tensor(mk("tensor"))
        block.vector(mk("vector"))

    return nc



def _pm_proj(W, HC):
    O = W.shape[0]
    OC = O // 128
    WT = np.ascontiguousarray(W.T)
    return np.ascontiguousarray(
        WT.reshape(HC, 128, OC, 128).transpose(1, 2, 0, 3).reshape(128, -1)
    ).astype(np.float32)


def _pm_col(vec):
    n = vec.shape[0]
    return np.ascontiguousarray(vec.reshape(n // 128, 128).T).astype(np.float32)


def _pm_gu(M, HC, ICH):
    MT = np.ascontiguousarray(M.T)
    return MT.reshape(HC, 128, ICH, 128).transpose(1, 0, 2, 3).reshape(128, -1)


def host_inputs(cfg, inputs):
    HC, T, NE, ICH = cfg["HC"], cfg["T"], cfg["NE"], cfg["ICH"]
    NPE = NE + 2
    KC = NPE * ICH
    I = ICH * 128

    f = lambda k: np.asarray(inputs[k], dtype=np.float32)
    x = f("L0_attn_x")
    egw, euw, edw = (f("L0_ffn_expert_gate_w"), f("L0_ffn_expert_up_w"),
                     f("L0_ffn_expert_down_w"))
    sgw, suw, sdw = (f("L0_ffn_shared_gate_w"), f("L0_ffn_shared_up_w"),
                     f("L0_ffn_shared_down_w"))
    gw = f("L0_ffn_gate_w")

    shared = {
        "wv": _pm_proj(f("L0_attn_Wv"), HC),
        "wr": _pm_proj(f("L0_attn_Wr"), HC),
        "wo": _pm_proj(f("L0_attn_Wo"), HC),
        "td": _pm_col(f("L0_attn_time_decay")),
        "tmv": _pm_col(f("L0_attn_time_mix_v")),
        "tmr": _pm_col(f("L0_attn_time_mix_r")),
        "gatew": np.ascontiguousarray(
            gw.T.reshape(HC, 128, NE).transpose(1, 0, 2).reshape(128, -1)
        ).astype(np.float32),
        "lbrow": np.ascontiguousarray(
            f("L0_ffn_lb_bias").reshape(1, NE)).astype(np.float32),
    }
    gu = []
    for e in range(NE):
        gu.append(np.stack([_pm_gu(egw[e], HC, ICH), _pm_gu(euw[e], HC, ICH)],
                           axis=1).reshape(128, -1))
    gu.append(np.stack([_pm_gu(sgw[0:I], HC, ICH), _pm_gu(suw[0:I], HC, ICH)],
                       axis=1).reshape(128, -1))
    gu.append(np.stack([_pm_gu(sgw[I:2 * I], HC, ICH),
                        _pm_gu(suw[I:2 * I], HC, ICH)], axis=1).reshape(128, -1))
    shared["gwuw"] = np.ascontiguousarray(np.stack(gu, axis=0)).astype(np.float32)
    dwT = np.concatenate([edw[e].T for e in range(NE)] + [sdw.T], axis=0)
    shared["dwcat"] = np.ascontiguousarray(
        dwT.reshape(KC, 128, HC, 128).transpose(1, 2, 0, 3).reshape(128, -1)
    ).astype(np.float32)

    in_maps = []
    for b in range(cfg["ncores"]):
        m = dict(shared)
        m["x"] = np.ascontiguousarray(
            x[b].T.reshape(HC, 128, T).transpose(1, 0, 2).reshape(128, -1)
        ).astype(np.float32)
        in_maps.append(m)
    return in_maps


def host_output(cfg, results):
    HC, T = cfg["HC"], cfg["T"]
    H = 128 * HC
    outs = []
    for r in results:
        arr = np.asarray(r["out"]).reshape(128, HC, T).transpose(1, 0, 2)
        outs.append(arr.reshape(H, T).T)
    return np.ascontiguousarray(np.stack(outs, axis=0)).astype(np.float32)


def kernel(**inputs):
    cfg = FULL_CFG
    nc = build_program(cfg)
    in_maps = host_inputs(cfg, inputs)
    res = run_bass_kernel_spmd(nc, in_maps, core_ids=list(range(cfg["ncores"])))
    return host_output(cfg, res.results)


# version 10 (final)
# speedup vs baseline: 1.0491x; 1.0250x over previous
import numpy as np
from contextlib import ExitStack

import concourse.bass as bass
import concourse.mybir as mybir
from concourse.bass_utils import run_bass_kernel_spmd

F32 = mybir.dt.float32
BF16 = mybir.dt.bfloat16
I32 = mybir.dt.int32
ALU = mybir.AluOpType
AF = mybir.ActivationFunctionType
AXX = mybir.AxisListType.X

FULL_CFG = dict(ncores=8, HC=8, T=2048, TC=512, NE=8, ICH=4)


class _Dummy:
    def then_inc(self, *a, **k):
        return self


class _EngP:

    def __getattr__(self, k):
        return lambda *a, **kw: _Dummy()


_DUMMY_ENG = _EngP()


_COMPUTE_OPS = {
    "tensor_tensor", "tensor_scalar", "tensor_copy", "tensor_add",
    "tensor_sub", "tensor_mul", "tensor_max", "tensor_scalar_add",
    "tensor_scalar_mul", "tensor_scalar_max", "tensor_scalar_min",
    "tensor_single_scalar", "scalar_tensor_tensor", "tensor_tensor_scan",
    "tensor_reduce", "reduce_max", "reduce_sum", "reciprocal", "memset",
    "activation", "copy", "add", "mul", "iota", "select", "copy_predicated",
}


class _ParanoidEng:

    def __init__(self, eng):
        self._e = eng

    def __getattr__(self, k):
        a = getattr(self._e, k)
        if k in _COMPUTE_OPS:
            def wrapped(*args, **kw):
                self._e.drain()
                return a(*args, **kw)
            return wrapped
        return a


def build_program(cfg, dbg_out=False):
    ncores = cfg["ncores"]
    HC, T, TC, NE, ICH = (cfg["HC"], cfg["T"], cfg["TC"], cfg["NE"],
                          cfg["ICH"])
    NCH = T // TC
    NPE = NE + 2
    KC = NPE * ICH
    NSTRIP = T // 128
    SPT = TC // 128
    NJG = (NSTRIP + 3) // 4
    invH = 1.0 / (128 * HC)
    EPS = 1e-5

    nc = bass.Bass("TRN2", target_bir_lowering=False,
                   detect_race_conditions=bool(cfg.get("race", False)))

    d_x = nc.dram_tensor("x", [128, HC * T], F32, kind="ExternalInput")
    d_wv = nc.dram_tensor("wv", [128, HC * HC * 128], F32, kind="ExternalInput")
    d_wr = nc.dram_tensor("wr", [128, HC * HC * 128], F32, kind="ExternalInput")
    d_wo = nc.dram_tensor("wo", [128, HC * HC * 128], F32, kind="ExternalInput")
    d_td = nc.dram_tensor("td", [128, HC], F32, kind="ExternalInput")
    d_tmv = nc.dram_tensor("tmv", [128, HC], F32, kind="ExternalInput")
    d_tmr = nc.dram_tensor("tmr", [128, HC], F32, kind="ExternalInput")
    d_gatew = nc.dram_tensor("gatew", [128, HC * NE], F32, kind="ExternalInput")
    d_lbrow = nc.dram_tensor("lbrow", [1, NE], F32, kind="ExternalInput")
    d_gwuw = nc.dram_tensor("gwuw", [NPE, 128, 2 * HC * ICH * 128], F32,
                            kind="ExternalInput")
    d_dw = nc.dram_tensor("dwcat", [128, HC * KC * 128], F32,
                          kind="ExternalInput")
    d_out = nc.dram_tensor("out", [128, HC * T], F32, kind="ExternalOutput")
    d_gwuwb = nc.dram_tensor("gwuwb", [NPE, 128, 2 * HC * ICH * 128], BF16)
    d_dwb = nc.dram_tensor("dwb", [128, HC * KC * 128], BF16)
    dbg_kind = dict(kind="ExternalOutput") if dbg_out else {}
    d_x1 = nc.dram_tensor("x1d", [128, HC * T], F32, **dbg_kind)
    d_ln2 = nc.dram_tensor("ln2d", [128, HC * T], BF16, **dbg_kind)
    d_sem = nc.dram_tensor("semrows", [8, T], F32, **dbg_kind)
    d_ccin = nc.dram_tensor("ccin", [1, 16], F32)
    d_ccout = nc.dram_tensor("ccout", [1, 16], F32)

    st = ExitStack()
    nc._ctx_keepalive = st

    def sb(name, shape, dt):
        return st.enter_context(nc.sbuf_tensor(name, shape, dt))

    ones_bfc = sb("ones_bfc", [128, 1], BF16)
    ones_fc = sb("ones_fc", [128, 1], F32)
    ones_fr = sb("ones_fr", [1, 128], F32)
    i128f = sb("i128f", [128, 128], F32)
    i8f = sb("i8f", [8, 8], F32)
    itmp = sb("itmp", [128, 128], I32)
    dcol = sb("dcol", [128, HC], F32)
    dtmp = sb("dtmp", [128, HC], F32)
    tmvc = sb("tmvc", [128, HC], F32)
    tmrc = sb("tmrc", [128, HC], F32)
    gatew = sb("gatew_sb", [128, HC * NE], F32)
    lbrow = sb("lbrow_sb", [1, NE], F32)
    carry = sb("carry", [128, HC], F32)
    rawsem = sb("rawsem", [8, T], F32)
    raw_em = rawsem
    s_em = rawsem
    g_t1 = sb("g_t1", [128, NSTRIP * 8], F32)
    g_lg = sb("g_lg", [128, NSTRIP * 8], F32)
    g_sc = sb("g_sc", [128, NSTRIP * 8], F32)
    g_sel = sb("g_sel", [128, NSTRIP * 8], F32)
    g_wc = sb("g_wc", [128, NSTRIP * 8], F32)
    g_m1 = sb("g_m1", [128, NSTRIP], F32)
    g_m2 = sb("g_m2", [128, NSTRIP], F32)
    g_s12 = sb("g_s12", [128, NSTRIP], F32)
    g_i12 = sb("g_i12", [128, NSTRIP], F32)
    mcols = sb("mcols", [128, NSTRIP], F32)
    iscols = sb("iscols", [128, NSTRIP], F32)
    gsumr = sb("gsumr", [1, NE], F32)
    gsumbc = sb("gsumbc", [128, NE], F32)
    lbbc = sb("lbbc", [128, NE], F32)
    wmrow = sb("wmrow", [1, NE], F32)
    wmcol = sb("wmcol", [8, 1], F32)
    ccin_sb = sb("ccin_sb", [1, 16], F32)
    ccout_sb = sb("ccout_sb", [1, 16], F32)
    srow = sb("srow", [1, 2 * TC], F32)

    LAY = {}
    off = [0]

    def lay(name, nbytes):
        a = off[0]
        off[0] = (a + nbytes + 31) & ~31
        LAY[name] = a

    lay("xres", 3 * HC * TC * 4)
    lay("wv", HC * HC * 128 * 2)
    lay("wr", HC * HC * 128 * 2)
    lay("wo", HC * HC * 128 * 2)
    lay("vin", 2 * HC * TC * 2)
    lay("rin", 2 * HC * TC * 2)
    lay("v_c", HC * TC * 2)
    lay("r_c", HC * TC * 2)
    lay("lnb", TC * 4)
    lay("diffb", TC * 4)
    lay("zb", (TC + 1) * 4)
    lay("cast1", 2 * 2 * TC * 2)
    lay("cast2", 2 * 2 * TC * 2)
    lay("ln2bf", 4 * TC * 2)
    lay("ab1", 2 * TC * 4)
    lay("ab2", 2 * TC * 4)
    lay("r_mean", TC * 4)
    lay("r_e2", TC * 4)
    lay("r_msq", TC * 4)
    lay("r_is", TC * 4)
    lay("r_b", TC * 4)
    lay("r2_mean", TC * 4)
    lay("r2_is", TC * 4)
    lay("r2_b", TC * 4)
    attn_end = off[0]
    off[0] = 0
    lay("x1t", 2 * HC * TC * 4)
    lay("ln2t", 2 * HC * TC * 2)
    assert off[0] <= LAY["wv"], "x1t/ln2t must stay inside xres"
    off[0] = LAY["wv"]
    lay("gwuws", 2 * 2 * HC * ICH * 128 * 2)
    lay("dws", 2 * KC * 128 * 2)
    lay("hid", KC * TC * 2)
    lay("gsb", 2 * TC * 4)
    lay("h1b", TC * 4)
    ffn_end = off[0]
    ARENA = max(attn_end, ffn_end)
    arena = sb("arena", [128, ARENA], mybir.dt.uint8)
    assert LAY["gwuws"] == LAY["wv"]

    def view(name, nbytes, dt):
        a = LAY[name]
        return arena[:, a:a + nbytes].bitcast(dt)

    v_xres = view("xres", 3 * HC * TC * 4, F32).rearrange(
        "p (s h t) -> p s h t", s=3, h=HC)
    v_wv = view("wv", HC * HC * 128 * 2, BF16)
    v_wr = view("wr", HC * HC * 128 * 2, BF16)
    v_wo = view("wo", HC * HC * 128 * 2, BF16)
    v_vin = view("vin", 2 * HC * TC * 2, BF16).rearrange(
        "p (s h t) -> p s h t", s=2, h=HC)
    v_rin = view("rin", 2 * HC * TC * 2, BF16).rearrange(
        "p (s h t) -> p s h t", s=2, h=HC)
    v_vc = view("v_c", HC * TC * 2, BF16).rearrange("p (h t) -> p h t", h=HC)
    v_rc = view("r_c", HC * TC * 2, BF16).rearrange("p (h t) -> p h t", h=HC)
    v_lnb = view("lnb", TC * 4, F32)
    v_diff = view("diffb", TC * 4, F32)
    v_zb = view("zb", (TC + 1) * 4, F32)
    v_cast1 = view("cast1", 2 * 2 * TC * 2, BF16).rearrange(
        "p (s w t) -> p s w t", s=2, w=2)
    v_cast2 = view("cast2", 2 * 2 * TC * 2, BF16).rearrange(
        "p (s w t) -> p s w t", s=2, w=2)
    v_ln2bf = view("ln2bf", 4 * TC * 2, BF16).rearrange(
        "p (s t) -> p s t", s=4)
    v_ab1 = view("ab1", 2 * TC * 4, F32)
    v_ab2 = view("ab2", 2 * TC * 4, F32)
    v_rmean = view("r_mean", TC * 4, F32)
    v_re2 = view("r_e2", TC * 4, F32)
    v_rmsq = view("r_msq", TC * 4, F32)
    v_ris = view("r_is", TC * 4, F32)
    v_rb = view("r_b", TC * 4, F32)
    v_r2mean = view("r2_mean", TC * 4, F32)
    v_r2is = view("r2_is", TC * 4, F32)
    v_r2b = view("r2_b", TC * 4, F32)
    v_x1t = view("x1t", 2 * HC * TC * 4, F32).rearrange(
        "p (s h t) -> p s h t", s=2, h=HC)
    v_ln2t = view("ln2t", 2 * HC * TC * 2, BF16).rearrange(
        "p (s h t) -> p s h t", s=2, h=HC)
    v_gwuw = view("gwuws", 2 * 2 * HC * ICH * 128 * 2, BF16).rearrange(
        "p (s g h i j) -> p s g h i j", s=2, g=2, h=HC, i=ICH)
    v_dws = view("dws", 2 * KC * 128 * 2, BF16).rearrange(
        "p (s k j) -> p s k j", s=2, k=KC)
    v_hid = view("hid", KC * TC * 2, BF16).rearrange("p (k t) -> p k t", k=KC)
    v_gsb = view("gsb", 2 * TC * 4, F32)
    v_h1b = view("h1b", TC * 4, F32)

    def xr(c):
        return v_xres[:, c % 3, :, :]

    def wvw(w, oc, hc):
        o = (oc * HC + hc) * 128
        return w[:, o:o + 128]

    pb = [st.enter_context(nc.psum_tensor(f"pb{i}", [128, 512], F32))
          for i in range(8)]

    sem_names = ["ge", "cc", "pe", "ve", "se",
                 "sm", "wat", "cst", "gwa", "gwb", "dwa", "dwb", "ccb",
                 "sx0", "sx1", "sx2", "x1s0", "x1s1", "x1s2", "x1s3",
                 "l2a", "l2b", "l2c", "l2d", "smr", "xta", "xtb", "lta", "ltb",
                 "sra", "srb", "out"]
    SEM = {n: st.enter_context(nc.semaphore(n)) for n in sem_names}

    def dsem(key):
        if key == "smalls":
            return "sm"
        if key in ("wv", "wr", "wo"):
            return "wat"
        if key == "dwcast" or key.startswith("gwcast"):
            return "cst"
        if key.startswith("gws_"):
            _, p, e = key.split("_")
            eord = [NE, NE + 1] + list(range(NE))
            si = eord.index(int(e))
            return "gwa" if (int(p) * NPE + si) % 2 == 0 else "gwb"
        if key.startswith("dws_"):
            _, p, hc = key.split("_")
            return "dwa" if (int(p) * HC + int(hc)) % 2 == 0 else "dwb"
        if key in ("ccin_d", "ccout"):
            return "ccb"
        if key.startswith("x_"):
            return f"sx{int(key[2:]) % 3}"
        if key.startswith("x1sp_"):
            return f"x1s{int(key[5:])}"
        if key.startswith("ln2sp_"):
            _, c, hc = key.split("_")
            return ("l2a", "l2b", "l2c", "l2d")[(int(c) * HC + int(hc)) % 4]
        if key == "semrows_d":
            return "smr"
        if key.startswith("x1t_"):
            return "xta" if int(key[4:]) % 2 == 0 else "xtb"
        if key.startswith("ln2t_"):
            return "lta" if int(key[5:]) % 2 == 0 else "ltb"
        if key.startswith("srow_"):
            _, p, e = key.split("_")
            return "sra" if (int(p) * NE + int(e)) % 2 == 0 else "srb"
        if key.startswith("out_"):
            return "out"
        raise KeyError(key)

    S = {}

    def reset(emit):
        S["emit"] = emit
        S["cnt"] = {n: 0 for n in sem_names}
        S["pe_rd"] = {}
        if not emit:
            S["led"] = {}

    def INC(inst, sem, key=None, n=1):
        if sem in ("gw", "sx", "sp"):
            sem = dsem(key)
            n = 16
        S["cnt"][sem] += n
        if key is not None and not S["emit"]:
            S["led"][(sem, key)] = S["cnt"][sem]
        if S["emit"] and inst is not None and not isinstance(inst, _Dummy):
            if sem in ("se", "ve") and S.get("cur") is not None and cfg.get("dr", 1):
                S["cur"].drain()
                inst = S["cur"].nop()
            inst.then_inc(SEM[sem], n)

    def ALIAS(sem, key, src):
        if not S["emit"]:
            S["led"][(sem, key)] = S["led"][(sem, src)]

    def WAIT(eng, sem, key):
        if sem in ("gw", "sx", "sp"):
            sem = dsem(key)
        if S["emit"]:
            v = S["led"].get((sem, key))
            assert v is not None, f"missing ledger {(sem, key)}"
            eng.wait_ge(SEM[sem], v)

    def bank_w(eng, bi, reader):
        prev = S["pe_rd"].get(bi)
        if prev is not None:
            WAIT(eng, *prev)
        S["pe_rd"][bi] = reader

    def emit_gpsimd(g):
        S["cur"] = g if not isinstance(g, _EngP) else None
        for dst, src in ((dtmp, d_td), (tmvc, d_tmv), (tmrc, d_tmr),
                         (gatew, d_gatew)):
            i0 = g.dma_start(out=dst[:, :], in_=src[:, :])
            INC(i0, "gw", "smalls")
        i0 = g.dma_start(out=lbrow[:, :], in_=d_lbrow[:, :])
        INC(i0, "gw", "smalls")
        g.iota(itmp[:, :], pattern=[[1, 128]], base=0, channel_multiplier=-1)
        g.drain()
        g.tensor_scalar(out=i128f[:, :], in0=itmp[:, :], scalar1=0.0,
                        scalar2=None, op0=ALU.is_equal)
        g.iota(itmp[0:8, 0:8], pattern=[[1, 8]], base=0, channel_multiplier=-1)
        g.drain()
        i0 = g.tensor_scalar(out=i8f[:, :], in0=itmp[0:8, 0:8], scalar1=0.0,
                             scalar2=None, op0=ALU.is_equal)
        g.drain()
        INC(i0, "ge", "ident")
        i0 = g.dma_start(out=v_wv, in_=d_wv[:, :])
        INC(i0, "gw", "wv")
        i0 = g.dma_start(out=v_wr, in_=d_wr[:, :])
        INC(i0, "gw", "wr")
        i0 = g.dma_start(out=v_wo, in_=d_wo[:, :])
        INC(i0, "gw", "wo")
        pace = {2: 0, 5: 1, 8: 2}
        for e in range(NPE):
            if e in pace:
                WAIT(g, "sp", f"x1sp_{min(pace[e], NCH - 1)}")
            i0 = g.dma_start(out=d_gwuwb[e, :, :], in_=d_gwuw[e, :, :])
            INC(i0, "gw", f"gwcast{e}")
        i0 = g.dma_start(out=d_dwb[:, :], in_=d_dw[:, :])
        INC(i0, "gw", "dwcast")
        EORDER_G = [NE, NE + 1] + list(range(NE))
        for s0 in range(min(2, NPE)):
            e0 = EORDER_G[s0]
            WAIT(g, "pe", f"wr_{NCH - 1}_{HC - 1}")
            WAIT(g, "gw", f"gwcast{NPE - 1}")
            i0 = g.dma_start(out=v_gwuw[:, s0 % 2, :, :, :, :],
                             in_=d_gwuwb[e0, :, :])
            INC(i0, "gw", f"gws_0_{e0}")
        WAIT(g, "gw", "ccin_d")
        i0 = g.collective_compute(
            "AllReduce", ALU.add, replica_groups=[list(range(ncores))],
            ins=[d_ccin.ap().opt()], outs=[d_ccout.ap().opt()])
        INC(i0, "cc", "ccdone")
        for p in range(NCH):
            for si, e in enumerate(EORDER_G):
                k = p * NPE + si
                if k < 2:
                    continue
                pk, sk = divmod(k - 2, NPE)
                WAIT(g, "pe", f"guexp_{pk}_{sk}")
                WAIT(g, "gw", f"gwcast{NPE - 1}")
                i0 = g.dma_start(out=v_gwuw[:, k % 2, :, :, :, :],
                                 in_=d_gwuwb[e, :, :])
                INC(i0, "gw", f"gws_{p}_{e}")
            for hc in range(HC):
                k = p * HC + hc
                if k == 0:
                    WAIT(g, "pe", f"wo_{NCH - 1}_{HC - 1}")
                    WAIT(g, "gw", "dwcast")
                if k >= 2:
                    pk, hk = divmod(k - 2, HC)
                    WAIT(g, "pe", f"dn_{pk}_{hk}")
                i0 = g.dma_start(out=v_dws[:, k % 2, :, :],
                                 in_=d_dwb[:, hc * KC * 128:(hc + 1) * KC * 128])
                INC(i0, "gw", f"dws_{p}_{hc}")

    def emit_sync(sy):
        S["cur"] = None
        for c in range(min(3, NCH)):
            i0 = sy.dma_start(
                out=xr(c)[:, :, :],
                in_=d_x[:, :].rearrange("p (h t) -> p h t", h=HC)[:, :, c * TC:(c + 1) * TC])
            INC(i0, "sx", f"x_{c}")
        for i in range(NCH + 2):
            if 1 <= i <= NCH:
                c = i - 1
                WAIT(sy, "ve", f"epi_{c}")
                i0 = sy.dma_start(
                    out=d_x1[:, :].rearrange("p (h t) -> p h t", h=HC)[:, :, c * TC:(c + 1) * TC],
                    in_=xr(c)[:, :, :])
                INC(i0, "sp", f"x1sp_{c}")
            if 2 <= i <= NCH + 1:
                c = i - 2
                for hc in range(HC):
                    WAIT(sy, "ve", f"ln2_{c}_{hc}")
                    i0 = sy.dma_start(
                        out=d_ln2[:, :].rearrange("p (h t) -> p h t", h=HC)[:, hc, c * TC:(c + 1) * TC],
                        in_=v_ln2bf[:, (c * HC + hc) % 4, :])
                    INC(i0, "sp", f"ln2sp_{c}_{hc}")
                if i == 4 and NCH >= 3:
                    WAIT(sy, "sp", "x1sp_2")
                    WAIT(sy, "pe", "raw_2")
                    for kk in range(max(0, HC - 4), HC):
                        WAIT(sy, "sp", f"ln2sp_0_{kk}")
                    i0 = sy.dma_start(
                        out=v_ln2t[:, 0, :, :],
                        in_=d_ln2[:, :].rearrange("p (h t) -> p h t", h=HC)[:, :, 0:TC])
                    INC(i0, "sx", "ln2t_0")
            c = i + 1
            if 3 <= c < NCH:
                WAIT(sy, "pe", f"raw_{c - 3}")
                WAIT(sy, "ve", f"ln2_{c - 3}_{HC - 1}")
                i0 = sy.dma_start(
                    out=xr(c)[:, :, :],
                    in_=d_x[:, :].rearrange("p (h t) -> p h t", h=HC)[:, :, c * TC:(c + 1) * TC])
                INC(i0, "sx", f"x_{c}")
        def rd_pass(p):
            WAIT(sy, "sp", f"x1sp_{p}")
            if p == 0:
                WAIT(sy, "pe", f"raw_{NCH - 1}")
                WAIT(sy, "ve", f"ln2_{NCH - 1}_{HC - 1}")
                for pp in range(NCH):
                    WAIT(sy, "sp", f"x1sp_{pp}")
            if p >= 2:
                WAIT(sy, "ve", f"evacend_{p - 2}")
            i0 = sy.dma_start(
                out=v_x1t[:, p % 2, :, :],
                in_=d_x1[:, :].rearrange("p (h t) -> p h t", h=HC)[:, :, p * TC:(p + 1) * TC])
            INC(i0, "sx", f"x1t_{p}")
            if p == 0 and NCH >= 3:
                return
            k_last = p * HC + HC - 1
            for kk in range(max(0, k_last - 3), k_last + 1):
                WAIT(sy, "sp", f"ln2sp_{kk // HC}_{kk % HC}")
            if p >= 2:
                WAIT(sy, "pe", f"guend_{p - 2}")
            i0 = sy.dma_start(
                out=v_ln2t[:, p % 2, :, :],
                in_=d_ln2[:, :].rearrange("p (h t) -> p h t", h=HC)[:, :, p * TC:(p + 1) * TC])
            INC(i0, "sx", f"ln2t_{p}")

        def rd_srows(p):
            for e in range(NE):
                k = p * NE + e
                if k == 0:
                    WAIT(sy, "sp", "semrows_d")
                if k >= 2:
                    pk, ek = divmod(k - 2, NE)
                    WAIT(sy, "pe", f"bselmm_{pk}_{ek}")
                i0 = sy.dma_start(out=srow[0:1, (k % 2) * TC:(k % 2 + 1) * TC],
                                  in_=d_sem[e:e + 1, p * TC:(p + 1) * TC])
                INC(i0, "sx", f"srow_{p}_{e}")

        rd_pass(0)
        rd_pass(1)
        WAIT(sy, "ve", "ccin")
        i0 = sy.dma_start(out=d_ccin[:, :], in_=ccin_sb[:, :])
        INC(i0, "gw", "ccin_d")
        WAIT(sy, "cc", "ccdone")
        i0 = sy.dma_start(out=ccout_sb[:, :], in_=d_ccout[:, :])
        INC(i0, "gw", "ccout")
        WAIT(sy, "ve", "semrows")
        i0 = sy.dma_start(out=d_sem[:, :], in_=s_em[:, :])
        INC(i0, "sp", "semrows_d")
        rd_srows(0)
        for p in range(NCH):
            if p + 1 < NCH:
                rd_srows(p + 1)
            for hc in range(HC):
                WAIT(sy, "ve", f"evac_{p}_{hc}")
                i0 = sy.dma_start(
                    out=d_out[:, :].rearrange("p (h t) -> p h t", h=HC)[:, hc, p * TC:(p + 1) * TC],
                    in_=v_x1t[:, p % 2, hc, :])
                INC(i0, "sp", f"out_{p}_{hc}")
            if p + 2 < NCH:
                rd_pass(p + 2)
        WAIT(sy, "sp", f"out_{NCH - 1}_{HC - 1}")

    def emit_scalar(s):
        S["cur"] = s if not isinstance(s, _EngP) else None
        WAIT(s, "gw", "smalls")
        s.activation(out=dtmp[:, :], in_=dtmp[:, :], func=AF.Exp)
        s.drain()
        i0 = s.activation(out=dcol[:, :], in_=dtmp[:, :], func=AF.Exp,
                          scale=-1.0)
        s.drain()
        INC(i0, "se", "dcol")

        def casts(c, cviews, tag):
            if tag == "1":
                WAIT(s, "sx", f"x_{c}")
            for hc in range(HC):
                k = c * HC + hc
                if tag == "2":
                    WAIT(s, "ve", f"epioc_{c}_{hc}")
                if k >= 2:
                    pk, hk = divmod(k - 2, HC)
                    WAIT(s, "pe", f"st{tag}mm_{pk}_{hk}")
                s.copy(out=cviews[:, hc % 2, 0, :], in_=xr(c)[:, hc, :])
                i0 = s.activation(out=cviews[:, hc % 2, 1, :],
                                  in_=xr(c)[:, hc, :], func=AF.Square)
                if hc % 2 == 1 or hc == HC - 1:
                    INC(i0, "se", f"cast{tag}_{c}_{hc}")
                    if hc % 2 == 1:
                        ALIAS("se", f"cast{tag}_{c}_{hc - 1}", f"cast{tag}_{c}_{hc}")

        for i in range(NCH + 2):
            if i < NCH:
                casts(i, v_cast1, "1")
            if 2 <= i <= NCH + 1:
                c = i - 2
                WAIT(s, "ve", f"vrow2_{c}")
                i0 = s.activation(out=v_rmsq[0:1, :], in_=v_re2[0:1, :], func=AF.Sqrt)
                INC(i0, "se", f"sqrt2_{c}")
            if i < NCH:
                WAIT(s, "ve", f"vrow1_{i}")
                i0 = s.activation(out=v_rmsq[0:1, :], in_=v_re2[0:1, :], func=AF.Sqrt)
                INC(i0, "se", f"sqrt1_{i}")
            if 1 <= i <= NCH:
                c = i - 1
                for oc in range(HC):
                    WAIT(s, "pe", f"wv_{c}_{oc}")
                    i0 = s.copy(out=v_vc[:, oc, :], in_=pb[4 + oc % 2][:, 0:TC])
                    if oc % 2 == 1 or oc == HC - 1:
                        INC(i0, "se", f"evv_{c}_{oc}")
                        if oc % 2 == 1:
                            ALIAS("se", f"evv_{c}_{oc - 1}", f"evv_{c}_{oc}")
                    else:
                        pass
            if i < NCH:
                c = i
                if c >= 1:
                    WAIT(s, "ve", f"mix_{c - 1}")
                WAIT(s, "pe", f"bc1a_{c}")
                s.copy(out=v_ab1[:, 0:TC], in_=pb[2][:, 0:TC])
                WAIT(s, "pe", f"bc1_{c}")
                i0 = s.copy(out=v_ab1[:, TC:2 * TC], in_=pb[3][:, 0:TC])
                INC(i0, "se", f"ab1_{c}")
            if 2 <= i <= NCH + 1:
                c = i - 2
                if c >= 1:
                    WAIT(s, "ve", f"ln2_{c - 1}_{HC - 1}")
                WAIT(s, "pe", f"bc2a_{c}")
                s.copy(out=v_ab2[:, 0:TC], in_=pb[2][:, 0:TC])
                WAIT(s, "pe", f"bc2_{c}")
                i0 = s.copy(out=v_ab2[:, TC:2 * TC], in_=pb[3][:, 0:TC])
                INC(i0, "se", f"ab2_{c}")
            if 2 <= i <= NCH + 1:
                c = i - 2
                WAIT(s, "pe", f"mcolT_{c}")
                s.copy(out=mcols[:, c * SPT:(c + 1) * SPT],
                       in_=pb[4][:, 0:SPT])
                i0 = s.copy(out=iscols[:, c * SPT:(c + 1) * SPT],
                            in_=pb[5][:, 0:SPT])
                INC(i0, "se", f"mcev_{c}")
            if 1 <= i <= NCH:
                c = i - 1
                for oc in range(HC):
                    WAIT(s, "pe", f"wr_{c}_{oc}")
                    i0 = s.activation(out=v_rc[:, oc, :],
                                      in_=pb[6 + oc % 2][:, 0:TC],
                                      func=AF.Sigmoid)
                    if oc % 2 == 1 or oc == HC - 1:
                        INC(i0, "se", f"sig_{c}_{oc}")
                        if oc % 2 == 1:
                            ALIAS("se", f"sig_{c}_{oc - 1}", f"sig_{c}_{oc}")
                    else:
                        pass
            if 1 <= i <= NCH:
                casts(i - 1, v_cast2, "2")
        WAIT(s, "pe", "gsumbc_mm")
        i0 = s.copy(out=gsumbc[:, :], in_=pb[6][:, 0:NE])
        INC(i0, "se", "gsumbc")
        WAIT(s, "pe", "lbbc_mm")
        i0 = s.copy(out=lbbc[:, :], in_=pb[7][:, 0:NE])
        INC(i0, "se", "lbbc")
        WAIT(s, "ve", "logits")
        i0 = s.activation(out=g_sc[:, :], in_=g_lg[:, :], func=AF.Sigmoid)
        INC(i0, "se", "scores")
        EORDER_S = [NE, NE + 1] + list(range(NE))
        sq_keys = []
        for p in range(NCH):
            for si, e in enumerate(EORDER_S):
                for ic in range(ICH):
                    kc = e * ICH + ic
                    q = len(sq_keys)
                    sq_keys.append((p, kc))
                    if q == 0:
                        k_last = NCH * HC - 1
                        for kk in range(max(0, k_last - 3), k_last + 1):
                            WAIT(s, "sp", f"ln2sp_{kk // HC}_{kk % HC}")
                    if q >= 2:
                        pk, kk = sq_keys[q - 2]
                        WAIT(s, "ve", f"hid_{pk}_{kk}")
                    WAIT(s, "pe", f"g_{p}_{e}_{ic}")
                    i0 = s.activation(out=v_gsb[:, (q % 2) * TC:(q % 2 + 1) * TC],
                                      in_=pb[q % 2][:, 0:TC], func=AF.Silu)
                    INC(i0, "se", f"silu_{p}_{e}_{ic}")

    def emit_pe(t):
        S["cur"] = t if not isinstance(t, _EngP) else None
        WAIT(t, "ve", "ones")
        def stats_mm(c, tag, cviews, blo, bhi):
            for hc in range(HC):
                WAIT(t, "se", f"cast{tag}_{c}_{hc}")
                if hc == 0:
                    bank_w(t, blo, ("ve", f"stread{tag}_{c}"))
                    bank_w(t, bhi, ("ve", f"stread{tag}_{c}"))
                t.matmul(pb[blo][0:1, 0:TC], ones_bfc[:, :],
                         cviews[:, hc % 2, 0, :],
                         start=(hc == 0), stop=(hc == HC - 1))
                i0 = t.matmul(pb[bhi][0:1, 0:TC], ones_bfc[:, :],
                              cviews[:, hc % 2, 1, :],
                              start=(hc == 0), stop=(hc == HC - 1))
                INC(i0, "pe", f"st{tag}mm_{c}_{hc}")
            ALIAS("pe", f"st{tag}_{c}", f"st{tag}mm_{c}_{HC - 1}")

        for i in range(NCH + 2):
            if i < NCH:
                stats_mm(i, "1", v_cast1, 0, 1)
            if 1 <= i <= NCH:
                c = i - 1
                WAIT(t, "ve", f"mix_{c}")
                if c == 0:
                    WAIT(t, "gw", "wo")
                for oc in range(HC):
                    bank_w(t, 4 + oc % 2, ("se", f"evv_{c}_{oc}"))
                    for hc in range(HC):
                        i0 = t.matmul(pb[4 + oc % 2][:, 0:TC],
                                      wvw(v_wv, oc, hc), v_vin[:, c % 2, hc, :],
                                      start=(hc == 0), stop=(hc == HC - 1))
                    INC(i0, "pe", f"wv_{c}_{oc}")
            if i < NCH:
                c = i
                WAIT(t, "ve", f"rows1_{c}")
                bank_w(t, 2, ("se", f"ab1_{c}"))
                i0 = t.matmul(pb[2][:, 0:TC], ones_fr[:, :], v_ris[0:1, :],
                              start=True, stop=True)
                INC(i0, "pe", f"bc1a_{c}")
                bank_w(t, 3, ("se", f"ab1_{c}"))
                i0 = t.matmul(pb[3][:, 0:TC], ones_fr[:, :], v_rb[0:1, :],
                              start=True, stop=True)
                INC(i0, "pe", f"bc1_{c}")
            if 2 <= i <= NCH + 1:
                c = i - 2
                WAIT(t, "ve", f"rows2_{c}")
                bank_w(t, 2, ("se", f"ab2_{c}"))
                i0 = t.matmul(pb[2][:, 0:TC], ones_fr[:, :], v_r2is[0:1, :],
                              start=True, stop=True)
                INC(i0, "pe", f"bc2a_{c}")
                bank_w(t, 3, ("se", f"ab2_{c}"))
                i0 = t.matmul(pb[3][:, 0:TC], ones_fr[:, :], v_r2b[0:1, :],
                              start=True, stop=True)
                INC(i0, "pe", f"bc2_{c}")
                bank_w(t, 4, ("se", f"mcev_{c}"))
                for sp in range(SPT):
                    i0 = t.matmul(pb[4][:, sp:sp + 1],
                                  v_r2mean[0:1, sp * 128:(sp + 1) * 128],
                                  ones_fr[0:1, 0:1], start=True, stop=True)
                INC(i0, "pe", f"mcolT_m_{c}")
                bank_w(t, 5, ("se", f"mcev_{c}"))
                for sp in range(SPT):
                    i0 = t.matmul(pb[5][:, sp:sp + 1],
                                  v_r2is[0:1, sp * 128:(sp + 1) * 128],
                                  ones_fr[0:1, 0:1], start=True, stop=True)
                INC(i0, "pe", f"mcolT_{c}")
                WAIT(t, "se", f"ab2_{c}")
                bank_w(t, 2, ("ve", f"rawev_{c}"))
                for hc in range(HC):
                    i0 = t.matmul(pb[2][0:8, 0:TC],
                                  gatew[:, hc * NE:(hc + 1) * NE],
                                  xr(c)[:, hc, :],
                                  start=(hc == 0), stop=(hc == HC - 1))
                INC(i0, "pe", f"raw_{c}")
            if 1 <= i <= NCH:
                c = i - 1
                for oc in range(HC):
                    bank_w(t, 6 + oc % 2, ("se", f"sig_{c}_{oc}"))
                    for hc in range(HC):
                        i0 = t.matmul(pb[6 + oc % 2][:, 0:TC],
                                      wvw(v_wr, oc, hc), v_rin[:, c % 2, hc, :],
                                      start=(hc == 0), stop=(hc == HC - 1))
                    INC(i0, "pe", f"wr_{c}_{oc}")
            if 1 <= i <= NCH:
                c = i - 1
                WAIT(t, "se", f"evv_{c}_{HC - 1}")
                for oc in range(HC):
                    bank_w(t, 6 + oc % 2, ("ve", f"epioc_{c}_{oc}"))
                    for vc in range(HC):
                        i0 = t.matmul(pb[6 + oc % 2][:, 0:TC],
                                      wvw(v_wo, oc, vc), v_vc[:, vc, :],
                                      start=(vc == 0), stop=(vc == HC - 1))
                    INC(i0, "pe", f"wo_{c}_{oc}")
            if 1 <= i <= NCH:
                stats_mm(i - 1, "2", v_cast2, 4, 5)
        WAIT(t, "ge", "ident")
        bank_w(t, 6, ("ve", "gsumr"))
        i0 = t.matmul(pb[6][0:1, 0:HC * NE], ones_fc[:, :], gatew[:, :],
                      start=True, stop=True)
        INC(i0, "pe", "gsum_mm")
        WAIT(t, "ve", "gsumr")
        bank_w(t, 6, ("se", "gsumbc"))
        i0 = t.matmul(pb[6][:, 0:NE], ones_fr[:, :], gsumr[:, :],
                      start=True, stop=True)
        INC(i0, "pe", "gsumbc_mm")
        bank_w(t, 7, ("se", "lbbc"))
        WAIT(t, "gw", "smalls")
        i0 = t.matmul(pb[7][:, 0:NE], ones_fr[:, :], lbrow[:, :],
                      start=True, stop=True)
        INC(i0, "pe", "lbbc_mm")
        for j in range(NSTRIP):
            bi = j % 2
            bank_w(t, bi, ("ve", f"t1_{j}"))
            i0 = t.matmul(pb[bi][:, (j // 2 % 2) * 8:(j // 2 % 2) * 8 + 8],
                          raw_em[0:8, j * 128:(j + 1) * 128], i8f[:, :],
                          start=True, stop=True)
            INC(i0, "pe", f"rawT_{j}")
        def gate_tail_a():
            WAIT(t, "ve", "wc")
            bank_w(t, 6, ("ve", "ccin"))
            i0 = t.matmul(pb[6][0:1, 0:NSTRIP * 8], ones_fc[:, :], g_wc[:, :],
                          start=True, stop=True)
            INC(i0, "pe", "part_wc")
            bank_w(t, 7, ("ve", "ccin"))
            i0 = t.matmul(pb[7][0:1, 0:NSTRIP * 8], ones_fc[:, :], g_sel[:, :],
                          start=True, stop=True)
            INC(i0, "pe", "part_sel")

        def gate_tail_b():
            WAIT(t, "ve", "wmrow")
            bank_w(t, 6, ("ve", "wmcol"))
            i0 = t.matmul(pb[6][0:8, 0:1], wmrow[:, :], ones_fr[0:1, 0:1],
                          start=True, stop=True)
            INC(i0, "pe", "wmcol_mm")
            for jg in range(NJG):
                bank_w(t, 7, ("ve", f"sem_{jg}"))
                n_in_g = min(4, NSTRIP - jg * 4)
                for jj in range(n_in_g):
                    j = jg * 4 + jj
                    i0 = t.matmul(pb[7][0:8, jj * 128:(jj + 1) * 128],
                                  g_sel[:, :].rearrange("p (j e) -> p j e", e=8)[:, j, :],
                                  i128f[:, :], start=True, stop=True)
                INC(i0, "pe", f"selT_{jg}")

        EORDER = [NE, NE + 1] + list(range(NE))

        def bselmms(p, es):
            for e in es:
                WAIT(t, "sx", f"srow_{p}_{e}")
                if p == 0:
                    bank_w(t, 4, ("ve", f"hsc_{e}"))
                else:
                    bank_w(t, 4, ("ve", f"hid_{p}_{e * ICH + ICH - 1}"))
                k = p * NE + e
                i0 = t.matmul(pb[4][:, 0:TC], ones_fr[:, :],
                              srow[0:1, (k % 2) * TC:(k % 2 + 1) * TC],
                              start=True, stop=True)
                INC(i0, "pe", f"bselmm_{p}_{e}")

        for p in range(NCH):
            WAIT(t, "sx", f"ln2t_{p}")
            for si, e in enumerate(EORDER):
                if p == 0 and si == 2:
                    gate_tail_a()
                if p > 0 and e < NE:
                    bselmms(p, [e])
                WAIT(t, "gw", f"gws_{p}_{e}")
                slot = (p * NPE + si) % 2
                for ic in range(ICH):
                    kc = e * ICH + ic
                    sq = si * ICH + ic
                    bank_w(t, sq % 2, ("se", f"silu_{p}_{e}_{ic}"))
                    for hc in range(HC):
                        i0 = t.matmul(pb[sq % 2][:, 0:TC],
                                      v_gwuw[:, slot, 0, hc, ic, :],
                                      v_ln2t[:, p % 2, hc, :],
                                      start=(hc == 0), stop=(hc == HC - 1))
                    INC(i0, "pe", f"g_{p}_{e}_{ic}")
                    bank_w(t, 2 + sq % 2, ("ve", f"hid_{p}_{kc}"))
                    for hc in range(HC):
                        i0 = t.matmul(pb[2 + sq % 2][:, 0:TC],
                                      v_gwuw[:, slot, 1, hc, ic, :],
                                      v_ln2t[:, p % 2, hc, :],
                                      start=(hc == 0), stop=(hc == HC - 1))
                    INC(i0, "pe", f"u_{p}_{e}_{ic}")
                ALIAS("pe", f"guexp_{p}_{si}", f"u_{p}_{e}_{ICH - 1}")
            ALIAS("pe", f"guend_{p}", f"u_{p}_{EORDER[-1]}_{ICH - 1}")
            if p == 0:
                gate_tail_b()
                bselmms(0, range(NE))
            for hc in range(HC):
                WAIT(t, "gw", f"dws_{p}_{hc}")
                bank_w(t, 5 + hc % 2, ("ve", f"evac_{p}_{hc}"))
                slot = (p * HC + hc) % 2
                for kc in range(KC):
                    if hc == 0 and kc == 0:
                        WAIT(t, "ve", f"hidend_{p}")
                    i0 = t.matmul(pb[5 + hc % 2][:, 0:TC],
                                  v_dws[:, slot, kc, :], v_hid[:, kc, :],
                                  start=(kc == 0), stop=(kc == KC - 1))
                INC(i0, "pe", f"dn_{p}_{hc}")

    def emit_vector(v):
        S["cur"] = v if not isinstance(v, _EngP) else None
        v.memset(carry[:, :], 0.0)
        v.memset(ones_bfc[:, :], 1.0)
        v.memset(ones_fc[:, :], 1.0)
        v.memset(ones_fr[:, :], 1.0)
        i0 = v.nop()
        INC(i0, "ve", "ones")
        rows_prev = [None]

        def rows(c, tag):
            if tag == "1":
                rm, ri, rb2 = v_rmean, v_ris, v_rb
                prev_pe = f"bc1_{c - 1}" if c >= 1 else None
                blo, bhi = 0, 1
            else:
                rm, ri, rb2 = v_r2mean, v_r2is, v_r2b
                prev_pe = f"mcolT_{c - 1}" if c >= 1 else None
                blo, bhi = 4, 5
            if prev_pe is not None:
                WAIT(v, "pe", prev_pe)
            if rows_prev[0] is not None:
                WAIT(v, "se", rows_prev[0])
            rows_prev[0] = f"sqrt{tag}_{c}"
            WAIT(v, "pe", f"st{tag}_{c}")
            v.tensor_scalar(out=rm[0:1, :], in0=pb[blo][0:1, 0:TC],
                            scalar1=invH, scalar2=None, op0=ALU.mult)
            v.tensor_scalar(out=v_re2[0:1, :], in0=pb[bhi][0:1, 0:TC],
                            scalar1=invH, scalar2=EPS,
                            op0=ALU.mult, op1=ALU.add)
            i0 = v.nop()
            INC(i0, "ve", f"stread{tag}_{c}")
            v.tensor_tensor(out=v_rmsq[0:1, :], in0=rm[0:1, :],
                            in1=rm[0:1, :], op=ALU.mult)
            v.drain()
            v.tensor_tensor(out=v_re2[0:1, :], in0=v_re2[0:1, :],
                            in1=v_rmsq[0:1, :], op=ALU.subtract)
            i0 = v.nop()
            INC(i0, "ve", f"vrow{tag}_{c}")
            WAIT(v, "se", f"sqrt{tag}_{c}")
            v.reciprocal(out=ri[0:1, :], in_=v_rmsq[0:1, :])
            v.drain()
            i0 = v.tensor_tensor(out=rb2[0:1, :], in0=rm[0:1, :],
                                 in1=ri[0:1, :], op=ALU.mult)
            INC(i0, "ve", f"rows{tag}_{c}")

        for i in range(NCH + 2):
            if 2 <= i <= NCH + 1:
                rows(i - 2, "2")
            if i < NCH:
                rows(i, "1")
            if i < NCH:
                c = i
                WAIT(v, "se", f"ab1_{c}")
                WAIT(v, "se", "dcol")
                if c >= 2:
                    WAIT(v, "pe", f"wr_{c - 2}_{HC - 1}")
                for hc in range(HC):
                    v.tensor_copy(out=v_zb[:, 0:1], in_=carry[:, hc:hc + 1])
                    v.tensor_tensor(out=v_lnb, in0=xr(c)[:, hc, :],
                                    in1=v_ab1[:, 0:TC], op=ALU.mult)
                    v.tensor_tensor(out=v_lnb, in0=v_lnb,
                                    in1=v_ab1[:, TC:2 * TC], op=ALU.subtract)
                    v.tensor_tensor_scan(
                        out=v_zb[:, 1:TC + 1],
                        data0=dcol[:, hc:hc + 1].to_broadcast([128, TC]),
                        data1=v_lnb, initial=v_zb[:, 0:1],
                        op0=ALU.mult, op1=ALU.add)
                    v.tensor_tensor(out=v_diff, in0=v_lnb, in1=v_zb[:, 0:TC],
                                    op=ALU.subtract)
                    v.scalar_tensor_tensor(
                        out=v_vin[:, c % 2, hc, :], in0=v_diff,
                        scalar=tmvc[:, hc:hc + 1], in1=v_zb[:, 0:TC],
                        op0=ALU.mult, op1=ALU.add)
                    v.scalar_tensor_tensor(
                        out=v_rin[:, c % 2, hc, :], in0=v_diff,
                        scalar=tmrc[:, hc:hc + 1], in1=v_zb[:, 0:TC],
                        op0=ALU.mult, op1=ALU.add)
                    v.tensor_copy(out=carry[:, hc:hc + 1],
                                  in_=v_zb[:, TC:TC + 1])
                    if hc == HC - 1:
                        i0 = v.nop()
                        INC(i0, "ve", f"mix_{c}")
            if 1 <= i <= NCH:
                c = i - 1
                for oc in range(HC):
                    WAIT(v, "pe", f"wo_{c}_{oc}")
                    WAIT(v, "se", f"sig_{c}_{oc}")
                    v.tensor_tensor(out=v_diff, in0=pb[6 + oc % 2][:, 0:TC],
                                    in1=v_rc[:, oc, :], op=ALU.mult)
                    v.tensor_tensor(out=xr(c)[:, oc, :], in0=v_diff,
                                    in1=xr(c)[:, oc, :], op=ALU.add)
                    i0 = v.nop()
                    INC(i0, "ve", f"epioc_{c}_{oc}")
                i0 = v.nop()
                INC(i0, "ve", f"epi_{c}")
            if 2 <= i <= NCH + 1:
                c = i - 2
                WAIT(v, "se", f"ab2_{c}")
                for hc in range(HC):
                    k = c * HC + hc
                    if k >= 4:
                        pk, hk = divmod(k - 4, HC)
                        WAIT(v, "sp", f"ln2sp_{pk}_{hk}")
                    v.tensor_tensor(out=v_lnb, in0=xr(c)[:, hc, :],
                                    in1=v_ab2[:, 0:TC], op=ALU.mult)
                    v.tensor_tensor(out=v_ln2bf[:, k % 4, :],
                                    in0=v_lnb, in1=v_ab2[:, TC:2 * TC],
                                    op=ALU.subtract)
                    i0 = v.nop()
                    INC(i0, "ve", f"ln2_{c}_{hc}")
                WAIT(v, "pe", f"raw_{c}")
                v.tensor_copy(out=raw_em[:, c * TC:(c + 1) * TC],
                              in_=pb[2][0:8, 0:TC])
                i0 = v.nop()
                INC(i0, "ve", f"rawev_{c}")
        WAIT(v, "pe", "gsum_mm")
        i0 = v.tensor_reduce(
            out=gsumr[0:1, :].unsqueeze(2),
            in_=pb[6][0:1, 0:HC * NE].rearrange("p (h e) -> p e h", e=NE),
            axis=AXX, op=ALU.add)
        i0 = v.nop()
        INC(i0, "ve", "gsumr")
        WAIT(v, "se", "gsumbc")
        for j in range(NSTRIP):
            WAIT(v, "pe", f"rawT_{j}")
            reg = (j // 2 % 2) * 8
            v.scalar_tensor_tensor(
                out=g_t1[:, j * 8:(j + 1) * 8], in0=gsumbc[:, :],
                scalar=mcols[:, j:j + 1], in1=pb[j % 2][:, reg:reg + 8],
                op0=ALU.mult, op1=ALU.subtract)
            i0 = v.nop()
            INC(i0, "ve", f"t1_{j}")
        for j in range(NSTRIP):
            v.tensor_scalar(out=g_t1[:, j * 8:(j + 1) * 8],
                            in0=g_t1[:, j * 8:(j + 1) * 8],
                            scalar1=iscols[:, j:j + 1], scalar2=-1.0,
                            op0=ALU.mult, op1=ALU.mult)
        v.drain()
        WAIT(v, "se", "lbbc")
        for j in range(NSTRIP):
            i0 = v.tensor_tensor(out=g_lg[:, j * 8:(j + 1) * 8],
                                 in0=g_t1[:, j * 8:(j + 1) * 8],
                                 in1=lbbc[:, :], op=ALU.add)
        i0 = v.nop()
        INC(i0, "ve", "logits")
        WAIT(v, "se", "scores")
        v.reduce_max(out=g_m1[:, :].unsqueeze(2),
                     in_=g_sc[:, :].rearrange("p (j e) -> p j e", e=8),
                     axis=AXX)
        v.drain()
        v.tensor_tensor(out=g_lg[:, :].rearrange("p (j e) -> p j e", e=8),
                        in0=g_sc[:, :].rearrange("p (j e) -> p j e", e=8),
                        in1=g_m1[:, :].unsqueeze(2).to_broadcast([128, NSTRIP, 8]),
                        op=ALU.is_ge)
        v.drain()
        v.scalar_tensor_tensor(out=g_lg[:, :], in0=g_lg[:, :], scalar=1e30,
                               in1=g_sc[:, :], op0=ALU.mult, op1=ALU.subtract)
        v.drain()
        v.tensor_reduce(out=g_m2[:, :].unsqueeze(2),
                        in_=g_lg[:, :].rearrange("p (j e) -> p j e", e=8),
                        axis=AXX, op=ALU.min, negate=True)
        v.drain()
        v.tensor_tensor(out=g_sel[:, :].rearrange("p (j e) -> p j e", e=8),
                        in0=g_sc[:, :].rearrange("p (j e) -> p j e", e=8),
                        in1=g_m2[:, :].unsqueeze(2).to_broadcast([128, NSTRIP, 8]),
                        op=ALU.is_ge)
        v.tensor_tensor(out=g_s12[:, :], in0=g_m1[:, :], in1=g_m2[:, :],
                        op=ALU.add)
        v.drain()
        v.tensor_scalar(out=g_s12[:, :], in0=g_s12[:, :], scalar1=1e-8,
                        scalar2=None, op0=ALU.add)
        v.drain()
        v.reciprocal(out=g_i12[:, :], in_=g_s12[:, :])
        v.tensor_tensor(out=g_wc[:, :], in0=g_sc[:, :], in1=g_sel[:, :],
                        op=ALU.mult)
        v.drain()
        i0 = v.tensor_tensor(
            out=g_wc[:, :].rearrange("p (j e) -> p j e", e=8),
            in0=g_wc[:, :].rearrange("p (j e) -> p j e", e=8),
            in1=g_i12[:, :].unsqueeze(2).to_broadcast([128, NSTRIP, 8]),
            op=ALU.mult)
        i0 = v.nop()
        INC(i0, "ve", "wc")

        EORDER_V = [NE, NE + 1] + list(range(NE))
        qv = [0]

        def hid_ops(p, si_list):
            for si in si_list:
                e = EORDER_V[si]
                for ic in range(ICH):
                    kc = e * ICH + ic
                    q = qv[0]
                    qv[0] += 1
                    WAIT(v, "se", f"silu_{p}_{e}_{ic}")
                    WAIT(v, "pe", f"u_{p}_{e}_{ic}")
                    gs = v_gsb[:, (q % 2) * TC:(q % 2 + 1) * TC]
                    if e < NE and p > 0:
                        WAIT(v, "pe", f"bselmm_{p}_{e}")
                        v.tensor_tensor(out=v_h1b, in0=gs,
                                        in1=pb[2 + q % 2][:, 0:TC],
                                        op=ALU.mult)
                        v.tensor_tensor(out=v_hid[:, kc, :], in0=v_h1b,
                                        in1=pb[4][:, 0:TC], op=ALU.mult)
                    else:
                        v.tensor_tensor(out=v_hid[:, kc, :], in0=gs,
                                        in1=pb[2 + q % 2][:, 0:TC],
                                        op=ALU.mult)
                    i0 = v.nop()
                    INC(i0, "ve", f"hid_{p}_{kc}")

        def hid_scale0():
            for e in range(NE):
                WAIT(v, "pe", f"bselmm_0_{e}")
                for ic in range(ICH):
                    kc = e * ICH + ic
                    v.tensor_tensor(out=v_hid[:, kc, :], in0=v_hid[:, kc, :],
                                    in1=pb[4][:, 0:TC], op=ALU.mult)
                i0 = v.nop()
                INC(i0, "ve", f"hsc_{e}")

        hid_ops(0, [0, 1])
        WAIT(v, "pe", "part_wc")
        v.tensor_reduce(
            out=ccin_sb[0:1, 0:8].unsqueeze(2),
            in_=pb[6][0:1, 0:NSTRIP * 8].rearrange("p (j e) -> p e j", e=8),
            axis=AXX, op=ALU.add)
        WAIT(v, "pe", "part_sel")
        v.tensor_reduce(
            out=ccin_sb[0:1, 8:16].unsqueeze(2),
            in_=pb[7][0:1, 0:NSTRIP * 8].rearrange("p (j e) -> p e j", e=8),
            axis=AXX, op=ALU.add)
        i0 = v.nop()
        INC(i0, "ve", "ccin")
        WAIT(v, "gw", "ccout")
        v.tensor_scalar_max(out=ccout_sb[0:1, 8:16], in0=ccout_sb[0:1, 8:16],
                            scalar1=1.0)
        v.drain()
        v.reciprocal(out=ccout_sb[0:1, 8:16], in_=ccout_sb[0:1, 8:16])
        v.drain()
        v.tensor_tensor(out=wmrow[:, :], in0=ccout_sb[0:1, 0:8],
                        in1=ccout_sb[0:1, 8:16], op=ALU.mult)
        i0 = v.nop()
        INC(i0, "ve", "wmrow")
        for p in range(NCH):
            if p >= 1:
                WAIT(v, "pe", f"dn_{p - 1}_{HC - 1}")
                hid_ops(p, range(2))
            hid_ops(p, range(2, NPE))
            if p == 0:
                WAIT(v, "pe", "wmcol_mm")
                i0 = v.tensor_copy(out=wmcol[:, :], in_=pb[6][0:8, 0:1])
                v.drain()
                INC(i0, "ve", "wmcol")
                for jg in range(NJG):
                    WAIT(v, "pe", f"selT_{jg}")
                    n_in_g = min(4, NSTRIP - jg * 4)
                    v.tensor_scalar(
                        out=s_em[:, jg * 512:jg * 512 + n_in_g * 128],
                        in0=pb[7][0:8, 0:n_in_g * 128],
                        scalar1=wmcol[:, :], scalar2=None, op0=ALU.mult)
                    i0 = v.nop()
                    INC(i0, "ve", f"sem_{jg}")
                i0 = v.nop()
                INC(i0, "ve", "semrows")
                hid_scale0()
            i0 = v.nop()
            INC(i0, "ve", f"hidend_{p}")
            WAIT(v, "sx", f"x1t_{p}")
            for hc in range(HC):
                WAIT(v, "pe", f"dn_{p}_{hc}")
                v.tensor_tensor(out=v_x1t[:, p % 2, hc, :],
                                in0=pb[5 + hc % 2][:, 0:TC],
                                in1=v_x1t[:, p % 2, hc, :], op=ALU.add)
                i0 = v.nop()
                INC(i0, "ve", f"evac_{p}_{hc}")
            i0 = v.nop()
            INC(i0, "ve", f"evacend_{p}")

    def run_all(engs):
        emit_gpsimd(engs["gpsimd"])
        emit_sync(engs["sync"])
        emit_scalar(engs["scalar"])
        emit_pe(engs["tensor"])
        emit_vector(engs["vector"])

    dummies = {k: _DUMMY_ENG for k in
               ["gpsimd", "sync", "scalar", "tensor", "vector"]}
    reset(False)
    run_all(dummies)
    led = dict(S["led"])
    nc._debug_led = led

    with nc.Block() as block:

        def mk(name):
            def f(eng):
                reset(True)
                S["led"] = led
                engs = dict(dummies)
                if cfg.get("paranoid") and name in ("vector", "scalar", "gpsimd"):
                    eng = _ParanoidEng(eng)
                engs[name] = eng
                run_all(engs)
            f.__name__ = f"_{name}"
            return f

        block.gpsimd(mk("gpsimd"))
        block.sync(mk("sync"))
        block.scalar(mk("scalar"))
        block.tensor(mk("tensor"))
        block.vector(mk("vector"))

    return nc



def _pm_proj(W, HC):
    O = W.shape[0]
    OC = O // 128
    WT = np.ascontiguousarray(W.T)
    return np.ascontiguousarray(
        WT.reshape(HC, 128, OC, 128).transpose(1, 2, 0, 3).reshape(128, -1)
    ).astype(np.float32)


def _pm_col(vec):
    n = vec.shape[0]
    return np.ascontiguousarray(vec.reshape(n // 128, 128).T).astype(np.float32)


def _pm_gu(M, HC, ICH):
    MT = np.ascontiguousarray(M.T)
    return MT.reshape(HC, 128, ICH, 128).transpose(1, 0, 2, 3).reshape(128, -1)


def host_inputs(cfg, inputs):
    HC, T, NE, ICH = cfg["HC"], cfg["T"], cfg["NE"], cfg["ICH"]
    NPE = NE + 2
    KC = NPE * ICH
    I = ICH * 128

    f = lambda k: np.asarray(inputs[k], dtype=np.float32)
    x = f("L0_attn_x")
    egw, euw, edw = (f("L0_ffn_expert_gate_w"), f("L0_ffn_expert_up_w"),
                     f("L0_ffn_expert_down_w"))
    sgw, suw, sdw = (f("L0_ffn_shared_gate_w"), f("L0_ffn_shared_up_w"),
                     f("L0_ffn_shared_down_w"))
    gw = f("L0_ffn_gate_w")

    shared = {
        "wv": _pm_proj(f("L0_attn_Wv"), HC),
        "wr": _pm_proj(f("L0_attn_Wr"), HC),
        "wo": _pm_proj(f("L0_attn_Wo"), HC),
        "td": _pm_col(f("L0_attn_time_decay")),
        "tmv": _pm_col(f("L0_attn_time_mix_v")),
        "tmr": _pm_col(f("L0_attn_time_mix_r")),
        "gatew": np.ascontiguousarray(
            gw.T.reshape(HC, 128, NE).transpose(1, 0, 2).reshape(128, -1)
        ).astype(np.float32),
        "lbrow": np.ascontiguousarray(
            f("L0_ffn_lb_bias").reshape(1, NE)).astype(np.float32),
    }
    gu = []
    for e in range(NE):
        gu.append(np.stack([_pm_gu(egw[e], HC, ICH), _pm_gu(euw[e], HC, ICH)],
                           axis=1).reshape(128, -1))
    gu.append(np.stack([_pm_gu(sgw[0:I], HC, ICH), _pm_gu(suw[0:I], HC, ICH)],
                       axis=1).reshape(128, -1))
    gu.append(np.stack([_pm_gu(sgw[I:2 * I], HC, ICH),
                        _pm_gu(suw[I:2 * I], HC, ICH)], axis=1).reshape(128, -1))
    shared["gwuw"] = np.ascontiguousarray(np.stack(gu, axis=0)).astype(np.float32)
    dwT = np.concatenate([edw[e].T for e in range(NE)] + [sdw.T], axis=0)
    shared["dwcat"] = np.ascontiguousarray(
        dwT.reshape(KC, 128, HC, 128).transpose(1, 2, 0, 3).reshape(128, -1)
    ).astype(np.float32)

    in_maps = []
    for b in range(cfg["ncores"]):
        m = dict(shared)
        m["x"] = np.ascontiguousarray(
            x[b].T.reshape(HC, 128, T).transpose(1, 0, 2).reshape(128, -1)
        ).astype(np.float32)
        in_maps.append(m)
    return in_maps


def host_output(cfg, results):
    HC, T = cfg["HC"], cfg["T"]
    H = 128 * HC
    outs = []
    for r in results:
        arr = np.asarray(r["out"]).reshape(128, HC, T).transpose(1, 0, 2)
        outs.append(arr.reshape(H, T).T)
    return np.ascontiguousarray(np.stack(outs, axis=0)).astype(np.float32)


def kernel(**inputs):
    cfg = FULL_CFG
    nc = build_program(cfg)
    in_maps = host_inputs(cfg, inputs)
    res = run_bass_kernel_spmd(nc, in_maps, core_ids=list(range(cfg["ncores"])))
    return host_output(cfg, res.results)


# version 11
# speedup vs baseline: 1.0616x; 1.0119x over previous
import numpy as np
from contextlib import ExitStack

import concourse.bass as bass
import concourse.mybir as mybir
from concourse.bass_utils import run_bass_kernel_spmd

F32 = mybir.dt.float32
BF16 = mybir.dt.bfloat16
I32 = mybir.dt.int32
ALU = mybir.AluOpType
AF = mybir.ActivationFunctionType
AXX = mybir.AxisListType.X

FULL_CFG = dict(ncores=8, HC=8, T=2048, TC=512, NE=8, ICH=4)


class _Dummy:
    def then_inc(self, *a, **k):
        return self


class _EngP:

    def __getattr__(self, k):
        return lambda *a, **kw: _Dummy()


_DUMMY_ENG = _EngP()


_COMPUTE_OPS = {
    "tensor_tensor", "tensor_scalar", "tensor_copy", "tensor_add",
    "tensor_sub", "tensor_mul", "tensor_max", "tensor_scalar_add",
    "tensor_scalar_mul", "tensor_scalar_max", "tensor_scalar_min",
    "tensor_single_scalar", "scalar_tensor_tensor", "tensor_tensor_scan",
    "tensor_reduce", "reduce_max", "reduce_sum", "reciprocal", "memset",
    "activation", "copy", "add", "mul", "iota", "select", "copy_predicated",
}


class _ParanoidEng:

    def __init__(self, eng):
        self._e = eng

    def __getattr__(self, k):
        a = getattr(self._e, k)
        if k in _COMPUTE_OPS:
            def wrapped(*args, **kw):
                self._e.drain()
                return a(*args, **kw)
            return wrapped
        return a


def build_program(cfg, dbg_out=False):
    ncores = cfg["ncores"]
    HC, T, TC, NE, ICH = (cfg["HC"], cfg["T"], cfg["TC"], cfg["NE"],
                          cfg["ICH"])
    NCH = T // TC
    NPE = NE + 2
    KC = NPE * ICH
    NSTRIP = T // 128
    SPT = TC // 128
    NJG = (NSTRIP + 3) // 4
    invH = 1.0 / (128 * HC)
    EPS = 1e-5

    nc = bass.Bass("TRN2", target_bir_lowering=False,
                   detect_race_conditions=bool(cfg.get("race", False)))

    d_x = nc.dram_tensor("x", [128, HC * T], F32, kind="ExternalInput")
    d_wv = nc.dram_tensor("wv", [128, HC * HC * 128], F32, kind="ExternalInput")
    d_wr = nc.dram_tensor("wr", [128, HC * HC * 128], F32, kind="ExternalInput")
    d_wo = nc.dram_tensor("wo", [128, HC * HC * 128], F32, kind="ExternalInput")
    d_td = nc.dram_tensor("td", [128, HC], F32, kind="ExternalInput")
    d_tmv = nc.dram_tensor("tmv", [128, HC], F32, kind="ExternalInput")
    d_tmr = nc.dram_tensor("tmr", [128, HC], F32, kind="ExternalInput")
    d_gatew = nc.dram_tensor("gatew", [128, HC * NE], F32, kind="ExternalInput")
    d_lbrow = nc.dram_tensor("lbrow", [1, NE], F32, kind="ExternalInput")
    d_gwuw = nc.dram_tensor("gwuw", [NPE, 128, 2 * HC * ICH * 128], F32,
                            kind="ExternalInput")
    d_dw = nc.dram_tensor("dwcat", [128, HC * KC * 128], F32,
                          kind="ExternalInput")
    d_out = nc.dram_tensor("out", [128, HC * T], F32, kind="ExternalOutput")
    d_gwuwb = nc.dram_tensor("gwuwb", [NPE, 128, 2 * HC * ICH * 128], BF16)
    d_dwb = nc.dram_tensor("dwb", [128, HC * KC * 128], BF16)
    dbg_kind = dict(kind="ExternalOutput") if dbg_out else {}
    d_x1 = nc.dram_tensor("x1d", [128, HC * T], F32, **dbg_kind)
    d_ln2 = nc.dram_tensor("ln2d", [128, HC * T], BF16, **dbg_kind)
    d_sem = nc.dram_tensor("semrows", [8, T], F32, **dbg_kind)
    d_ccin = nc.dram_tensor("ccin", [1, 16], F32)
    d_ccout = nc.dram_tensor("ccout", [1, 16], F32)

    st = ExitStack()
    nc._ctx_keepalive = st

    def sb(name, shape, dt):
        return st.enter_context(nc.sbuf_tensor(name, shape, dt))

    ones_bfc = sb("ones_bfc", [128, 1], BF16)
    ones_fc = sb("ones_fc", [128, 1], F32)
    ones_fr = sb("ones_fr", [1, 128], F32)
    i128f = sb("i128f", [128, 128], F32)
    i8f = sb("i8f", [8, 8], F32)
    itmp = sb("itmp", [128, 128], I32)
    dcol = sb("dcol", [128, HC], F32)
    dtmp = sb("dtmp", [128, HC], F32)
    tmvc = sb("tmvc", [128, HC], F32)
    tmrc = sb("tmrc", [128, HC], F32)
    gatew = sb("gatew_sb", [128, HC * NE], F32)
    lbrow = sb("lbrow_sb", [1, NE], F32)
    carry = sb("carry", [128, HC], F32)
    rawsem = sb("rawsem", [8, T], F32)
    raw_em = rawsem
    s_em = rawsem
    g_t1 = sb("g_t1", [128, NSTRIP * 8], F32)
    g_lg = sb("g_lg", [128, NSTRIP * 8], F32)
    g_sc = sb("g_sc", [128, NSTRIP * 8], F32)
    g_sel = sb("g_sel", [128, NSTRIP * 8], F32)
    g_wc = sb("g_wc", [128, NSTRIP * 8], F32)
    g_m1 = sb("g_m1", [128, NSTRIP], F32)
    g_m2 = sb("g_m2", [128, NSTRIP], F32)
    g_s12 = sb("g_s12", [128, NSTRIP], F32)
    g_i12 = sb("g_i12", [128, NSTRIP], F32)
    mcols = sb("mcols", [128, NSTRIP], F32)
    iscols = sb("iscols", [128, NSTRIP], F32)
    gsumr = sb("gsumr", [1, NE], F32)
    gsumbc = sb("gsumbc", [128, NE], F32)
    lbbc = sb("lbbc", [128, NE], F32)
    wmrow = sb("wmrow", [1, NE], F32)
    wmcol = sb("wmcol", [8, 1], F32)
    ccin_sb = sb("ccin_sb", [1, 16], F32)
    ccout_sb = sb("ccout_sb", [1, 16], F32)
    srow = sb("srow", [1, 2 * TC], F32)

    LAY = {}
    off = [0]

    def lay(name, nbytes):
        a = off[0]
        off[0] = (a + nbytes + 31) & ~31
        LAY[name] = a

    lay("xres", 3 * HC * TC * 4)
    lay("wv", HC * HC * 128 * 2)
    lay("wr", HC * HC * 128 * 2)
    lay("wo", HC * HC * 128 * 2)
    lay("vin", 2 * HC * TC * 2)
    lay("rin", 2 * HC * TC * 2)
    lay("v_c", HC * TC * 2)
    lay("r_c", HC * TC * 2)
    lay("lnb", TC * 4)
    lay("diffb", TC * 4)
    lay("zb", (TC + 1) * 4)
    lay("cast1", 2 * 2 * TC * 2)
    lay("cast2", 2 * 2 * TC * 2)
    lay("ln2bf", 4 * TC * 2)
    lay("ab1", 2 * TC * 4)
    lay("ab2", 2 * TC * 4)
    lay("r_mean", TC * 4)
    lay("r_e2", TC * 4)
    lay("r_msq", TC * 4)
    lay("r_is", TC * 4)
    lay("r_b", TC * 4)
    lay("r2_mean", TC * 4)
    lay("r2_is", TC * 4)
    lay("r2_b", TC * 4)
    attn_end = off[0]
    off[0] = 0
    lay("x1t", 2 * HC * TC * 4)
    lay("ln2t", 2 * HC * TC * 2)
    assert off[0] <= LAY["wv"], "x1t/ln2t must stay inside xres"
    off[0] = LAY["wv"]
    lay("gwuws", 2 * 2 * HC * ICH * 128 * 2)
    lay("dws", 2 * KC * 128 * 2)
    lay("hid", KC * TC * 2)
    lay("gsb", 2 * TC * 4)
    lay("h1b", TC * 4)
    ffn_end = off[0]
    ARENA = max(attn_end, ffn_end)
    arena = sb("arena", [128, ARENA], mybir.dt.uint8)
    assert LAY["gwuws"] == LAY["wv"]

    def view(name, nbytes, dt):
        a = LAY[name]
        return arena[:, a:a + nbytes].bitcast(dt)

    v_xres = view("xres", 3 * HC * TC * 4, F32).rearrange(
        "p (s h t) -> p s h t", s=3, h=HC)
    v_wv = view("wv", HC * HC * 128 * 2, BF16)
    v_wr = view("wr", HC * HC * 128 * 2, BF16)
    v_wo = view("wo", HC * HC * 128 * 2, BF16)
    v_vin = view("vin", 2 * HC * TC * 2, BF16).rearrange(
        "p (s h t) -> p s h t", s=2, h=HC)
    v_rin = view("rin", 2 * HC * TC * 2, BF16).rearrange(
        "p (s h t) -> p s h t", s=2, h=HC)
    v_vc = view("v_c", HC * TC * 2, BF16).rearrange("p (h t) -> p h t", h=HC)
    v_rc = view("r_c", HC * TC * 2, BF16).rearrange("p (h t) -> p h t", h=HC)
    v_lnb = view("lnb", TC * 4, F32)
    v_diff = view("diffb", TC * 4, F32)
    v_zb = view("zb", (TC + 1) * 4, F32)
    v_cast1 = view("cast1", 2 * 2 * TC * 2, BF16).rearrange(
        "p (s w t) -> p s w t", s=2, w=2)
    v_cast2 = view("cast2", 2 * 2 * TC * 2, BF16).rearrange(
        "p (s w t) -> p s w t", s=2, w=2)
    v_ln2bf = view("ln2bf", 4 * TC * 2, BF16).rearrange(
        "p (s t) -> p s t", s=4)
    v_ab1 = view("ab1", 2 * TC * 4, F32)
    v_ab2 = view("ab2", 2 * TC * 4, F32)
    v_rmean = view("r_mean", TC * 4, F32)
    v_re2 = view("r_e2", TC * 4, F32)
    v_rmsq = view("r_msq", TC * 4, F32)
    v_ris = view("r_is", TC * 4, F32)
    v_rb = view("r_b", TC * 4, F32)
    v_r2mean = view("r2_mean", TC * 4, F32)
    v_r2is = view("r2_is", TC * 4, F32)
    v_r2b = view("r2_b", TC * 4, F32)
    v_x1t = view("x1t", 2 * HC * TC * 4, F32).rearrange(
        "p (s h t) -> p s h t", s=2, h=HC)
    v_ln2t = view("ln2t", 2 * HC * TC * 2, BF16).rearrange(
        "p (s h t) -> p s h t", s=2, h=HC)
    v_gwuw = view("gwuws", 2 * 2 * HC * ICH * 128 * 2, BF16).rearrange(
        "p (s g h i j) -> p s g h i j", s=2, g=2, h=HC, i=ICH)
    v_dws = view("dws", 2 * KC * 128 * 2, BF16).rearrange(
        "p (s k j) -> p s k j", s=2, k=KC)
    v_hid = view("hid", KC * TC * 2, BF16).rearrange("p (k t) -> p k t", k=KC)
    v_gsb = view("gsb", 2 * TC * 4, F32)
    v_h1b = view("h1b", TC * 4, F32)

    def xr(c):
        return v_xres[:, c % 3, :, :]

    def wvw(w, oc, hc):
        o = (oc * HC + hc) * 128
        return w[:, o:o + 128]

    pb = [st.enter_context(nc.psum_tensor(f"pb{i}", [128, 512], F32))
          for i in range(8)]

    sem_names = ["ge", "cc", "pe", "ve", "se",
                 "sm", "wat", "cst", "gwa", "gwb", "dwa", "dwb", "ccb",
                 "sx0", "sx1", "sx2", "x1s0", "x1s1", "x1s2", "x1s3",
                 "l2a", "l2b", "l2c", "l2d", "smr", "xta", "xtb", "lta", "ltb",
                 "sra", "srb", "out"]
    SEM = {n: st.enter_context(nc.semaphore(n)) for n in sem_names}

    def dsem(key):
        if key == "smalls":
            return "sm"
        if key in ("wv", "wr", "wo"):
            return "wat"
        if key == "dwcast" or key.startswith("gwcast"):
            return "cst"
        if key.startswith("gws_"):
            _, p, e = key.split("_")
            eord = [NE, NE + 1] + list(range(NE))
            si = eord.index(int(e))
            return "gwa" if (int(p) * NPE + si) % 2 == 0 else "gwb"
        if key.startswith("dws_"):
            _, p, hc = key.split("_")
            return "dwa" if (int(p) * HC + int(hc)) % 2 == 0 else "dwb"
        if key in ("ccin_d", "ccout"):
            return "ccb"
        if key.startswith("x_"):
            return f"sx{int(key[2:]) % 3}"
        if key.startswith("x1sp_"):
            return f"x1s{int(key[5:])}"
        if key.startswith("ln2sp_"):
            _, c, hc = key.split("_")
            return ("l2a", "l2b", "l2c", "l2d")[(int(c) * HC + int(hc)) % 4]
        if key == "semrows_d":
            return "smr"
        if key.startswith("x1t_"):
            return "xta" if int(key[4:]) % 2 == 0 else "xtb"
        if key.startswith("ln2t_"):
            return "lta" if int(key[5:]) % 2 == 0 else "ltb"
        if key.startswith("srow_"):
            _, p, e = key.split("_")
            return "sra" if (int(p) * NE + int(e)) % 2 == 0 else "srb"
        if key.startswith("out_"):
            return "out"
        raise KeyError(key)

    S = {}

    def reset(emit):
        S["emit"] = emit
        S["cnt"] = {n: 0 for n in sem_names}
        S["pe_rd"] = {}
        if not emit:
            S["led"] = {}

    def INC(inst, sem, key=None, n=1):
        if sem in ("gw", "sx", "sp"):
            sem = dsem(key)
            n = 16
        S["cnt"][sem] += n
        if key is not None and not S["emit"]:
            S["led"][(sem, key)] = S["cnt"][sem]
        if S["emit"] and inst is not None and not isinstance(inst, _Dummy):
            if sem in ("se", "ve") and S.get("cur") is not None and cfg.get("dr", 1):
                S["cur"].drain()
                inst = S["cur"].nop()
            inst.then_inc(SEM[sem], n)

    def ALIAS(sem, key, src):
        if not S["emit"]:
            S["led"][(sem, key)] = S["led"][(sem, src)]

    def WAIT(eng, sem, key):
        if sem in ("gw", "sx", "sp"):
            sem = dsem(key)
        if S["emit"]:
            v = S["led"].get((sem, key))
            assert v is not None, f"missing ledger {(sem, key)}"
            eng.wait_ge(SEM[sem], v)

    def bank_w(eng, bi, reader):
        prev = S["pe_rd"].get(bi)
        if prev is not None:
            WAIT(eng, *prev)
        S["pe_rd"][bi] = reader

    def emit_gpsimd(g):
        S["cur"] = g if not isinstance(g, _EngP) else None
        for dst, src in ((dtmp, d_td), (tmvc, d_tmv), (tmrc, d_tmr),
                         (gatew, d_gatew)):
            i0 = g.dma_start(out=dst[:, :], in_=src[:, :])
            INC(i0, "gw", "smalls")
        i0 = g.dma_start(out=lbrow[:, :], in_=d_lbrow[:, :])
        INC(i0, "gw", "smalls")
        g.iota(itmp[:, :], pattern=[[1, 128]], base=0, channel_multiplier=-1)
        g.drain()
        g.tensor_scalar(out=i128f[:, :], in0=itmp[:, :], scalar1=0.0,
                        scalar2=None, op0=ALU.is_equal)
        g.iota(itmp[0:8, 0:8], pattern=[[1, 8]], base=0, channel_multiplier=-1)
        g.drain()
        i0 = g.tensor_scalar(out=i8f[:, :], in0=itmp[0:8, 0:8], scalar1=0.0,
                             scalar2=None, op0=ALU.is_equal)
        g.drain()
        INC(i0, "ge", "ident")
        i0 = g.dma_start(out=v_wv, in_=d_wv[:, :])
        INC(i0, "gw", "wv")
        i0 = g.dma_start(out=v_wr, in_=d_wr[:, :])
        INC(i0, "gw", "wr")
        i0 = g.dma_start(out=v_wo, in_=d_wo[:, :])
        INC(i0, "gw", "wo")
        pace = {2: 0, 5: 1, 8: 2}
        for j, e in enumerate([NE, NE + 1] + list(range(NE))):
            if j in pace:
                WAIT(g, "sp", f"x1sp_{min(pace[j], NCH - 1)}")
            i0 = g.dma_start(out=d_gwuwb[e, :, :], in_=d_gwuw[e, :, :])
            INC(i0, "gw", f"gwcast{e}")
        i0 = g.dma_start(out=d_dwb[:, :], in_=d_dw[:, :])
        INC(i0, "gw", "dwcast")
        EORDER_G = [NE, NE + 1] + list(range(NE))
        for s0 in range(min(2, NPE)):
            e0 = EORDER_G[s0]
            WAIT(g, "pe", f"wr_{NCH - 1}_{HC - 1}")
            WAIT(g, "gw", f"gwcast{e0}")
            i0 = g.dma_start(out=v_gwuw[:, s0 % 2, :, :, :, :],
                             in_=d_gwuwb[e0, :, :])
            INC(i0, "gw", f"gws_0_{e0}")
        WAIT(g, "gw", "ccin_d")
        i0 = g.collective_compute(
            "AllReduce", ALU.add, replica_groups=[list(range(ncores))],
            ins=[d_ccin.ap().opt()], outs=[d_ccout.ap().opt()])
        INC(i0, "cc", "ccdone")
        for p in range(NCH):
            for si, e in enumerate(EORDER_G):
                k = p * NPE + si
                if k < 2:
                    continue
                pk, sk = divmod(k - 2, NPE)
                WAIT(g, "pe", f"guexp_{pk}_{sk}")
                WAIT(g, "gw", f"gwcast{e}")
                i0 = g.dma_start(out=v_gwuw[:, k % 2, :, :, :, :],
                                 in_=d_gwuwb[e, :, :])
                INC(i0, "gw", f"gws_{p}_{e}")
            for hc in range(HC):
                k = p * HC + hc
                if k == 0:
                    WAIT(g, "pe", f"wo_{NCH - 1}_{HC - 1}")
                    WAIT(g, "gw", "dwcast")
                if k >= 2:
                    pk, hk = divmod(k - 2, HC)
                    WAIT(g, "pe", f"dn_{pk}_{hk}")
                i0 = g.dma_start(out=v_dws[:, k % 2, :, :],
                                 in_=d_dwb[:, hc * KC * 128:(hc + 1) * KC * 128])
                INC(i0, "gw", f"dws_{p}_{hc}")

    def emit_sync(sy):
        S["cur"] = None
        for c in range(min(3, NCH)):
            i0 = sy.dma_start(
                out=xr(c)[:, :, :],
                in_=d_x[:, :].rearrange("p (h t) -> p h t", h=HC)[:, :, c * TC:(c + 1) * TC])
            INC(i0, "sx", f"x_{c}")
        for i in range(NCH + 2):
            if 1 <= i <= NCH:
                c = i - 1
                WAIT(sy, "ve", f"epi_{c}")
                i0 = sy.dma_start(
                    out=d_x1[:, :].rearrange("p (h t) -> p h t", h=HC)[:, :, c * TC:(c + 1) * TC],
                    in_=xr(c)[:, :, :])
                INC(i0, "sp", f"x1sp_{c}")
            if 2 <= i <= NCH + 1:
                c = i - 2
                for hc in range(HC):
                    WAIT(sy, "ve", f"ln2_{c}_{hc}")
                    i0 = sy.dma_start(
                        out=d_ln2[:, :].rearrange("p (h t) -> p h t", h=HC)[:, hc, c * TC:(c + 1) * TC],
                        in_=v_ln2bf[:, (c * HC + hc) % 4, :])
                    INC(i0, "sp", f"ln2sp_{c}_{hc}")
                if i == 4 and NCH >= 3:
                    WAIT(sy, "sp", "x1sp_2")
                    WAIT(sy, "pe", "raw_2")
                    for kk in range(max(0, HC - 4), HC):
                        WAIT(sy, "sp", f"ln2sp_0_{kk}")
                    i0 = sy.dma_start(
                        out=v_ln2t[:, 0, :, :],
                        in_=d_ln2[:, :].rearrange("p (h t) -> p h t", h=HC)[:, :, 0:TC])
                    INC(i0, "sx", "ln2t_0")
            c = i + 1
            if 3 <= c < NCH:
                WAIT(sy, "pe", f"raw_{c - 3}")
                WAIT(sy, "ve", f"ln2_{c - 3}_{HC - 1}")
                i0 = sy.dma_start(
                    out=xr(c)[:, :, :],
                    in_=d_x[:, :].rearrange("p (h t) -> p h t", h=HC)[:, :, c * TC:(c + 1) * TC])
                INC(i0, "sx", f"x_{c}")
        def rd_pass(p):
            WAIT(sy, "sp", f"x1sp_{p}")
            if p == 0:
                WAIT(sy, "pe", f"raw_{NCH - 1}")
                WAIT(sy, "ve", f"ln2_{NCH - 1}_{HC - 1}")
                for pp in range(NCH):
                    WAIT(sy, "sp", f"x1sp_{pp}")
            if p >= 2:
                WAIT(sy, "ve", f"evacend_{p - 2}")
            i0 = sy.dma_start(
                out=v_x1t[:, p % 2, :, :],
                in_=d_x1[:, :].rearrange("p (h t) -> p h t", h=HC)[:, :, p * TC:(p + 1) * TC])
            INC(i0, "sx", f"x1t_{p}")
            if p == 0 and NCH >= 3:
                return
            k_last = p * HC + HC - 1
            for kk in range(max(0, k_last - 3), k_last + 1):
                WAIT(sy, "sp", f"ln2sp_{kk // HC}_{kk % HC}")
            if p >= 2:
                WAIT(sy, "pe", f"guend_{p - 2}")
            i0 = sy.dma_start(
                out=v_ln2t[:, p % 2, :, :],
                in_=d_ln2[:, :].rearrange("p (h t) -> p h t", h=HC)[:, :, p * TC:(p + 1) * TC])
            INC(i0, "sx", f"ln2t_{p}")

        def rd_srows(p):
            for e in range(NE):
                k = p * NE + e
                if k == 0:
                    WAIT(sy, "sp", "semrows_d")
                if k >= 2:
                    pk, ek = divmod(k - 2, NE)
                    WAIT(sy, "pe", f"bselmm_{pk}_{ek}")
                i0 = sy.dma_start(out=srow[0:1, (k % 2) * TC:(k % 2 + 1) * TC],
                                  in_=d_sem[e:e + 1, p * TC:(p + 1) * TC])
                INC(i0, "sx", f"srow_{p}_{e}")

        rd_pass(0)
        rd_pass(1)
        WAIT(sy, "ve", "ccin")
        i0 = sy.dma_start(out=d_ccin[:, :], in_=ccin_sb[:, :])
        INC(i0, "gw", "ccin_d")
        WAIT(sy, "cc", "ccdone")
        i0 = sy.dma_start(out=ccout_sb[:, :], in_=d_ccout[:, :])
        INC(i0, "gw", "ccout")
        WAIT(sy, "ve", "semrows")
        i0 = sy.dma_start(out=d_sem[:, :], in_=s_em[:, :])
        INC(i0, "sp", "semrows_d")
        rd_srows(0)
        for p in range(NCH):
            if p + 1 < NCH:
                rd_srows(p + 1)
            for hc in range(HC):
                WAIT(sy, "ve", f"evac_{p}_{hc}")
                i0 = sy.dma_start(
                    out=d_out[:, :].rearrange("p (h t) -> p h t", h=HC)[:, hc, p * TC:(p + 1) * TC],
                    in_=v_x1t[:, p % 2, hc, :])
                INC(i0, "sp", f"out_{p}_{hc}")
            if p + 2 < NCH:
                rd_pass(p + 2)
        WAIT(sy, "sp", f"out_{NCH - 1}_{HC - 1}")

    def emit_scalar(s):
        S["cur"] = s if not isinstance(s, _EngP) else None
        WAIT(s, "gw", "smalls")
        s.activation(out=dtmp[:, :], in_=dtmp[:, :], func=AF.Exp)
        s.drain()
        i0 = s.activation(out=dcol[:, :], in_=dtmp[:, :], func=AF.Exp,
                          scale=-1.0)
        s.drain()
        INC(i0, "se", "dcol")

        def casts(c, cviews, tag):
            if tag == "1":
                WAIT(s, "sx", f"x_{c}")
            for hc in range(HC):
                k = c * HC + hc
                if tag == "2":
                    WAIT(s, "ve", f"epioc_{c}_{hc}")
                if k >= 2:
                    pk, hk = divmod(k - 2, HC)
                    WAIT(s, "pe", f"st{tag}mm_{pk}_{hk}")
                s.copy(out=cviews[:, hc % 2, 0, :], in_=xr(c)[:, hc, :])
                i0 = s.activation(out=cviews[:, hc % 2, 1, :],
                                  in_=xr(c)[:, hc, :], func=AF.Square)
                if hc % 2 == 1 or hc == HC - 1:
                    INC(i0, "se", f"cast{tag}_{c}_{hc}")
                    if hc % 2 == 1:
                        ALIAS("se", f"cast{tag}_{c}_{hc - 1}", f"cast{tag}_{c}_{hc}")

        for i in range(NCH + 2):
            if i < NCH:
                casts(i, v_cast1, "1")
            if 2 <= i <= NCH + 1:
                c = i - 2
                WAIT(s, "ve", f"vrow2_{c}")
                i0 = s.activation(out=v_rmsq[0:1, :], in_=v_re2[0:1, :], func=AF.Sqrt)
                INC(i0, "se", f"sqrt2_{c}")
            if i < NCH:
                WAIT(s, "ve", f"vrow1_{i}")
                i0 = s.activation(out=v_rmsq[0:1, :], in_=v_re2[0:1, :], func=AF.Sqrt)
                INC(i0, "se", f"sqrt1_{i}")
            if 1 <= i <= NCH:
                c = i - 1
                for oc in range(HC):
                    WAIT(s, "pe", f"wv_{c}_{oc}")
                    i0 = s.copy(out=v_vc[:, oc, :], in_=pb[4 + oc % 2][:, 0:TC])
                    if oc % 2 == 1 or oc == HC - 1:
                        INC(i0, "se", f"evv_{c}_{oc}")
                        if oc % 2 == 1:
                            ALIAS("se", f"evv_{c}_{oc - 1}", f"evv_{c}_{oc}")
                    else:
                        pass
            if i < NCH:
                c = i
                if c >= 1:
                    WAIT(s, "ve", f"mix_{c - 1}")
                WAIT(s, "pe", f"bc1a_{c}")
                s.copy(out=v_ab1[:, 0:TC], in_=pb[2][:, 0:TC])
                WAIT(s, "pe", f"bc1_{c}")
                i0 = s.copy(out=v_ab1[:, TC:2 * TC], in_=pb[3][:, 0:TC])
                INC(i0, "se", f"ab1_{c}")
            if 2 <= i <= NCH + 1:
                c = i - 2
                if c >= 1:
                    WAIT(s, "ve", f"ln2_{c - 1}_{HC - 1}")
                WAIT(s, "pe", f"bc2a_{c}")
                s.copy(out=v_ab2[:, 0:TC], in_=pb[2][:, 0:TC])
                WAIT(s, "pe", f"bc2_{c}")
                i0 = s.copy(out=v_ab2[:, TC:2 * TC], in_=pb[3][:, 0:TC])
                INC(i0, "se", f"ab2_{c}")
            if 2 <= i <= NCH + 1:
                c = i - 2
                WAIT(s, "pe", f"mcolT_{c}")
                s.copy(out=mcols[:, c * SPT:(c + 1) * SPT],
                       in_=pb[4][:, 0:SPT])
                i0 = s.copy(out=iscols[:, c * SPT:(c + 1) * SPT],
                            in_=pb[5][:, 0:SPT])
                INC(i0, "se", f"mcev_{c}")
            if 1 <= i <= NCH:
                c = i - 1
                for oc in range(HC):
                    WAIT(s, "pe", f"wr_{c}_{oc}")
                    i0 = s.activation(out=v_rc[:, oc, :],
                                      in_=pb[6 + oc % 2][:, 0:TC],
                                      func=AF.Sigmoid)
                    if oc % 2 == 1 or oc == HC - 1:
                        INC(i0, "se", f"sig_{c}_{oc}")
                        if oc % 2 == 1:
                            ALIAS("se", f"sig_{c}_{oc - 1}", f"sig_{c}_{oc}")
                    else:
                        pass
            if 1 <= i <= NCH:
                casts(i - 1, v_cast2, "2")
        WAIT(s, "pe", "gsumbc_mm")
        i0 = s.copy(out=gsumbc[:, :], in_=pb[6][:, 0:NE])
        INC(i0, "se", "gsumbc")
        WAIT(s, "pe", "lbbc_mm")
        i0 = s.copy(out=lbbc[:, :], in_=pb[7][:, 0:NE])
        INC(i0, "se", "lbbc")
        WAIT(s, "ve", "logits")
        i0 = s.activation(out=g_sc[:, :], in_=g_lg[:, :], func=AF.Sigmoid)
        INC(i0, "se", "scores")
        EORDER_S = [NE, NE + 1] + list(range(NE))
        sq_keys = []
        for p in range(NCH):
            for si, e in enumerate(EORDER_S):
                for ic in range(ICH):
                    kc = e * ICH + ic
                    q = len(sq_keys)
                    sq_keys.append((p, kc))
                    if q == 0:
                        k_last = NCH * HC - 1
                        for kk in range(max(0, k_last - 3), k_last + 1):
                            WAIT(s, "sp", f"ln2sp_{kk // HC}_{kk % HC}")
                    if q >= 2:
                        pk, kk = sq_keys[q - 2]
                        WAIT(s, "ve", f"hid_{pk}_{kk}")
                    WAIT(s, "pe", f"g_{p}_{e}_{ic}")
                    i0 = s.activation(out=v_gsb[:, (q % 2) * TC:(q % 2 + 1) * TC],
                                      in_=pb[q % 2][:, 0:TC], func=AF.Silu)
                    INC(i0, "se", f"silu_{p}_{e}_{ic}")

    def emit_pe(t):
        S["cur"] = t if not isinstance(t, _EngP) else None
        WAIT(t, "ve", "ones")
        def stats_mm(c, tag, cviews, blo, bhi):
            for hc in range(HC):
                WAIT(t, "se", f"cast{tag}_{c}_{hc}")
                if hc == 0:
                    bank_w(t, blo, ("ve", f"stread{tag}_{c}"))
                    bank_w(t, bhi, ("ve", f"stread{tag}_{c}"))
                t.matmul(pb[blo][0:1, 0:TC], ones_bfc[:, :],
                         cviews[:, hc % 2, 0, :],
                         start=(hc == 0), stop=(hc == HC - 1))
                i0 = t.matmul(pb[bhi][0:1, 0:TC], ones_bfc[:, :],
                              cviews[:, hc % 2, 1, :],
                              start=(hc == 0), stop=(hc == HC - 1))
                INC(i0, "pe", f"st{tag}mm_{c}_{hc}")
            ALIAS("pe", f"st{tag}_{c}", f"st{tag}mm_{c}_{HC - 1}")

        for i in range(NCH + 2):
            if i < NCH:
                stats_mm(i, "1", v_cast1, 0, 1)
            if 1 <= i <= NCH:
                c = i - 1
                WAIT(t, "ve", f"mix_{c}")
                if c == 0:
                    WAIT(t, "gw", "wo")
                for oc in range(HC):
                    bank_w(t, 4 + oc % 2, ("se", f"evv_{c}_{oc}"))
                    for hc in range(HC):
                        i0 = t.matmul(pb[4 + oc % 2][:, 0:TC],
                                      wvw(v_wv, oc, hc), v_vin[:, c % 2, hc, :],
                                      start=(hc == 0), stop=(hc == HC - 1))
                    INC(i0, "pe", f"wv_{c}_{oc}")
            if i < NCH:
                c = i
                WAIT(t, "ve", f"rows1_{c}")
                bank_w(t, 2, ("se", f"ab1_{c}"))
                i0 = t.matmul(pb[2][:, 0:TC], ones_fr[:, :], v_ris[0:1, :],
                              start=True, stop=True)
                INC(i0, "pe", f"bc1a_{c}")
                bank_w(t, 3, ("se", f"ab1_{c}"))
                i0 = t.matmul(pb[3][:, 0:TC], ones_fr[:, :], v_rb[0:1, :],
                              start=True, stop=True)
                INC(i0, "pe", f"bc1_{c}")
            if 2 <= i <= NCH + 1:
                c = i - 2
                WAIT(t, "ve", f"rows2_{c}")
                bank_w(t, 2, ("se", f"ab2_{c}"))
                i0 = t.matmul(pb[2][:, 0:TC], ones_fr[:, :], v_r2is[0:1, :],
                              start=True, stop=True)
                INC(i0, "pe", f"bc2a_{c}")
                bank_w(t, 3, ("se", f"ab2_{c}"))
                i0 = t.matmul(pb[3][:, 0:TC], ones_fr[:, :], v_r2b[0:1, :],
                              start=True, stop=True)
                INC(i0, "pe", f"bc2_{c}")
                bank_w(t, 4, ("se", f"mcev_{c}"))
                for sp in range(SPT):
                    i0 = t.matmul(pb[4][:, sp:sp + 1],
                                  v_r2mean[0:1, sp * 128:(sp + 1) * 128],
                                  ones_fr[0:1, 0:1], start=True, stop=True)
                INC(i0, "pe", f"mcolT_m_{c}")
                bank_w(t, 5, ("se", f"mcev_{c}"))
                for sp in range(SPT):
                    i0 = t.matmul(pb[5][:, sp:sp + 1],
                                  v_r2is[0:1, sp * 128:(sp + 1) * 128],
                                  ones_fr[0:1, 0:1], start=True, stop=True)
                INC(i0, "pe", f"mcolT_{c}")
                WAIT(t, "se", f"ab2_{c}")
                bank_w(t, 2, ("ve", f"rawev_{c}"))
                for hc in range(HC):
                    i0 = t.matmul(pb[2][0:8, 0:TC],
                                  gatew[:, hc * NE:(hc + 1) * NE],
                                  xr(c)[:, hc, :],
                                  start=(hc == 0), stop=(hc == HC - 1))
                INC(i0, "pe", f"raw_{c}")
            if 1 <= i <= NCH:
                c = i - 1
                for oc in range(HC):
                    bank_w(t, 6 + oc % 2, ("se", f"sig_{c}_{oc}"))
                    for hc in range(HC):
                        i0 = t.matmul(pb[6 + oc % 2][:, 0:TC],
                                      wvw(v_wr, oc, hc), v_rin[:, c % 2, hc, :],
                                      start=(hc == 0), stop=(hc == HC - 1))
                    INC(i0, "pe", f"wr_{c}_{oc}")
            if 1 <= i <= NCH:
                c = i - 1
                WAIT(t, "se", f"evv_{c}_{HC - 1}")
                for oc in range(HC):
                    bank_w(t, 6 + oc % 2, ("ve", f"epioc_{c}_{oc}"))
                    for vc in range(HC):
                        i0 = t.matmul(pb[6 + oc % 2][:, 0:TC],
                                      wvw(v_wo, oc, vc), v_vc[:, vc, :],
                                      start=(vc == 0), stop=(vc == HC - 1))
                    INC(i0, "pe", f"wo_{c}_{oc}")
            if 1 <= i <= NCH:
                stats_mm(i - 1, "2", v_cast2, 4, 5)
        WAIT(t, "ge", "ident")
        bank_w(t, 6, ("ve", "gsumr"))
        i0 = t.matmul(pb[6][0:1, 0:HC * NE], ones_fc[:, :], gatew[:, :],
                      start=True, stop=True)
        INC(i0, "pe", "gsum_mm")
        WAIT(t, "ve", "gsumr")
        bank_w(t, 6, ("se", "gsumbc"))
        i0 = t.matmul(pb[6][:, 0:NE], ones_fr[:, :], gsumr[:, :],
                      start=True, stop=True)
        INC(i0, "pe", "gsumbc_mm")
        bank_w(t, 7, ("se", "lbbc"))
        WAIT(t, "gw", "smalls")
        i0 = t.matmul(pb[7][:, 0:NE], ones_fr[:, :], lbrow[:, :],
                      start=True, stop=True)
        INC(i0, "pe", "lbbc_mm")
        for j in range(NSTRIP):
            bi = j % 2
            bank_w(t, bi, ("ve", f"t1_{j}"))
            i0 = t.matmul(pb[bi][:, (j // 2 % 2) * 8:(j // 2 % 2) * 8 + 8],
                          raw_em[0:8, j * 128:(j + 1) * 128], i8f[:, :],
                          start=True, stop=True)
            INC(i0, "pe", f"rawT_{j}")
        def gate_tail_a():
            WAIT(t, "ve", "wc")
            bank_w(t, 6, ("ve", "ccin"))
            i0 = t.matmul(pb[6][0:1, 0:NSTRIP * 8], ones_fc[:, :], g_wc[:, :],
                          start=True, stop=True)
            INC(i0, "pe", "part_wc")
            bank_w(t, 7, ("ve", "ccin"))
            i0 = t.matmul(pb[7][0:1, 0:NSTRIP * 8], ones_fc[:, :], g_sel[:, :],
                          start=True, stop=True)
            INC(i0, "pe", "part_sel")

        def gate_tail_b():
            WAIT(t, "ve", "wmrow")
            bank_w(t, 6, ("ve", "wmcol"))
            i0 = t.matmul(pb[6][0:8, 0:1], wmrow[:, :], ones_fr[0:1, 0:1],
                          start=True, stop=True)
            INC(i0, "pe", "wmcol_mm")
            for jg in range(NJG):
                bank_w(t, 7, ("ve", f"sem_{jg}"))
                n_in_g = min(4, NSTRIP - jg * 4)
                for jj in range(n_in_g):
                    j = jg * 4 + jj
                    i0 = t.matmul(pb[7][0:8, jj * 128:(jj + 1) * 128],
                                  g_sel[:, :].rearrange("p (j e) -> p j e", e=8)[:, j, :],
                                  i128f[:, :], start=True, stop=True)
                INC(i0, "pe", f"selT_{jg}")

        EORDER = [NE, NE + 1] + list(range(NE))

        def bselmms(p, es):
            for e in es:
                WAIT(t, "sx", f"srow_{p}_{e}")
                if p == 0:
                    bank_w(t, 4, ("ve", f"hsc_{e}"))
                else:
                    bank_w(t, 4, ("ve", f"hid_{p}_{e * ICH + ICH - 1}"))
                k = p * NE + e
                i0 = t.matmul(pb[4][:, 0:TC], ones_fr[:, :],
                              srow[0:1, (k % 2) * TC:(k % 2 + 1) * TC],
                              start=True, stop=True)
                INC(i0, "pe", f"bselmm_{p}_{e}")

        for p in range(NCH):
            WAIT(t, "sx", f"ln2t_{p}")
            for si, e in enumerate(EORDER):
                if p == 0 and si == 2:
                    gate_tail_a()
                if p > 0 and e < NE:
                    bselmms(p, [e])
                WAIT(t, "gw", f"gws_{p}_{e}")
                slot = (p * NPE + si) % 2
                for ic in range(ICH):
                    kc = e * ICH + ic
                    sq = si * ICH + ic
                    bank_w(t, sq % 2, ("se", f"silu_{p}_{e}_{ic}"))
                    for hc in range(HC):
                        i0 = t.matmul(pb[sq % 2][:, 0:TC],
                                      v_gwuw[:, slot, 0, hc, ic, :],
                                      v_ln2t[:, p % 2, hc, :],
                                      start=(hc == 0), stop=(hc == HC - 1))
                    INC(i0, "pe", f"g_{p}_{e}_{ic}")
                    bank_w(t, 2 + sq % 2, ("ve", f"hid_{p}_{kc}"))
                    for hc in range(HC):
                        i0 = t.matmul(pb[2 + sq % 2][:, 0:TC],
                                      v_gwuw[:, slot, 1, hc, ic, :],
                                      v_ln2t[:, p % 2, hc, :],
                                      start=(hc == 0), stop=(hc == HC - 1))
                    INC(i0, "pe", f"u_{p}_{e}_{ic}")
                ALIAS("pe", f"guexp_{p}_{si}", f"u_{p}_{e}_{ICH - 1}")
            ALIAS("pe", f"guend_{p}", f"u_{p}_{EORDER[-1]}_{ICH - 1}")
            if p == 0:
                gate_tail_b()
                bselmms(0, range(NE))
            for hc in range(HC):
                WAIT(t, "gw", f"dws_{p}_{hc}")
                bank_w(t, 5 + hc % 2, ("ve", f"evac_{p}_{hc}"))
                slot = (p * HC + hc) % 2
                for kc in range(KC):
                    if hc == 0 and kc == 0:
                        WAIT(t, "ve", f"hidend_{p}")
                    i0 = t.matmul(pb[5 + hc % 2][:, 0:TC],
                                  v_dws[:, slot, kc, :], v_hid[:, kc, :],
                                  start=(kc == 0), stop=(kc == KC - 1))
                INC(i0, "pe", f"dn_{p}_{hc}")

    def emit_vector(v):
        S["cur"] = v if not isinstance(v, _EngP) else None
        v.memset(carry[:, :], 0.0)
        v.memset(ones_bfc[:, :], 1.0)
        v.memset(ones_fc[:, :], 1.0)
        v.memset(ones_fr[:, :], 1.0)
        i0 = v.nop()
        INC(i0, "ve", "ones")
        rows_prev = [None]

        def rows(c, tag):
            if tag == "1":
                rm, ri, rb2 = v_rmean, v_ris, v_rb
                prev_pe = f"bc1_{c - 1}" if c >= 1 else None
                blo, bhi = 0, 1
            else:
                rm, ri, rb2 = v_r2mean, v_r2is, v_r2b
                prev_pe = f"mcolT_{c - 1}" if c >= 1 else None
                blo, bhi = 4, 5
            if prev_pe is not None:
                WAIT(v, "pe", prev_pe)
            if rows_prev[0] is not None:
                WAIT(v, "se", rows_prev[0])
            rows_prev[0] = f"sqrt{tag}_{c}"
            WAIT(v, "pe", f"st{tag}_{c}")
            v.tensor_scalar(out=rm[0:1, :], in0=pb[blo][0:1, 0:TC],
                            scalar1=invH, scalar2=None, op0=ALU.mult)
            v.tensor_scalar(out=v_re2[0:1, :], in0=pb[bhi][0:1, 0:TC],
                            scalar1=invH, scalar2=EPS,
                            op0=ALU.mult, op1=ALU.add)
            i0 = v.nop()
            INC(i0, "ve", f"stread{tag}_{c}")
            v.tensor_tensor(out=v_rmsq[0:1, :], in0=rm[0:1, :],
                            in1=rm[0:1, :], op=ALU.mult)
            v.drain()
            v.tensor_tensor(out=v_re2[0:1, :], in0=v_re2[0:1, :],
                            in1=v_rmsq[0:1, :], op=ALU.subtract)
            i0 = v.nop()
            INC(i0, "ve", f"vrow{tag}_{c}")
            WAIT(v, "se", f"sqrt{tag}_{c}")
            v.reciprocal(out=ri[0:1, :], in_=v_rmsq[0:1, :])
            v.drain()
            i0 = v.tensor_tensor(out=rb2[0:1, :], in0=rm[0:1, :],
                                 in1=ri[0:1, :], op=ALU.mult)
            INC(i0, "ve", f"rows{tag}_{c}")

        for i in range(NCH + 2):
            if 2 <= i <= NCH + 1:
                rows(i - 2, "2")
            if i < NCH:
                rows(i, "1")
            if i < NCH:
                c = i
                WAIT(v, "se", f"ab1_{c}")
                WAIT(v, "se", "dcol")
                if c >= 2:
                    WAIT(v, "pe", f"wr_{c - 2}_{HC - 1}")
                for hc in range(HC):
                    v.tensor_copy(out=v_zb[:, 0:1], in_=carry[:, hc:hc + 1])
                    v.tensor_tensor(out=v_lnb, in0=xr(c)[:, hc, :],
                                    in1=v_ab1[:, 0:TC], op=ALU.mult)
                    v.tensor_tensor(out=v_lnb, in0=v_lnb,
                                    in1=v_ab1[:, TC:2 * TC], op=ALU.subtract)
                    v.tensor_tensor_scan(
                        out=v_zb[:, 1:TC + 1],
                        data0=dcol[:, hc:hc + 1].to_broadcast([128, TC]),
                        data1=v_lnb, initial=v_zb[:, 0:1],
                        op0=ALU.mult, op1=ALU.add)
                    v.tensor_tensor(out=v_diff, in0=v_lnb, in1=v_zb[:, 0:TC],
                                    op=ALU.subtract)
                    v.scalar_tensor_tensor(
                        out=v_vin[:, c % 2, hc, :], in0=v_diff,
                        scalar=tmvc[:, hc:hc + 1], in1=v_zb[:, 0:TC],
                        op0=ALU.mult, op1=ALU.add)
                    v.scalar_tensor_tensor(
                        out=v_rin[:, c % 2, hc, :], in0=v_diff,
                        scalar=tmrc[:, hc:hc + 1], in1=v_zb[:, 0:TC],
                        op0=ALU.mult, op1=ALU.add)
                    v.tensor_copy(out=carry[:, hc:hc + 1],
                                  in_=v_zb[:, TC:TC + 1])
                    if hc == HC - 1:
                        i0 = v.nop()
                        INC(i0, "ve", f"mix_{c}")
            if 1 <= i <= NCH:
                c = i - 1
                for oc in range(HC):
                    WAIT(v, "pe", f"wo_{c}_{oc}")
                    WAIT(v, "se", f"sig_{c}_{oc}")
                    v.tensor_tensor(out=v_diff, in0=pb[6 + oc % 2][:, 0:TC],
                                    in1=v_rc[:, oc, :], op=ALU.mult)
                    v.tensor_tensor(out=xr(c)[:, oc, :], in0=v_diff,
                                    in1=xr(c)[:, oc, :], op=ALU.add)
                    i0 = v.nop()
                    INC(i0, "ve", f"epioc_{c}_{oc}")
                i0 = v.nop()
                INC(i0, "ve", f"epi_{c}")
            if 2 <= i <= NCH + 1:
                c = i - 2
                WAIT(v, "se", f"ab2_{c}")
                for hc in range(HC):
                    k = c * HC + hc
                    if k >= 4:
                        pk, hk = divmod(k - 4, HC)
                        WAIT(v, "sp", f"ln2sp_{pk}_{hk}")
                    v.tensor_tensor(out=v_lnb, in0=xr(c)[:, hc, :],
                                    in1=v_ab2[:, 0:TC], op=ALU.mult)
                    v.tensor_tensor(out=v_ln2bf[:, k % 4, :],
                                    in0=v_lnb, in1=v_ab2[:, TC:2 * TC],
                                    op=ALU.subtract)
                    i0 = v.nop()
                    INC(i0, "ve", f"ln2_{c}_{hc}")
                WAIT(v, "pe", f"raw_{c}")
                v.tensor_copy(out=raw_em[:, c * TC:(c + 1) * TC],
                              in_=pb[2][0:8, 0:TC])
                i0 = v.nop()
                INC(i0, "ve", f"rawev_{c}")
        WAIT(v, "pe", "gsum_mm")
        i0 = v.tensor_reduce(
            out=gsumr[0:1, :].unsqueeze(2),
            in_=pb[6][0:1, 0:HC * NE].rearrange("p (h e) -> p e h", e=NE),
            axis=AXX, op=ALU.add)
        i0 = v.nop()
        INC(i0, "ve", "gsumr")
        WAIT(v, "se", "gsumbc")
        for j in range(NSTRIP):
            WAIT(v, "pe", f"rawT_{j}")
            reg = (j // 2 % 2) * 8
            v.scalar_tensor_tensor(
                out=g_t1[:, j * 8:(j + 1) * 8], in0=gsumbc[:, :],
                scalar=mcols[:, j:j + 1], in1=pb[j % 2][:, reg:reg + 8],
                op0=ALU.mult, op1=ALU.subtract)
            i0 = v.nop()
            INC(i0, "ve", f"t1_{j}")
        for j in range(NSTRIP):
            v.tensor_scalar(out=g_t1[:, j * 8:(j + 1) * 8],
                            in0=g_t1[:, j * 8:(j + 1) * 8],
                            scalar1=iscols[:, j:j + 1], scalar2=-1.0,
                            op0=ALU.mult, op1=ALU.mult)
        v.drain()
        WAIT(v, "se", "lbbc")
        for j in range(NSTRIP):
            i0 = v.tensor_tensor(out=g_lg[:, j * 8:(j + 1) * 8],
                                 in0=g_t1[:, j * 8:(j + 1) * 8],
                                 in1=lbbc[:, :], op=ALU.add)
        i0 = v.nop()
        INC(i0, "ve", "logits")
        WAIT(v, "se", "scores")
        v.reduce_max(out=g_m1[:, :].unsqueeze(2),
                     in_=g_sc[:, :].rearrange("p (j e) -> p j e", e=8),
                     axis=AXX)
        v.drain()
        v.tensor_tensor(out=g_lg[:, :].rearrange("p (j e) -> p j e", e=8),
                        in0=g_sc[:, :].rearrange("p (j e) -> p j e", e=8),
                        in1=g_m1[:, :].unsqueeze(2).to_broadcast([128, NSTRIP, 8]),
                        op=ALU.is_ge)
        v.drain()
        v.scalar_tensor_tensor(out=g_lg[:, :], in0=g_lg[:, :], scalar=1e30,
                               in1=g_sc[:, :], op0=ALU.mult, op1=ALU.subtract)
        v.drain()
        v.tensor_reduce(out=g_m2[:, :].unsqueeze(2),
                        in_=g_lg[:, :].rearrange("p (j e) -> p j e", e=8),
                        axis=AXX, op=ALU.min, negate=True)
        v.drain()
        v.tensor_tensor(out=g_sel[:, :].rearrange("p (j e) -> p j e", e=8),
                        in0=g_sc[:, :].rearrange("p (j e) -> p j e", e=8),
                        in1=g_m2[:, :].unsqueeze(2).to_broadcast([128, NSTRIP, 8]),
                        op=ALU.is_ge)
        v.tensor_tensor(out=g_s12[:, :], in0=g_m1[:, :], in1=g_m2[:, :],
                        op=ALU.add)
        v.drain()
        v.tensor_scalar(out=g_s12[:, :], in0=g_s12[:, :], scalar1=1e-8,
                        scalar2=None, op0=ALU.add)
        v.drain()
        v.reciprocal(out=g_i12[:, :], in_=g_s12[:, :])
        v.tensor_tensor(out=g_wc[:, :], in0=g_sc[:, :], in1=g_sel[:, :],
                        op=ALU.mult)
        v.drain()
        i0 = v.tensor_tensor(
            out=g_wc[:, :].rearrange("p (j e) -> p j e", e=8),
            in0=g_wc[:, :].rearrange("p (j e) -> p j e", e=8),
            in1=g_i12[:, :].unsqueeze(2).to_broadcast([128, NSTRIP, 8]),
            op=ALU.mult)
        i0 = v.nop()
        INC(i0, "ve", "wc")

        EORDER_V = [NE, NE + 1] + list(range(NE))
        qv = [0]

        def hid_ops(p, si_list):
            for si in si_list:
                e = EORDER_V[si]
                for ic in range(ICH):
                    kc = e * ICH + ic
                    q = qv[0]
                    qv[0] += 1
                    WAIT(v, "se", f"silu_{p}_{e}_{ic}")
                    WAIT(v, "pe", f"u_{p}_{e}_{ic}")
                    gs = v_gsb[:, (q % 2) * TC:(q % 2 + 1) * TC]
                    if e < NE and p > 0:
                        WAIT(v, "pe", f"bselmm_{p}_{e}")
                        v.tensor_tensor(out=v_h1b, in0=gs,
                                        in1=pb[2 + q % 2][:, 0:TC],
                                        op=ALU.mult)
                        v.tensor_tensor(out=v_hid[:, kc, :], in0=v_h1b,
                                        in1=pb[4][:, 0:TC], op=ALU.mult)
                    else:
                        v.tensor_tensor(out=v_hid[:, kc, :], in0=gs,
                                        in1=pb[2 + q % 2][:, 0:TC],
                                        op=ALU.mult)
                    i0 = v.nop()
                    INC(i0, "ve", f"hid_{p}_{kc}")

        def hid_scale0():
            for e in range(NE):
                WAIT(v, "pe", f"bselmm_0_{e}")
                for ic in range(ICH):
                    kc = e * ICH + ic
                    v.tensor_tensor(out=v_hid[:, kc, :], in0=v_hid[:, kc, :],
                                    in1=pb[4][:, 0:TC], op=ALU.mult)
                i0 = v.nop()
                INC(i0, "ve", f"hsc_{e}")

        hid_ops(0, [0, 1])
        WAIT(v, "pe", "part_wc")
        v.tensor_reduce(
            out=ccin_sb[0:1, 0:8].unsqueeze(2),
            in_=pb[6][0:1, 0:NSTRIP * 8].rearrange("p (j e) -> p e j", e=8),
            axis=AXX, op=ALU.add)
        WAIT(v, "pe", "part_sel")
        v.tensor_reduce(
            out=ccin_sb[0:1, 8:16].unsqueeze(2),
            in_=pb[7][0:1, 0:NSTRIP * 8].rearrange("p (j e) -> p e j", e=8),
            axis=AXX, op=ALU.add)
        i0 = v.nop()
        INC(i0, "ve", "ccin")
        WAIT(v, "gw", "ccout")
        v.tensor_scalar_max(out=ccout_sb[0:1, 8:16], in0=ccout_sb[0:1, 8:16],
                            scalar1=1.0)
        v.drain()
        v.reciprocal(out=ccout_sb[0:1, 8:16], in_=ccout_sb[0:1, 8:16])
        v.drain()
        v.tensor_tensor(out=wmrow[:, :], in0=ccout_sb[0:1, 0:8],
                        in1=ccout_sb[0:1, 8:16], op=ALU.mult)
        i0 = v.nop()
        INC(i0, "ve", "wmrow")
        for p in range(NCH):
            if p >= 1:
                WAIT(v, "pe", f"dn_{p - 1}_{HC - 1}")
                hid_ops(p, range(2))
            hid_ops(p, range(2, NPE))
            if p == 0:
                WAIT(v, "pe", "wmcol_mm")
                i0 = v.tensor_copy(out=wmcol[:, :], in_=pb[6][0:8, 0:1])
                v.drain()
                INC(i0, "ve", "wmcol")
                for jg in range(NJG):
                    WAIT(v, "pe", f"selT_{jg}")
                    n_in_g = min(4, NSTRIP - jg * 4)
                    v.tensor_scalar(
                        out=s_em[:, jg * 512:jg * 512 + n_in_g * 128],
                        in0=pb[7][0:8, 0:n_in_g * 128],
                        scalar1=wmcol[:, :], scalar2=None, op0=ALU.mult)
                    i0 = v.nop()
                    INC(i0, "ve", f"sem_{jg}")
                i0 = v.nop()
                INC(i0, "ve", "semrows")
                hid_scale0()
            i0 = v.nop()
            INC(i0, "ve", f"hidend_{p}")
            WAIT(v, "sx", f"x1t_{p}")
            for hc in range(HC):
                WAIT(v, "pe", f"dn_{p}_{hc}")
                v.tensor_tensor(out=v_x1t[:, p % 2, hc, :],
                                in0=pb[5 + hc % 2][:, 0:TC],
                                in1=v_x1t[:, p % 2, hc, :], op=ALU.add)
                i0 = v.nop()
                INC(i0, "ve", f"evac_{p}_{hc}")
            i0 = v.nop()
            INC(i0, "ve", f"evacend_{p}")

    def run_all(engs):
        emit_gpsimd(engs["gpsimd"])
        emit_sync(engs["sync"])
        emit_scalar(engs["scalar"])
        emit_pe(engs["tensor"])
        emit_vector(engs["vector"])

    dummies = {k: _DUMMY_ENG for k in
               ["gpsimd", "sync", "scalar", "tensor", "vector"]}
    reset(False)
    run_all(dummies)
    led = dict(S["led"])
    nc._debug_led = led

    with nc.Block() as block:

        def mk(name):
            def f(eng):
                reset(True)
                S["led"] = led
                engs = dict(dummies)
                if cfg.get("paranoid") and name in ("vector", "scalar", "gpsimd"):
                    eng = _ParanoidEng(eng)
                engs[name] = eng
                run_all(engs)
            f.__name__ = f"_{name}"
            return f

        block.gpsimd(mk("gpsimd"))
        block.sync(mk("sync"))
        block.scalar(mk("scalar"))
        block.tensor(mk("tensor"))
        block.vector(mk("vector"))

    return nc



def _pm_proj(W, HC):
    O = W.shape[0]
    OC = O // 128
    WT = np.ascontiguousarray(W.T)
    return np.ascontiguousarray(
        WT.reshape(HC, 128, OC, 128).transpose(1, 2, 0, 3).reshape(128, -1)
    ).astype(np.float32)


def _pm_col(vec):
    n = vec.shape[0]
    return np.ascontiguousarray(vec.reshape(n // 128, 128).T).astype(np.float32)


def _pm_gu(M, HC, ICH):
    MT = np.ascontiguousarray(M.T)
    return MT.reshape(HC, 128, ICH, 128).transpose(1, 0, 2, 3).reshape(128, -1)


def host_inputs(cfg, inputs):
    HC, T, NE, ICH = cfg["HC"], cfg["T"], cfg["NE"], cfg["ICH"]
    NPE = NE + 2
    KC = NPE * ICH
    I = ICH * 128

    f = lambda k: np.asarray(inputs[k], dtype=np.float32)
    x = f("L0_attn_x")
    egw, euw, edw = (f("L0_ffn_expert_gate_w"), f("L0_ffn_expert_up_w"),
                     f("L0_ffn_expert_down_w"))
    sgw, suw, sdw = (f("L0_ffn_shared_gate_w"), f("L0_ffn_shared_up_w"),
                     f("L0_ffn_shared_down_w"))
    gw = f("L0_ffn_gate_w")

    shared = {
        "wv": _pm_proj(f("L0_attn_Wv"), HC),
        "wr": _pm_proj(f("L0_attn_Wr"), HC),
        "wo": _pm_proj(f("L0_attn_Wo"), HC),
        "td": _pm_col(f("L0_attn_time_decay")),
        "tmv": _pm_col(f("L0_attn_time_mix_v")),
        "tmr": _pm_col(f("L0_attn_time_mix_r")),
        "gatew": np.ascontiguousarray(
            gw.T.reshape(HC, 128, NE).transpose(1, 0, 2).reshape(128, -1)
        ).astype(np.float32),
        "lbrow": np.ascontiguousarray(
            f("L0_ffn_lb_bias").reshape(1, NE)).astype(np.float32),
    }
    gu = []
    for e in range(NE):
        gu.append(np.stack([_pm_gu(egw[e], HC, ICH), _pm_gu(euw[e], HC, ICH)],
                           axis=1).reshape(128, -1))
    gu.append(np.stack([_pm_gu(sgw[0:I], HC, ICH), _pm_gu(suw[0:I], HC, ICH)],
                       axis=1).reshape(128, -1))
    gu.append(np.stack([_pm_gu(sgw[I:2 * I], HC, ICH),
                        _pm_gu(suw[I:2 * I], HC, ICH)], axis=1).reshape(128, -1))
    shared["gwuw"] = np.ascontiguousarray(np.stack(gu, axis=0)).astype(np.float32)
    dwT = np.concatenate([edw[e].T for e in range(NE)] + [sdw.T], axis=0)
    shared["dwcat"] = np.ascontiguousarray(
        dwT.reshape(KC, 128, HC, 128).transpose(1, 2, 0, 3).reshape(128, -1)
    ).astype(np.float32)

    in_maps = []
    for b in range(cfg["ncores"]):
        m = dict(shared)
        m["x"] = np.ascontiguousarray(
            x[b].T.reshape(HC, 128, T).transpose(1, 0, 2).reshape(128, -1)
        ).astype(np.float32)
        in_maps.append(m)
    return in_maps


def host_output(cfg, results):
    HC, T = cfg["HC"], cfg["T"]
    H = 128 * HC
    outs = []
    for r in results:
        arr = np.asarray(r["out"]).reshape(128, HC, T).transpose(1, 0, 2)
        outs.append(arr.reshape(H, T).T)
    return np.ascontiguousarray(np.stack(outs, axis=0)).astype(np.float32)


def kernel(**inputs):
    cfg = FULL_CFG
    nc = build_program(cfg)
    in_maps = host_inputs(cfg, inputs)
    res = run_bass_kernel_spmd(nc, in_maps, core_ids=list(range(cfg["ncores"])))
    return host_output(cfg, res.results)
